# version 32
# baseline (speedup 1.0000x reference)
import numpy as np
import ml_dtypes
import concourse.bass as bass
import concourse.mybir as mybir
import concourse.bass_utils as _bass_utils
from concourse.bass_utils import run_bass_kernel_spmd

if not getattr(_bass_utils, "_ldw_opt_patched", False):
    _orig_run_command = _bass_utils.run_command

    def _run_command_ldw(cmd, *a, **kw):
        cmd = ["--enable-ldw-opt=true" if c == "--enable-ldw-opt=false" else c for c in cmd]
        return _orig_run_command(cmd, *a, **kw)

    _bass_utils.run_command = _run_command_ldw
    _bass_utils._ldw_opt_patched = True

F32 = mybir.dt.float32
BF16 = mybir.dt.bfloat16
I32 = mybir.dt.int32
U8 = mybir.dt.uint8
AF = mybir.ActivationFunctionType
ALU = mybir.AluOpType
AX = mybir.AxisListType

T = 4096
C = 1024
NT = T // 128
NG = T // 512
EPS32 = float(np.finfo(np.float32).eps)

ENGMAP = {"pe": "tensor", "act": "scalar", "dve": "vector", "pool": "gpsimd", "sp": "sync"}
QUEUES = ["pe", "act", "dve", "pool", "sp"]


NDMASEM = 6
INHERIT_CLOCKS = True


class SemState:
    def __init__(self, nc, stack):
        self.nc = nc
        self.csem = {}
        self.dsem = {}
        self.ccount = {q: 0 for q in QUEUES}
        self.dcount = {}
        self.dtotal = {q: 0 for q in QUEUES}
        for q in QUEUES:
            self.csem[q] = stack.enter_context(nc.semaphore(f"c_{q}"))
        for q in ("sp", "pool", "act", "dve"):
            self.dsem[q] = [stack.enter_context(nc.semaphore(f"d_{q}{i}")) for i in range(NDMASEM)]
            for i in range(NDMASEM):
                self.dcount[(q, i)] = 0
        self.all_handles = list(self.csem.values()) + [h for l in self.dsem.values() for h in l]

    def emit_clear(self, nc):
        nums = sorted(h.num for h in self.all_handles)
        assert nums == list(range(nums[0], nums[0] + len(nums)))
        rng = range(nums[0], nums[-1] + 1)
        with nc.Block() as blk:
            @blk.gpsimd
            def _(eng):
                eng.dma_reset(rng)
                eng.sem_clear(rng)
        for q in QUEUES:
            self.ccount[q] = 0
            self.dtotal[q] = 0
        for k in self.dcount:
            self.dcount[k] = 0


class Op:
    __slots__ = ["q", "emit", "reads", "writes", "dma", "idx", "deps", "inc", "val",
                 "clock", "qidx", "accum_key"]
    def __init__(self, q, emit, reads, writes, dma, accum_key=None):
        self.q = q; self.emit = emit
        self.reads = reads; self.writes = writes
        self.dma = dma; self.accum_key = accum_key
        self.deps = []
        self.inc = False
        self.val = None
        self.clock = None


class MS:
    def __init__(self, sems):
        self.sems = sems
        self.ops = []
        self.last_writer = {}
        self.readers = {}
        self.psum_keys = set()

    def op(self, q, emit, reads=(), writes=(), dma=False, accum_key=None):
        o = Op(q, emit, tuple(reads), tuple(writes), dma, accum_key)
        o.idx = len(self.ops)
        dep = {}
        def add(d):
            if d is not None and d is not o:
                dep[d.idx] = d
        for b in o.reads:
            add(self.last_writer.get(b))
            if b in self.psum_keys:
                for r in self.readers.get(b, ()):
                    if r.q != o.q:
                        add(r)
        for b in o.writes:
            add(self.last_writer.get(b))
            for r in self.readers.get(b, ()):
                add(r)
        deps = []
        for d in dep.values():
            if d.q == o.q and d.dma and o.dma:
                continue
            if (accum_key is not None and d.accum_key == accum_key and d.q == o.q == "pe"):
                continue
            deps.append(d)
        o.deps = deps
        if dma:
            o.inc = True
        for d in deps:
            d.inc = True
        for b in o.writes:
            self.last_writer[b] = o
            self.readers[b] = []
        for b in o.reads:
            self.readers.setdefault(b, []).append(o)
        self.ops.append(o)
        return o

    def finalize_counts(self):
        last_compute = {}
        for o in self.ops:
            if not o.dma:
                last_compute[o.q] = o
        for o in last_compute.values():
            o.inc = True
        for o in self.ops:
            if o.q == "pool" and not o.dma:
                o.inc = True
        for o in self.ops:
            if o.inc:
                if o.dma:
                    slot = self.sems.dtotal[o.q] % NDMASEM
                    self.sems.dtotal[o.q] += 1
                    self.sems.dcount[(o.q, slot)] += 16
                    o.val = ("d", (o.q, slot), self.sems.dcount[(o.q, slot)])
                else:
                    self.sems.ccount[o.q] += 1
                    o.val = ("c", o.q, self.sems.ccount[o.q])

    @property
    def ops_last_per_queue(self):
        last = {}
        for o in self.ops:
            last[o.q] = o
        return last

    def emit_queue(self, q, eng, final_drain=False):
        known = {}
        def sem_handle(val):
            kind, qq, v = val
            if kind == "d":
                return self.sems.dsem[qq[0]][qq[1]], v
            return self.sems.csem[qq], v
        for o in self.ops:
            if o.q != q:
                continue
            for d in o.deps:
                assert d.val is not None
                key = (d.val[0], d.val[1])
                if known.get(key, 0) >= d.val[2]:
                    continue
                h, v = sem_handle(d.val)
                eng.wait_ge(h, v)
                known[key] = d.val[2]
                if INHERIT_CLOCKS and d.clock:
                    for k2, v2 in d.clock.items():
                        if known.get(k2, 0) < v2:
                            known[k2] = v2
            inst = o.emit(eng)
            if o.inc:
                h, v = sem_handle(o.val)
                inst.then_inc(h, 16 if o.dma else 1)
                o.clock = dict(known)
                o.clock[(o.val[0], o.val[1])] = o.val[2]
            else:
                o.clock = dict(known)
        if final_drain:
            for (qq, slot), v in self.sems.dcount.items():
                if v > 0:
                    eng.wait_ge(self.sems.dsem[qq][slot], v)
            for qq in QUEUES:
                if self.sems.ccount[qq] > 0 and qq != q:
                    eng.wait_ge(self.sems.csem[qq], self.sems.ccount[qq])

    def run_block(self, nc, drain_queue="sp"):
        self.finalize_counts()
        with nc.Block() as blk:
            @blk.tensor
            def _(eng):
                self.emit_queue("pe", eng, final_drain=(drain_queue == "pe"))
            @blk.scalar
            def _(eng):
                self.emit_queue("act", eng, final_drain=(drain_queue == "act"))
            @blk.vector
            def _(eng):
                self.emit_queue("dve", eng, final_drain=(drain_queue == "dve"))
            @blk.gpsimd
            def _(eng):
                self.emit_queue("pool", eng, final_drain=(drain_queue == "pool"))
            @blk.sync
            def _(eng):
                self.emit_queue("sp", eng, final_drain=(drain_queue == "sp"))



def _bf16(a):
    return np.ascontiguousarray(a).astype(ml_dtypes.bfloat16)


def retnet_consts():
    H = 16
    gamma = np.log(1.0 - 2.0 ** (-5.0 - np.arange(H, dtype=np.float64)))
    c_full = np.zeros((128, H * 32), dtype=np.float32)
    sprime = np.arange(128, dtype=np.float64)
    for h in range(H):
        for dti in range(32):
            dt = dti - 3
            c_full[:, h * 32 + dti] = np.exp(gamma[h] * (128.0 * dt - sprime)).astype(np.float32)
    q = np.arange(T, dtype=np.float64)
    r = np.exp(gamma)
    Ap = np.zeros((H, T), dtype=np.float32)
    for h in range(H):
        partial = np.cumsum(r[h] ** np.arange(T))
        scale = np.sqrt(partial)
        Ap[h] = (np.exp(gamma[h] * (q - 512.0 * (q // 512))) / scale).astype(np.float32)
    A_rep = np.ascontiguousarray(np.broadcast_to(Ap[:, None, :], (H, 128, T))).astype(np.float32)
    SAINV = (np.sqrt(EPS32) / Ap).astype(np.float32)
    SAINV_rep = np.ascontiguousarray(np.broadcast_to(SAINV[:, None, :], (H, 128, T))).astype(np.float32)
    return c_full, A_rep, SAINV_rep, np.ascontiguousarray(Ap), np.ascontiguousarray(SAINV)


def host_prep(inp):
    d = {}
    d["ident"] = np.eye(128, dtype=np.float32)
    d["ones_bf"] = np.ones((128, 128), dtype=ml_dtypes.bfloat16)
    c_full, A_rep, SAINV_rep, A_rows, SAINV_rows = retnet_consts()
    d["c_full"] = c_full
    d["A_rows"] = A_rows
    d["SAINV_rows"] = SAINV_rows
    sp = np.arange(128)[:, None]
    qp = np.arange(512)[None, :]
    d["tri"] = np.concatenate([(128 * r + sp <= qp).astype(np.float32) for r in range(4)], axis=1)
    d["A_rep"] = A_rep
    d["SAINV_rep"] = SAINV_rep
    
    d["Wqk"] = inp["L0_attn_W_qk"]
    d["Wv"] = inp["L0_attn_W_v"]
    d["Wg"] = inp["L0_attn_W_g"]
    d["Wo"] = inp["L0_attn_W_o"]
    d["bqk_fm"] = np.ascontiguousarray(inp["L0_attn_b_qk"].reshape(16, 128).T)
    d["bg_fm"] = np.ascontiguousarray(inp["L0_attn_b_g"].reshape(16, 128).T)
    d["bv_rep"] = np.ascontiguousarray(np.broadcast_to(inp["L0_attn_b_v"], (128, 2048)))
    d["bo_rep"] = np.ascontiguousarray(np.broadcast_to(inp["L0_attn_b_o"], (128, 1024)))
    d["f0_wgT"] = np.ascontiguousarray(inp["L0_ffn_w_gate"].T)
    d["f0_wuT"] = np.ascontiguousarray(inp["L0_ffn_w_up"].T)
    d["f0_wdT"] = np.ascontiguousarray(inp["L0_ffn_w_down"].T)
    d["qkvT"] = np.ascontiguousarray(inp["L1_attn_qkv_w"].T)
    qb = inp["L1_attn_qkv_b"]
    d["qb_fm"] = np.ascontiguousarray(qb[0:1024].reshape(8, 128).T)
    d["kb_rep"] = np.ascontiguousarray(np.broadcast_to(qb[1024:2048], (128, 1024)))
    d["vb_rep"] = np.ascontiguousarray(np.broadcast_to(qb[2048:3072], (128, 1024)))
    d["kng_rep"] = np.ascontiguousarray(np.broadcast_to(np.tile(inp["L1_attn_kn_g"], 16), (128, 1024)))
    d["knb_rep"] = np.ascontiguousarray(np.broadcast_to(np.tile(inp["L1_attn_kn_b"], 16), (128, 1024)))
    d["vng_rep"] = np.ascontiguousarray(np.broadcast_to(np.tile(inp["L1_attn_vn_g"], 16), (128, 1024)))
    d["vnb_rep"] = np.ascontiguousarray(np.broadcast_to(np.tile(inp["L1_attn_vn_b"], 16), (128, 1024)))
    temp = inp["L1_attn_temperature"].reshape(16)
    d["temp_rep"] = np.ascontiguousarray(np.broadcast_to(np.repeat(temp, 64)[None, :], (64, 1024)))
    d["kvbias"] = np.ascontiguousarray(inp["L1_attn_attn_bias"].transpose(1, 0, 2).reshape(64, 1024))
    d["p1T"] = np.ascontiguousarray(inp["L1_attn_p1_w"].T)
    d["p1b_rep"] = np.ascontiguousarray(np.broadcast_to(inp["L1_attn_p1_b"], (128, 1024)))
    d["p2T"] = np.ascontiguousarray(inp["L1_attn_p2_w"].T)
    d["p2b_rep"] = np.ascontiguousarray(np.broadcast_to(inp["L1_attn_p2_b"], (128, 1024)))
    d["gwT"] = np.ascontiguousarray(inp["L1_ffn_gate_w"].T)
    d["gb_rep"] = np.ascontiguousarray(np.broadcast_to(inp["L1_ffn_gate_bias"], (128, 32)))
    d["wgE"] = np.ascontiguousarray(inp["L1_ffn_w_gate"].transpose(0, 2, 1))
    d["wuE"] = np.ascontiguousarray(inp["L1_ffn_w_up"].transpose(0, 2, 1))
    d["wdE"] = np.ascontiguousarray(inp["L1_ffn_w_down"].transpose(0, 2, 1))
    d["shgT"] = np.ascontiguousarray(inp["L1_ffn_sh_gate"].T)
    d["shuT"] = np.ascontiguousarray(inp["L1_ffn_sh_up"].T)
    d["shdT"] = np.ascontiguousarray(inp["L1_ffn_sh_down"].T)
    return d


from contextlib import ExitStack


class Ctx:
    pass


def declare_dram(nc, cx, phases, first, last):
    E = "ExternalInput"
    O = "ExternalOutput"
    I = "Internal"
    D = {}
    def t(name, shape, dtype, kind):
        D[name] = nc.dram_tensor(name, list(shape), dtype, kind=kind).ap()
    t("ident", (128, 128), F32, E)
    if "A" in phases:
        t("ones_bf", (128, 128), BF16, E)
        t("x0", (T, C), F32, E)
        t("c_full", (128, 512), F32, E)
        t("tri", (128, 2048), F32, E)
        t("A_rows", (16, T), F32, E)
        t("SAINV_rows", (16, T), F32, E)
        for n, s in [("Wqk", (C, 2048)), ("Wv", (C, 2048)), ("Wg", (C, 2048)), ("Wo", (2048, C))]:
            t(n, s, F32, E)
        t("bqk_fm", (128, 16), F32, E)
        t("bg_fm", (128, 16), F32, E)
        t("bv_rep", (128, 2048), F32, E)
        t("bo_rep", (128, 1024), F32, E)
        t("s_q", (16, 64, T), BF16, I)
        t("s_k", (16, 64, T), BF16, I)
        t("s_v", (16, T, 128), BF16, I)
        t("s_gated", (16, 128, T), BF16, I)
        t("resid1", (T, C), F32, O if last == "A" else I)
    if "B" in phases:
        if "A" not in phases:
            t("resid1", (T, C), F32, E)
        for n, s in [("f0_wgT", (C, 3072)), ("f0_wuT", (C, 3072)), ("f0_wdT", (3072, C))]:
            t(n, s, I32, E)
        t("resid2", (T, C), F32, O if last == "B" else I)
        if B_DEBUG:
            t("dbg_gu", (128, 24 * 512), BF16, O)
            t("dbg_x", (128, 1024), F32, O)
            t("dbg_a", (128, 1024), F32, O)
            t("dbg_xT", (128, 8 * 4096), BF16, O)
            t("dbg_w", (128, 8 * 128), BF16, O)
    if "C" in phases:
        if "B" not in phases:
            t("resid2", (T, C), F32, E)
        t("qkvT", (C, 3072), F32, E)
        t("qb_fm", (128, 8), F32, E)
        for n in ["kb_rep", "vb_rep", "kng_rep", "knb_rep", "vng_rep", "vnb_rep",
                  "p1b_rep", "p2b_rep"]:
            t(n, (128, 1024), F32, E)
        t("temp_rep", (64, 1024), F32, E)
        t("kvbias", (64, 1024), F32, E)
        t("p1T", (C, C), F32, E)
        t("p2T", (C, C), F32, E)
        t("s_qc", (8, 128, T), BF16, I)
        t("resid3", (T, C), F32, O if last == "C" else I)
    if "D" in phases:
        if "C" not in phases:
            t("resid3", (T, C), F32, E)
        t("gwT", (C, 32), F32, E)
        t("gb_rep", (128, 32), F32, E)
        t("wgE", (32, C, 512), F32, E)
        t("wuE", (32, C, 512), F32, E)
        t("wdE", (32, 512, C), F32, E)
        t("shgT", (C, C), F32, E)
        t("shuT", (C, C), F32, E)
        t("shdT", (C, C), F32, E)
        t("out", (T, C), F32, O)
    cx.D = D


_UID = [0]


def uid():
    _UID[0] += 1
    return _UID[0]


def alloc_psums(nc, st, spec, ms=None):
    ps = {}
    for name, shape in spec:
        ps[name] = st.enter_context(nc.psum_tensor(f"{name}_{uid()}", list(shape), F32))
    if ms is not None:
        ms.psum_keys.update(ps.keys())
    return ps


def sb(nc, st, name, shape, dtype):
    return st.enter_context(nc.sbuf_tensor(f"{name}_{uid()}", list(shape), dtype))


def build_xT(nc, cx, stack, src_dram, ms, ps):
    load_ident(nc, cx, ms)
    xt = [sb(nc, stack, f"a0xt{i}", [128, C], F32) for i in range(2)]
    xTv = cx.xT[:, :].rearrange("p (a b) -> p a b", b=4096)
    for tt in range(NT):
        b = tt % 2
        ms.op("sp", lambda e, b=b, tt=tt: e.dma_start(out=xt[b][:, :], in_=src_dram[tt * 128:(tt + 1) * 128, :]),
              reads=(), writes=(f"xt{b}",), dma=True)
        for half in range(2):
            ph = f"p{half}"
            pt = ps[ph]
            for k in range(4):
                cc = half * 4 + k
                ms.op("pe", lambda e, pt=pt, k=k, b=b, cc=cc: e.matmul(
                    pt[:, k * 128:(k + 1) * 128], xt[b][:, cc * 128:(cc + 1) * 128], cx.identf[:, :],
                    start=(k == 0), stop=(k == 3), is_transpose=True),
                    reads=(f"xt{b}", "ident"), writes=(ph,), accum_key=("tr", tt, half))
            q = "dve" if half == 0 else "act"
            dst = xTv[:, half * 4:half * 4 + 4, tt * 128:(tt + 1) * 128]
            if q == "dve":
                ms.op(q, lambda e, pt=pt, dst=dst: e.tensor_copy(
                    dst, pt[:, :].rearrange("p (a b) -> p a b", b=128)),
                    reads=(ph,), writes=("xT",))
            else:
                ms.op(q, lambda e, pt=pt, dst=dst: e.copy(
                    dst, pt[:, :].rearrange("p (a b) -> p a b", b=128)),
                    reads=(ph,), writes=("xT",))


def phaseA(nc, cx, prog_stack):
    D = cx.D
    sems = cx.sems
    cx.a2_pend = []
    ast_ = prog_stack.enter_context(ExitStack())
    cx.xT = sb(nc, ast_, "a_xT", [128, 8 * 4096], BF16)
    with ExitStack() as st:
        ms = MS(sems)
        ps = alloc_psums(nc, st, [("p0", (128, 512)), ("p1", (128, 512))])
        build_xT(nc, cx, st, D["x0"], ms, ps)
        ms.run_block(nc)
    with ExitStack() as st:
        ms = MS(sems)
        ps = alloc_psums(nc, st, [(f"p{i}", (128, 512)) for i in range(4)], ms)
        wqk = st.enter_context(nc.sbuf_tensor("wqk", [128, 8 * 2048], BF16))
        wv = st.enter_context(nc.sbuf_tensor("wv", [128, 8 * 2048], BF16))
        bqk = st.enter_context(nc.sbuf_tensor("bqk", [128, 16], F32))
        bv = st.enter_context(nc.sbuf_tensor("bv", [128, 2048], F32))
        qkst = [st.enter_context(nc.sbuf_tensor(f"qkst{i}", [128, 512], BF16)) for i in range(4)]
        vst = [st.enter_context(nc.sbuf_tensor(f"vst{i}", [128, 512], BF16)) for i in range(4)]
        ms.op("sp", lambda e: e.dma_start(out=bqk[:, :], in_=D["bqk_fm"][:, :]), writes=("bqk",), dma=True)
        ms.op("sp", lambda e: e.dma_start(out=bv[:, :], in_=D["bv_rep"][:, :]), writes=("bv",), dma=True)
        for cc in range(8):
            ms.op("pool", lambda e, cc=cc: e.dma_start(out=wqk[:, cc * 2048:(cc + 1) * 2048],
                                                       in_=D["Wqk"][cc * 128:(cc + 1) * 128, :]),
                  writes=(f"wqk{cc}",), dma=True)
            ms.op("pool", lambda e, cc=cc: e.dma_start(out=wv[:, cc * 2048:(cc + 1) * 2048],
                                                       in_=D["Wv"][cc * 128:(cc + 1) * 128, :]),
                  writes=(f"wv{cc}",), dma=True)
        for jc in range(16):
            for tgh in range(2):
                for cc in range(8):
                    for tg4 in range(4):
                        tg = tgh * 4 + tg4
                        pt = ps[f"p{tg4}"]
                        ms.op("pe", lambda e, pt=pt, cc=cc, jc=jc, tg=tg: e.matmul(
                            pt[:, :], wqk[:, cc * 2048 + jc * 128: cc * 2048 + (jc + 1) * 128],
                            cx.xT[:, cc * 4096 + tg * 512: cc * 4096 + (tg + 1) * 512],
                            start=(cc == 0), stop=(cc == 7)),
                            reads=(f"wqk{cc}", "xT"), writes=(f"p{tg4}",), accum_key=("qk", jc, tg))
                for tg4 in range(4):
                    tg = tgh * 4 + tg4
                    ph = f"p{tg4}"
                    pt = ps[ph]
                    qsb = qkst[tg4]
                    sbk = f"qkst{tg4}"
                    ms.op("act", lambda e, qsb=qsb, pt=pt, jc=jc: e.activation(
                        qsb[:, :], pt[:, :], AF.Identity, bias=bqk[:, jc:jc + 1]),
                        reads=(ph, "bqk"), writes=(sbk,))
                    tgt = D["s_q"] if jc < 8 else D["s_k"]
                    hh = (jc % 8) * 2
                    ms.op("sp", lambda e, tgt=tgt, hh=hh, tg=tg, qsb=qsb: e.dma_start(
                        out=tgt[hh, :, tg * 512:(tg + 1) * 512], in_=qsb[0:64, :]),
                        reads=(sbk,), writes=(), dma=True)
                    ms.op("sp", lambda e, tgt=tgt, hh=hh, tg=tg, qsb=qsb: e.dma_start(
                        out=tgt[hh + 1, :, tg * 512:(tg + 1) * 512], in_=qsb[64:128, :]),
                        reads=(sbk,), writes=(), dma=True)
        for tt in range(NT):
            for cc in range(8):
                for jg in range(4):
                    pt = ps[f"p{jg}"]
                    ms.op("pe", lambda e, pt=pt, cc=cc, tt=tt, jg=jg: e.matmul(
                        pt[:, :], cx.xT[:, cc * 4096 + tt * 128: cc * 4096 + (tt + 1) * 128],
                        wv[:, cc * 2048 + jg * 512: cc * 2048 + (jg + 1) * 512],
                        start=(cc == 0), stop=(cc == 7)),
                        reads=(f"wv{cc}", "xT"), writes=(f"p{jg}",), accum_key=("v", tt, jg))
            for jg in range(4):
                ph = f"p{jg}"
                pt = ps[ph]
                vsb = vst[jg]
                sbk = f"vst{jg}"
                ms.op("dve", lambda e, vsb=vsb, pt=pt, jg=jg: e.scalar_tensor_tensor(
                    vsb[:, :], pt[:, :], 1.0, bv[:, jg * 512:(jg + 1) * 512], ALU.bypass, ALU.add),
                    reads=(ph, "bv"), writes=(sbk,))
                ms.op("sp", lambda e, vsb=vsb, tt=tt, jg=jg: e.dma_start(
                    out=D["s_v"][jg * 4:(jg + 1) * 4, tt * 128:(tt + 1) * 128, :].rearrange("h p v -> p h v"),
                    in_=vsb[:, :].rearrange("p (h v) -> p h v", v=128)),
                    reads=(sbk,), writes=(), dma=True)
        ms.run_block(nc)
    with ExitStack() as st:
        ms = MS(sems)
        ps = alloc_psums(nc, st, [("p0", (128, 512)), ("p1", (128, 512)), ("p2", (128, 512)),
                                  ("p3", (128, 512)),
                                  ("psret0", (128, 512)), ("psret1", (128, 512)),
                                  ("psden", (1, 512)), ("psrms", (128, 512))], ms)
        wg = st.enter_context(nc.sbuf_tensor("wg", [128, 8 * 2048], BF16))
        bg = st.enter_context(nc.sbuf_tensor("bg", [128, 16], F32))
        cfu = st.enter_context(nc.sbuf_tensor("cfu", [128, 512], F32))
        trisb = st.enter_context(nc.sbuf_tensor("trisb", [128, 2048], F32))
        onesb = st.enter_context(nc.sbuf_tensor("onesb", [128, 1], BF16))
        onesf = st.enter_context(nc.sbuf_tensor("onesf", [1, 128], F32))
        qTh = [st.enter_context(nc.sbuf_tensor(f"qTh{i}", [64, T], BF16)) for i in range(2)]
        kTh = [st.enter_context(nc.sbuf_tensor(f"kTh{i}", [64, T], BF16)) for i in range(2)]
        vh = [st.enter_context(nc.sbuf_tensor(f"vh{i}", [128, T], BF16)) for i in range(2)]
        gTh = st.enter_context(nc.sbuf_tensor("gTh", [128, T], BF16))
        msst = [st.enter_context(nc.sbuf_tensor(f"msst{i}", [128, 512], BF16)) for i in range(4)]
        absst = [st.enter_context(nc.sbuf_tensor(f"absst{i}", [128, 512], BF16)) for i in range(4)]
        sqst = st.enter_context(nc.sbuf_tensor("sqst", [128, 512], BF16))
        arow = [st.enter_context(nc.sbuf_tensor(f"arow{i}", [1, 512], F32)) for i in range(2)]
        sarow = [st.enter_context(nc.sbuf_tensor(f"sarow{i}", [1, 512], F32)) for i in range(2)]
        denrow = st.enter_context(nc.sbuf_tensor("denrow", [1, 512], F32))
        rmsrow = st.enter_context(nc.sbuf_tensor("rmsrow", [1, 512], F32))
        dcrow = st.enter_context(nc.sbuf_tensor("dcrow", [1, 512], F32))
        w2row = st.enter_context(nc.sbuf_tensor("w2row", [1, 512], F32))
        frow = st.enter_context(nc.sbuf_tensor("frow", [1, 512], F32))
        fsb = st.enter_context(nc.sbuf_tensor("fsb", [128, 512], F32))
        y1sb = st.enter_context(nc.sbuf_tensor("y1sb", [128, 512], F32))
        gst = st.enter_context(nc.sbuf_tensor("gst", [128, 512], BF16))
        ms.op("sp", lambda e: e.dma_start(out=bg[:, :], in_=D["bg_fm"][:, :]), writes=("bg",), dma=True)
        ms.op("sp", lambda e: e.dma_start(out=cfu[:, :], in_=D["c_full"][:, :]), writes=("cfu",), dma=True)
        ms.op("sp", lambda e: e.dma_start(out=trisb[:, :], in_=D["tri"][:, :]), writes=("tri",), dma=True)
        ms.op("pool", lambda e: e.memset(onesb[:, :], 1.0), writes=("onesb",))
        ms.op("pool", lambda e: e.memset(onesf[:, :], 1.0), writes=("onesf",))
        for cc in range(8):
            ms.op("pool", lambda e, cc=cc: e.dma_start(out=wg[:, cc * 2048:(cc + 1) * 2048],
                                                       in_=D["Wg"][cc * 128:(cc + 1) * 128, :]),
                  writes=(f"wg{cc}",), dma=True)
        for h in range(16):
            hb = h % 2
            for fn_due in cx.a2_pend:
                fn_due[1]()
            cx.a2_pend = []
            ms.op("sp", lambda e, h=h, hb=hb: e.dma_start(out=qTh[hb][:, :], in_=D["s_q"][h, :, :]),
                  reads=(), writes=(f"qTh{hb}",), dma=True)
            ms.op("sp", lambda e, h=h, hb=hb: e.dma_start(out=kTh[hb][:, :], in_=D["s_k"][h, :, :]),
                  reads=(), writes=(f"kTh{hb}",), dma=True)
            for part in range(4):
                ms.op("sp", lambda e, h=h, hb=hb, part=part: e.dma_start(
                    out=vh[hb][:, :].rearrange("p (a v) -> p a v", v=128)[:, part * 8:(part + 1) * 8, :],
                    in_=D["s_v"][h, part * 1024:(part + 1) * 1024, :].rearrange("(a p) v -> p a v", p=128)),
                    reads=(), writes=(f"vh{hb}",), dma=True)
            for tgp in range(4):
                for cc in range(8):
                    for t2 in range(2):
                        tg = tgp * 2 + t2
                        pt = ps[f"p{t2}"]
                        ms.op("pe", lambda e, pt=pt, cc=cc, h=h, tg=tg: e.matmul(
                            pt[:, :], wg[:, cc * 2048 + h * 128: cc * 2048 + (h + 1) * 128],
                            cx.xT[:, cc * 4096 + tg * 512: cc * 4096 + (tg + 1) * 512],
                            start=(cc == 0), stop=(cc == 7)),
                            reads=(f"wg{cc}", "xT"), writes=(f"p{t2}",), accum_key=("g", h, tg))
                for t2 in range(2):
                    tg = tgp * 2 + t2
                    ph = f"p{t2}"
                    pt = ps[ph]
                    ms.op("act", lambda e, pt=pt, h=h, tg=tg: e.activation(
                        gTh[:, tg * 512:(tg + 1) * 512], pt[:, :], AF.Identity, bias=bg[:, h:h + 1]),
                        reads=(ph, "bg"), writes=("gTh",))
            for qg in range(NG):
                rk = f"psret{qg % 2}"
                qp = qg % 2
                ms.op("sp", lambda e, h=h, qg=qg, qp=qp: e.dma_start(
                    out=arow[qp][:, :], in_=D["A_rows"][h:h + 1, qg * 512:(qg + 1) * 512]),
                    writes=(f"arow{qp}",), dma=True)
                ms.op("sp", lambda e, h=h, qg=qg, qp=qp: e.dma_start(
                    out=sarow[qp][:, :], in_=D["SAINV_rows"][h:h + 1, qg * 512:(qg + 1) * 512]),
                    writes=(f"sarow{qp}",), dma=True)
                nkt = 4 * qg + 4

                def sc_emit(kt, h=h, qg=qg, hb=hb):
                    pn = f"p{kt % 4}"
                    pt = ps[pn]
                    ms.op("pe", lambda e, pt=pt, kt=kt, qg=qg, hb=hb: e.matmul(
                        pt[:, :], kTh[hb][:, kt * 128:(kt + 1) * 128],
                        qTh[hb][:, qg * 512:(qg + 1) * 512], start=True, stop=True),
                        reads=(f"kTh{hb}", f"qTh{hb}"), writes=(pn,))

                def ret_emit(kt, rk=rk, h=h, qg=qg, hb=hb, nkt=nkt):
                    kb = kt % 4
                    ms.op("pe", lambda e, rk=rk, kb=kb, hb=hb, kt=kt, nkt=nkt: e.matmul(
                        ps[rk][:, :], vh[hb][:, kt * 128:(kt + 1) * 128], msst[kb][:, :],
                        start=(kt == 0), stop=(kt == nkt - 1)),
                        reads=(f"vh{hb}", f"msst{kb}"), writes=(rk,), accum_key=("ret", h, qg))

                def den_emit(kt, h=h, qg=qg, nkt=nkt):
                    kb = kt % 4
                    ms.op("pe", lambda e, kb=kb, kt=kt, nkt=nkt: e.matmul(
                        ps["psden"][:, :], onesb[:, 0:1], absst[kb][:, :],
                        start=(kt == 0), stop=(kt == nkt - 1)),
                        reads=("onesb", f"absst{kb}"), writes=("psden",), accum_key=("den", h, qg))

                for kt in range(min(4, nkt)):
                    sc_emit(kt)
                for kt in range(nkt):
                    kb = kt % 4
                    pn = f"p{kb}"
                    pt = ps[pn]
                    didx = 4 * qg - kt + 3
                    r = kt - 4 * qg
                    if r >= 0:
                        ms.op("dve", lambda e, kb=kb, pt=pt, h=h, didx=didx, r=r: e.scalar_tensor_tensor(
                            msst[kb][:, :], pt[:, :], cfu[:, h * 32 + didx: h * 32 + didx + 1],
                            trisb[:, r * 512:(r + 1) * 512], ALU.mult, ALU.mult),
                            reads=(pn, "cfu", "tri"), writes=(f"msst{kb}",))
                    else:
                        ms.op("dve", lambda e, kb=kb, pt=pt, h=h, didx=didx: e.tensor_scalar(
                            msst[kb][:, :], pt[:, :], cfu[:, h * 32 + didx: h * 32 + didx + 1],
                            None, ALU.mult),
                            reads=(pn, "cfu"), writes=(f"msst{kb}",))
                    ms.op("act", lambda e, kb=kb: e.activation(absst[kb][:, :], msst[kb][:, :], AF.Abs),
                          reads=(f"msst{kb}",), writes=(f"absst{kb}",))
                    if kt + 4 < nkt:
                        sc_emit(kt + 4)
                    if kt >= 1:
                        ret_emit(kt - 1)
                    if kt >= 3:
                        den_emit(kt - 3)
                    while cx.a2_pend and cx.a2_pend[0][0] <= kt:
                        cx.a2_pend.pop(0)[1]()
                ret_emit(nkt - 1)
                for fn_due in cx.a2_pend:
                    fn_due[1]()
                cx.a2_pend = []
                den_emit(nkt - 3)
                den_emit(nkt - 2)
                den_emit(nkt - 1)
                def a2_head(rk=rk, qg=qg, h=h, qp=qp):
                    ms.op("act", lambda e, rk=rk: e.activation(sqst[:, :], ps[rk][:, :], AF.Square),
                          reads=(rk,), writes=("sqst",))
                    ms.op("pe", lambda e: e.matmul(ps["psrms"][0:1, :], onesb[:, 0:1], sqst[:, :],
                                                   start=True, stop=True),
                          reads=("onesb", "sqst"), writes=("psrms",))
                    ms.op("dve", lambda e: e.tensor_copy(denrow[:, :], ps["psden"][:, :]),
                          reads=("psden",), writes=("denrow",))
                    ms.op("dve", lambda e: e.tensor_scalar(rmsrow[:, :], ps["psrms"][0:1, :],
                                                           1.0 / 128.0, None, ALU.mult),
                          reads=("psrms",), writes=("rmsrow",))
                    ms.op("dve", lambda e, qp=qp: e.tensor_tensor(dcrow[:, :], denrow[:, :], arow[qp][:, :], ALU.mult),
                          reads=("denrow", f"arow{qp}"), writes=("dcrow",))
                    ms.op("dve", lambda e: e.tensor_scalar(dcrow[:, :], dcrow[:, :], 1.0, 5e4, ALU.max, ALU.min),
                          reads=("dcrow",), writes=("dcrow",))
                    ms.op("dve", lambda e, qp=qp: e.tensor_tensor(dcrow[:, :], dcrow[:, :], sarow[qp][:, :], ALU.mult),
                          reads=("dcrow", f"sarow{qp}"), writes=("dcrow",))
                    ms.op("act", lambda e: e.activation(w2row[:, :], dcrow[:, :], AF.Square),
                          reads=("dcrow",), writes=("w2row",))
                    ms.op("dve", lambda e: e.tensor_tensor(frow[:, :], w2row[:, :], rmsrow[:, :], ALU.add),
                          reads=("w2row", "rmsrow"), writes=("frow",))
                    ms.op("act", lambda e: e.activation(frow[:, :], frow[:, :], AF.Ln),
                          reads=("frow",), writes=("frow",))
                    ms.op("act", lambda e: e.activation(frow[:, :], frow[:, :], AF.Exp, scale=-0.5),
                          reads=("frow",), writes=("frow",))
                def a2_tail(rk=rk, qg=qg, h=h):
                    ms.op("pe", lambda e: e.matmul(ps["psrms"][:, :], onesf[0:1, :], frow[:, :],
                                                   start=True, stop=True),
                          reads=("onesf", "frow"), writes=("psrms",))
                    ms.op("dve", lambda e: e.tensor_copy(fsb[:, :], ps["psrms"][:, :]),
                          reads=("psrms",), writes=("fsb",))
                    ms.op("dve", lambda e, rk=rk: e.tensor_tensor(y1sb[:, :], ps[rk][:, :], fsb[:, :], ALU.mult),
                          reads=(rk, "fsb"), writes=("y1sb",))
                    ms.op("dve", lambda e, qg=qg: e.tensor_tensor(gst[:, :], y1sb[:, :],
                                                                  gTh[:, qg * 512:(qg + 1) * 512], ALU.mult),
                          reads=("y1sb", "gTh"), writes=("gst",))
                    ms.op("sp", lambda e, h=h, qg=qg: e.dma_start(
                        out=D["s_gated"][h, :, qg * 512:(qg + 1) * 512], in_=gst[:, :]),
                        reads=("gst",), writes=(), dma=True)
                cx.a2_pend = [(2, a2_head), (6, a2_tail)]
        for fn_due in cx.a2_pend:
            fn_due[1]()
        cx.a2_pend = []
        ms.run_block(nc)
    with ExitStack() as st:
        ms = MS(sems)
        emit_outproj_ln(nc, cx, st, ms,
                        n_jc=16, wo_dram=D["Wo"], wo_cols=1024, gsrc=D["s_gated"],
                        bias_rep=D["bo_rep"], res_dram=D["x0"], out_dram=D["resid1"],
                        scale=1.0)
        ms.run_block(nc)
    ast_.close()


def emit_outproj_ln(nc, cx, st, ms, n_jc, wo_dram, wo_cols, gsrc, bias_rep, res_dram,
                    out_dram, scale):
    D = cx.D
    ps = alloc_psums(nc, st, [("po0", (128, 512)), ("po1", (128, 512))], ms)
    wo = st.enter_context(nc.sbuf_tensor("wo_a3", [128, n_jc * wo_cols], BF16))
    brep = st.enter_context(nc.sbuf_tensor("brep_a3", [128, wo_cols], F32))
    gin = [st.enter_context(nc.sbuf_tensor(f"gin{i}", [128, n_jc * 128], BF16)) for i in range(2)]
    xres = [st.enter_context(nc.sbuf_tensor(f"xres{i}", [128, wo_cols], F32)) for i in range(2)]
    xb = [st.enter_context(nc.sbuf_tensor(f"xb{i}", [128, wo_cols], F32)) for i in range(2)]
    asb = [st.enter_context(nc.sbuf_tensor(f"asb{i}", [128, wo_cols], F32)) for i in range(2)]
    sums = st.enter_context(nc.sbuf_tensor("sums_a3", [128, 8], F32))
    xcsb = [st.enter_context(nc.sbuf_tensor(f"xcsb{i}", [128, wo_cols], F32)) for i in range(2)]
    sqtr = st.enter_context(nc.sbuf_tensor("sqtr_a3", [128, wo_cols], F32))
    x1st = [st.enter_context(nc.sbuf_tensor(f"x1st{i}", [128, wo_cols], F32)) for i in range(2)]
    ms.op("sp", lambda e: e.dma_start(out=brep[:, :], in_=bias_rep[:, :]), writes=("brep",), dma=True)
    for jc in range(n_jc):
        ms.op("pool", lambda e, jc=jc: e.dma_start(out=wo[:, jc * wo_cols:(jc + 1) * wo_cols],
                                                   in_=wo_dram[jc * 128:(jc + 1) * 128, :]),
              writes=(f"wo{jc}",), dma=True)
    nh = wo_cols // 512
    for tt in range(NT):
        b = tt % 2
        for part in range(4):
            npart = n_jc // 4
            ms.op("sp", lambda e, b=b, tt=tt, part=part, npart=npart: e.dma_start(
                out=gin[b][:, :].rearrange("p (a t) -> p a t", t=128)[:, part * npart:(part + 1) * npart, :],
                in_=gsrc[part * npart:(part + 1) * npart, :, tt * 128:(tt + 1) * 128].rearrange("a p t -> p a t")),
                writes=(f"gin{b}",), dma=True)
        ms.op("sp", lambda e, b=b, tt=tt: e.dma_start(out=xres[b][:, :],
                                                      in_=res_dram[tt * 128:(tt + 1) * 128, :]),
              writes=(f"xres{b}",), dma=True)
        ms.op("pool", lambda e, b=b: e.tensor_tensor(xb[b][:, :], xres[b][:, :], brep[:, :], ALU.add),
              reads=(f"xres{b}", "brep"), writes=(f"xb{b}",))
        for jc in range(n_jc):
            for half in range(nh):
                pt = ps[f"po{half}"]
                ms.op("pe", lambda e, pt=pt, jc=jc, b=b, half=half, n_jc=n_jc: e.matmul(
                    pt[:, :], gin[b][:, jc * 128:(jc + 1) * 128],
                    wo[:, jc * wo_cols + half * 512: jc * wo_cols + half * 512 + 512],
                    start=(jc == 0), stop=(jc == n_jc - 1)),
                    reads=(f"gin{b}", f"wo{jc}"), writes=(f"po{half}",), accum_key=("o", tt, half))
        for half in range(nh):
            ph = f"po{half}"
            pt = ps[ph]
            ms.op("dve", lambda e, pt=pt, b=b, half=half: e.scalar_tensor_tensor(
                asb[b][:, half * 512:(half + 1) * 512], pt[:, :], scale,
                xb[b][:, half * 512:(half + 1) * 512], ALU.mult, ALU.add,
                accum_out=sums[:, b * 4 + half: b * 4 + half + 1]),
                reads=(ph, f"xb{b}"), writes=(f"asb{b}", f"sums{b}"))
        ms.op("dve", lambda e, b=b, nh=nh: e.tensor_reduce(
            sums[:, b * 4 + 3: b * 4 + 4], sums[:, b * 4: b * 4 + nh], axis=AX.X, op=ALU.add)
            if nh > 2 else e.tensor_tensor(sums[:, b * 4 + 3: b * 4 + 4], sums[:, b * 4: b * 4 + 1],
                                           sums[:, b * 4 + 1: b * 4 + 2], ALU.add),
            reads=(f"sums{b}",), writes=(f"msum{b}",))
        ms.op("dve", lambda e, b=b: e.tensor_scalar(sums[:, b * 4 + 3: b * 4 + 4],
                                                    sums[:, b * 4 + 3: b * 4 + 4],
                                                    1.0 / wo_cols, None, ALU.mult),
              reads=(f"msum{b}",), writes=(f"msum{b}",))
        ms.op("dve", lambda e, b=b: e.tensor_scalar(xcsb[b][:, :], asb[b][:, :],
                                                    sums[:, b * 4 + 3: b * 4 + 4], None, ALU.subtract),
              reads=(f"asb{b}", f"msum{b}"), writes=(f"xcsb{b}",))
        ms.op("act", lambda e, b=b: e.activation(sqtr[:, :], xcsb[b][:, :], AF.Square,
                                                 accum_out=sums[:, b * 4 + 2: b * 4 + 3]),
              reads=(f"xcsb{b}",), writes=("sqtr", f"ssq{b}"))
        ms.op("dve", lambda e, b=b: e.tensor_scalar(sums[:, b * 4 + 2: b * 4 + 3],
                                                    sums[:, b * 4 + 2: b * 4 + 3],
                                                    1.0 / wo_cols, 1e-5, ALU.mult, ALU.add),
              reads=(f"ssq{b}",), writes=(f"ssq{b}",))
        ms.op("act", lambda e, b=b: e.activation(sums[:, b * 4 + 2: b * 4 + 3],
                                                 sums[:, b * 4 + 2: b * 4 + 3], AF.Ln),
              reads=(f"ssq{b}",), writes=(f"ssq{b}",))
        ms.op("act", lambda e, b=b: e.activation(sums[:, b * 4 + 2: b * 4 + 3],
                                                 sums[:, b * 4 + 2: b * 4 + 3], AF.Exp, scale=-0.5),
              reads=(f"ssq{b}",), writes=(f"ssq{b}",))
        ms.op("dve", lambda e, b=b: e.tensor_scalar(x1st[b][:, :], xcsb[b][:, :],
                                                    sums[:, b * 4 + 2: b * 4 + 3], None, ALU.mult),
              reads=(f"xcsb{b}", f"ssq{b}"), writes=(f"x1st{b}",))
        ms.op("sp", lambda e, b=b, tt=tt: e.dma_start(out=out_dram[tt * 128:(tt + 1) * 128, :],
                                                      in_=x1st[b][:, :]),
              reads=(f"x1st{b}",), writes=(), dma=True)



PHASE_FNS = {}


def build_program(phases):
    nc = bass.Bass("TRN2", target_bir_lowering=False)
    cx = Ctx()
    cx.ident_loaded = False
    st = ExitStack()
    with st:
        cx.sems = SemState(nc, st)
        declare_dram(nc, cx, phases, phases[0], phases[-1])
        cx.sems.emit_clear(nc)
        cx.identf = st.enter_context(nc.sbuf_tensor("identf", [128, 128], F32))
        for i, p in enumerate(phases):
            if i > 0:
                cx.sems.emit_clear(nc)
            PHASE_FNS[p](nc, cx, st)
    nc.finalize()
    return nc, cx


def load_ident(nc, cx, ms):
    if not cx.ident_loaded:
        D = cx.D
        ms.op("sp", lambda e: e.dma_start(out=cx.identf[:, :], in_=D["ident"][:, :]),
              writes=("ident",), dma=True)
        cx.ident_loaded = True


IN_KEYS = {
    "A": ["ident", "ones_bf", "c_full", "tri", "A_rows", "SAINV_rows", "Wqk", "Wv", "Wg", "Wo",
          "bqk_fm", "bg_fm", "bv_rep", "bo_rep"],
    "B": ["ident", "f0_wgT", "f0_wuT", "f0_wdT"],
    "C": ["ident", "qkvT", "qb_fm", "kb_rep", "vb_rep", "kng_rep", "knb_rep",
          "vng_rep", "vnb_rep", "temp_rep", "kvbias", "p1T", "p1b_rep", "p2T", "p2b_rep"],
    "D": ["ident", "gwT", "gb_rep", "wgE", "wuE", "wdE", "shgT", "shuT", "shdT"],
}
RESID_IN = {"A": "x0", "B": "resid1", "C": "resid2", "D": "resid3"}
RESID_OUT = {"A": "resid1", "B": "resid2", "C": "resid3", "D": "out"}


def run_phases(phases, prep, resid_per_core, n_cores=8):
    nc, cx = build_program(phases)
    keys = set()
    for p in phases:
        keys.update(IN_KEYS[p])
    base = {k: prep[k] for k in sorted(keys) if k in prep}
    in_maps = []
    for b in range(n_cores):
        m = dict(base)
        m[RESID_IN[phases[0]]] = resid_per_core[b]
        in_maps.append(m)
    res = run_bass_kernel_spmd(nc, in_maps, core_ids=list(range(n_cores)))
    okey = RESID_OUT[phases[-1]]
    return [res.results[b][okey] for b in range(n_cores)]


def kernel(**inputs):
    prep = host_prep(inputs)
    x = np.ascontiguousarray(inputs["L0_attn_x"].astype(np.float32, copy=False))
    n_cores = 8
    resid = [np.ascontiguousarray(x[b]) for b in range(n_cores)]
    for group in PHASE_GROUPS:
        resid = run_phases(group, prep, resid, n_cores)
    return np.stack([np.asarray(r, dtype=np.float32) for r in resid], axis=0)


PHASE_GROUPS = [["A", "B", "C", "D"]]
PHASE_FNS["A"] = phaseA


class LNEmit:
    def __init__(self, nc, st, tag):
        self.asb = [sb(nc, st, f"{tag}asb{i}", [128, 1024], F32) for i in range(2)]
        self.sums = sb(nc, st, f"{tag}sums", [128, 8], F32)
        self.sqtr = sb(nc, st, f"{tag}sqtr", [128, 1024], F32)
        self.tag = tag

    def evac_half(self, ms, b, half, src, phkey, xb, xbkey, scale):
        t = self.tag
        ms.op("dve", lambda e, src=src, b=b, half=half: e.scalar_tensor_tensor(
            self.asb[b][:, half * 512:(half + 1) * 512], src, scale,
            xb[:, half * 512:(half + 1) * 512], ALU.mult, ALU.add,
            accum_out=self.sums[:, b * 4 + half: b * 4 + half + 1]),
            reads=(phkey, xbkey), writes=(f"{t}asb{b}", f"{t}sums{b}"))

    def ln(self, ms, b, out_dram, tt):
        t = self.tag
        ms.op("dve", lambda e, b=b: e.tensor_tensor(
            self.sums[:, b * 4 + 3: b * 4 + 4], self.sums[:, b * 4: b * 4 + 1],
            self.sums[:, b * 4 + 1: b * 4 + 2], ALU.add),
            reads=(f"{t}sums{b}",), writes=(f"{t}msum{b}",))
        ms.op("dve", lambda e, b=b: e.tensor_scalar(
            self.sums[:, b * 4 + 3: b * 4 + 4], self.sums[:, b * 4 + 3: b * 4 + 4],
            1.0 / 1024.0, None, ALU.mult),
            reads=(f"{t}msum{b}",), writes=(f"{t}msum{b}",))
        ms.op("dve", lambda e, b=b: e.tensor_scalar(
            self.asb[b][:, :], self.asb[b][:, :], self.sums[:, b * 4 + 3: b * 4 + 4],
            None, ALU.subtract),
            reads=(f"{t}asb{b}", f"{t}msum{b}"), writes=(f"{t}asb{b}",))
        ms.op("act", lambda e, b=b: e.activation(self.sqtr[:, :], self.asb[b][:, :], AF.Square,
                                                 accum_out=self.sums[:, b * 4 + 2: b * 4 + 3]),
              reads=(f"{t}asb{b}",), writes=(f"{t}sqtr", f"{t}ssq{b}"))
        ms.op("dve", lambda e, b=b: e.tensor_scalar(
            self.sums[:, b * 4 + 2: b * 4 + 3], self.sums[:, b * 4 + 2: b * 4 + 3],
            1.0 / 1024.0, 1e-5, ALU.mult, ALU.add),
            reads=(f"{t}ssq{b}",), writes=(f"{t}ssq{b}",))
        ms.op("act", lambda e, b=b: e.activation(self.sums[:, b * 4 + 2: b * 4 + 3],
                                                 self.sums[:, b * 4 + 2: b * 4 + 3], AF.Ln),
              reads=(f"{t}ssq{b}",), writes=(f"{t}ssq{b}",))
        ms.op("act", lambda e, b=b: e.activation(self.sums[:, b * 4 + 2: b * 4 + 3],
                                                 self.sums[:, b * 4 + 2: b * 4 + 3], AF.Exp, scale=-0.5),
              reads=(f"{t}ssq{b}",), writes=(f"{t}ssq{b}",))
        ms.op("dve", lambda e, b=b: e.tensor_scalar(
            self.asb[b][:, :], self.asb[b][:, :], self.sums[:, b * 4 + 2: b * 4 + 3],
            None, ALU.mult),
            reads=(f"{t}asb{b}", f"{t}ssq{b}"), writes=(f"{t}asb{b}",))
        ms.op("act", lambda e, b=b, tt=tt: e.dma_start(out=out_dram[tt * 128:(tt + 1) * 128, :],
                                                       in_=self.asb[b][:, :]),
              reads=(f"{t}asb{b}",), writes=(), dma=True)


B_SKIP_LN = False
B_NG = NG
B_DEBUG = False


def phaseB(nc, cx, prog_stack):
    D = cx.D
    sems = cx.sems
    bst_ = prog_stack.enter_context(ExitStack())
    cx.xT = sb(nc, bst_, "b_xT", [128, 8 * 4096], BF16)
    with ExitStack() as st:
        ms = MS(sems)
        ps = alloc_psums(nc, st, [("p0", (128, 512)), ("p1", (128, 512))])
        build_xT(nc, cx, st, D["resid1"], ms, ps)
        ms.run_block(nc)
    with ExitStack() as st:
        ms = MS(sems)
        ps = alloc_psums(nc, st, [(f"pg{i}", (128, 512)) for i in range(4)] +
                                 [(f"pd{i}", (128, 512)) for i in range(4)], ms)
        wd = sb(nc, st, "f0wd", [128, 24 * 1024], BF16)
        wgc = [sb(nc, st, f"f0wg{i}", [128, 8 * 128], BF16) for i in range(2)]
        wuc = [sb(nc, st, f"f0wu{i}", [128, 8 * 128], BF16) for i in range(2)]
        gu = sb(nc, st, "f0gu", [128, 24 * 512], BF16)
        s1 = [sb(nc, st, f"f0s1{i}", [128, 512], F32) for i in range(2)]
        xres = [sb(nc, st, f"f0x{i}", [128, 1024], F32) for i in range(2)]
        lne = LNEmit(nc, st, "bln")
        for hc in range(24):
            ms.op("pool", lambda e, hc=hc: e.dma_start(out=wd[:, hc * 1024:(hc + 1) * 1024],
                                                       in_=D["f0_wdT"][hc * 128:(hc + 1) * 128, :]),
                  writes=(f"wd{hc}",), dma=True)
        for tg in range(B_NG):
            for hc in range(24):
                wb = hc % 2
                ms.op("pool", lambda e, hc=hc, wb=wb: e.dma_start(
                    out=wgc[wb][:, :].rearrange("p (a b) -> p a b", b=128),
                    in_=D["f0_wgT"][:, hc * 128:(hc + 1) * 128].rearrange("(a p) h -> p a h", p=128)),
                    writes=(f"wgc{wb}",), dma=True)
                ms.op("pool", lambda e, hc=hc, wb=wb: e.dma_start(
                    out=wuc[wb][:, :].rearrange("p (a b) -> p a b", b=128),
                    in_=D["f0_wuT"][:, hc * 128:(hc + 1) * 128].rearrange("(a p) h -> p a h", p=128)),
                    writes=(f"wuc{wb}",), dma=True)
                pgk = f"pg{hc % 2}"
                puk = f"pg{2 + hc % 2}"
                pg = ps[pgk]; pu = ps[puk]
                for cc in range(8):
                    ms.op("pe", lambda e, pg=pg, cc=cc, wb=wb, tg=tg: e.matmul(
                        pg[:, :], wgc[wb][:, cc * 128:(cc + 1) * 128],
                        cx.xT[:, cc * 4096 + tg * 512: cc * 4096 + (tg + 1) * 512],
                        start=(cc == 0), stop=(cc == 7)),
                        reads=(f"wgc{wb}", "xT"), writes=(pgk,), accum_key=("g", hc, tg))
                    ms.op("pe", lambda e, pu=pu, cc=cc, wb=wb, tg=tg: e.matmul(
                        pu[:, :], wuc[wb][:, cc * 128:(cc + 1) * 128],
                        cx.xT[:, cc * 4096 + tg * 512: cc * 4096 + (tg + 1) * 512],
                        start=(cc == 0), stop=(cc == 7)),
                        reads=(f"wuc{wb}", "xT"), writes=(puk,), accum_key=("u", hc, tg))
                s1b = s1[hc % 2]
                ms.op("act", lambda e, s1b=s1b, pg=pg: e.activation(
                    s1b[:, :], pg[:, :], AF.Silu, scale=1.0 / 127.0),
                    reads=(pgk,), writes=(f"s1{hc % 2}",))
                ms.op("dve", lambda e, pu=pu, s1b=s1b, hc=hc: e.scalar_tensor_tensor(
                    gu[:, hc * 512:(hc + 1) * 512],
                    pu[:, :], 1.0 / 127.0, s1b[:, :], ALU.mult, ALU.mult),
                    reads=(puk, f"s1{hc % 2}"), writes=(f"gu{hc}",))
            for t4 in range(4):
                tt = tg * 4 + t4
                b = t4 % 2
                ms.op("sp", lambda e, b=b, tt=tt: e.dma_start(
                    out=xres[b][:, :], in_=D["resid1"][tt * 128:(tt + 1) * 128, :]),
                    writes=(f"f0x{b}",), dma=True)
                for hc in range(24):
                    for half in range(2):
                        pdk = f"pd{(t4 % 2) * 2 + half}"
                        pd = ps[pdk]
                        ms.op("pe", lambda e, pd=pd, hc=hc, t4=t4, half=half: e.matmul(
                            pd[:, :], gu[:, hc * 512 + t4 * 128: hc * 512 + (t4 + 1) * 128],
                            wd[:, hc * 1024 + half * 512: hc * 1024 + half * 512 + 512],
                            start=(hc == 0), stop=(hc == 23)),
                            reads=(f"gu{hc}", f"wd{hc}"), writes=(pdk,),
                            accum_key=("d", tt, half))
                for half in range(2):
                    pdk = f"pd{(t4 % 2) * 2 + half}"
                    pd = ps[pdk]
                    lne.evac_half(ms, b, half, pd[:, :], pdk, xres[b], f"f0x{b}", 1.0 / 127.0)
                if B_SKIP_LN:
                    ms.op("sp", lambda e, b=b, tt=tt: e.dma_start(
                        out=D["resid2"][tt * 128:(tt + 1) * 128, :], in_=lne.asb[b][:, :]),
                        reads=(f"blnasb{b}",), writes=(), dma=True)
                else:
                    lne.ln(ms, b, D["resid2"], tt)
                if B_DEBUG and tt == 0:
                    ms.op("sp", lambda e: e.dma_start(out=D["dbg_gu"][:, :], in_=gu[:, :]),
                          reads=tuple(f"gu{h}" for h in range(24)), writes=(), dma=True)
                    ms.op("sp", lambda e: e.dma_start(out=D["dbg_xT"][:, :], in_=cx.xT[:, :]),
                          reads=("xT",), writes=(), dma=True)
                    ms.op("sp", lambda e: e.dma_start(out=D["dbg_w"][:, :], in_=wgc[1][:, :]),
                          reads=("wgc1",), writes=(), dma=True)
                    ms.op("sp", lambda e, b=b: e.dma_start(out=D["dbg_x"][:, :], in_=xres[b][:, :]),
                          reads=(f"f0x{b}",), writes=(), dma=True)
                    ms.op("sp", lambda e, b=b: e.dma_start(out=D["dbg_a"][:, :], in_=lne.asb[b][:, :]),
                          reads=(f"blnasb{b}",), writes=(), dma=True)
        ms.run_block(nc)
    bst_.close()


PHASE_FNS["B"] = phaseB


def phaseC(nc, cx, prog_stack):
    D = cx.D
    sems = cx.sems
    with ExitStack() as cst:
        kvpair = sb(nc, cst, "c_kvpair", [128, 8 * 128], BF16)
        with ExitStack() as st:
            cx.xT = sb(nc, st, "c_xT", [128, 8 * 4096], BF16)
            with ExitStack() as st0:
                ms = MS(sems)
                psx = alloc_psums(nc, st0, [("p0", (128, 512)), ("p1", (128, 512))], ms)
                build_xT(nc, cx, st0, D["resid2"], ms, psx)
                ms.run_block(nc)
            ms = MS(sems)
            ps = alloc_psums(nc, st, [("pk0", (128, 512)), ("pk1", (128, 512)),
                                      ("pv0", (128, 512)), ("pv1", (128, 512)),
                                      ("pkv0", (64, 512)), ("pkv1", (64, 512)),
                                      ("pq0", (128, 512)), ("pq1", (128, 512))], ms)
            wq = sb(nc, st, "c_wq", [128, 8 * 1024], BF16)
            wkv = sb(nc, st, "c_wkv", [128, 8 * 2048], BF16)
            qb = sb(nc, st, "c_qb", [128, 8], F32)
            reps = {}
            for nme, shape in [("kb_rep", (128, 1024)), ("vb_rep", (128, 1024)),
                               ("kng_rep", (128, 1024)), ("knb_rep", (128, 1024)),
                               ("vng_rep", (128, 1024)), ("vnb_rep", (128, 1024)),
                               ("temp_rep", (64, 1024)), ("kvbias", (64, 1024))]:
                reps[nme] = sb(nc, st, "c_" + nme, list(shape), F32)
                ms.op("sp", lambda e, nme=nme: e.dma_start(out=reps[nme][:, :], in_=D[nme][:, :]),
                      writes=(nme,), dma=True)
            ms.op("sp", lambda e: e.dma_start(out=qb[:, :], in_=D["qb_fm"][:, :]),
                  writes=("qb_fm",), dma=True)
            kraw = [sb(nc, st, f"c_kraw{i}", [128, 1024], F32) for i in range(2)]
            vraw = [sb(nc, st, f"c_vraw{i}", [128, 1024], F32) for i in range(2)]
            stats = {("k", 0): sb(nc, st, "c_stk0", [128, 64], F32),
                     ("k", 1): sb(nc, st, "c_stk1", [128, 64], F32),
                     ("v", 0): sb(nc, st, "c_stv0", [128, 64], F32),
                     ("v", 1): sb(nc, st, "c_stv1", [128, 64], F32)}
            tmp16 = sb(nc, st, "c_tmp16", [128, 16], F32)
            sqt = sb(nc, st, "c_sqt", [128, 1024], F32)
            t3b = [sb(nc, st, "c_t3s", [128, 1024], F32)] * 2
            klnb = [sb(nc, st, f"c_kln{i}", [128, 1024], BF16) for i in range(2)]
            vlnb = [sb(nc, st, f"c_vln{i}", [128, 1024], BF16) for i in range(2)]
            kvf = sb(nc, st, "c_kvf", [64, 1024], F32)
            qst = [sb(nc, st, f"c_qst{i}", [128, 512], BF16) for i in range(4)]
            for cc in range(8):
                ms.op("pool", lambda e, cc=cc: e.dma_start(
                    out=wq[:, cc * 1024:(cc + 1) * 1024],
                    in_=D["qkvT"][cc * 128:(cc + 1) * 128, 0:1024]),
                    writes=(f"wq{cc}",), dma=True)
                ms.op("pool", lambda e, cc=cc: e.dma_start(
                    out=wkv[:, cc * 2048:(cc + 1) * 2048],
                    in_=D["qkvT"][cc * 128:(cc + 1) * 128, 1024:3072]),
                    writes=(f"wkv{cc}",), dma=True)
            for jc in range(8):
                for tgp in range(4):
                    for cc in range(8):
                        for t2 in range(2):
                            tg = tgp * 2 + t2
                            pt = ps[f"pq{t2}"]
                            ms.op("pe", lambda e, pt=pt, cc=cc, jc=jc, tg=tg: e.matmul(
                                pt[:, :], wq[:, cc * 1024 + jc * 128: cc * 1024 + (jc + 1) * 128],
                                cx.xT[:, cc * 4096 + tg * 512: cc * 4096 + (tg + 1) * 512],
                                start=(cc == 0), stop=(cc == 7)),
                                reads=(f"wq{cc}", "xT"), writes=(f"pq{t2}",), accum_key=("q", jc, tg))
                    for t2 in range(2):
                        tg = tgp * 2 + t2
                        phn = f"pq{t2}"
                        pt = ps[phn]
                        sbq = qst[tg % 4]
                        sbk = f"qst{tg % 4}"
                        ms.op("act", lambda e, sbq=sbq, pt=pt, jc=jc: e.activation(
                            sbq[:, :], pt[:, :], AF.Identity, bias=qb[:, jc:jc + 1]),
                            reads=(phn, "qb_fm"), writes=(sbk,))
                        ms.op("sp", lambda e, sbq=sbq, jc=jc, tg=tg: e.dma_start(
                            out=D["s_qc"][jc, :, tg * 512:(tg + 1) * 512], in_=sbq[:, :]),
                            reads=(sbk,), writes=(), dma=True)
            for tt in range(NT):
                b = tt % 2
                for cc in range(8):
                    for nm in ("k", "v"):
                        wofs = 0 if nm == "k" else 1024
                        pko = "pk" if nm == "k" else "pv"
                        for half in range(2):
                            phn = f"{pko}{half}"
                            pt = ps[phn]
                            ms.op("pe", lambda e, pt=pt, cc=cc, tt=tt, half=half, wofs=wofs: e.matmul(
                                pt[:, :], cx.xT[:, cc * 4096 + tt * 128: cc * 4096 + (tt + 1) * 128],
                                wkv[:, cc * 2048 + wofs + half * 512: cc * 2048 + wofs + half * 512 + 512],
                                start=(cc == 0), stop=(cc == 7)),
                                reads=(f"wkv{cc}", "xT"), writes=(phn,), accum_key=(nm, tt, half))
                for nmi, nm in enumerate(("k", "v")):
                    raw = (kraw if nm == "k" else vraw)[b]
                    rawk = f"c_{nm}raw{b}"
                    stt_ = stats[(nm, b)]
                    stk = f"c_st{nm}{b}"
                    lnout = (klnb if nm == "k" else vlnb)[b]
                    lnk = f"c_{nm}ln{b}"
                    gname = f"{nm}ng_rep"; bname = f"{nm}nb_rep"
                    biasr = reps["kb_rep" if nm == "k" else "vb_rep"]
                    biask = "kb_rep" if nm == "k" else "vb_rep"
                    pko = "pk" if nm == "k" else "pv"
                    for half in range(2):
                        phn = f"{pko}{half}"
                        pt = ps[phn]
                        ms.op("dve", lambda e, raw=raw, pt=pt, half=half, biasr=biasr: e.scalar_tensor_tensor(
                            raw[:, half * 512:(half + 1) * 512], pt[:, :], 1.0,
                            biasr[:, half * 512:(half + 1) * 512], ALU.bypass, ALU.add),
                            reads=(phn, biask), writes=(rawk,))
                    ms.op("dve", lambda e, raw=raw, stt_=stt_: e.reduce_sum(
                        stt_[:, 0:16], raw[:, :].rearrange("p (a b) -> p a b", b=64), axis=AX.X),
                        reads=(rawk,), writes=(stk + "s",))
                    ms.op("act", lambda e, raw=raw: e.activation(sqt[:, :], raw[:, :], AF.Square),
                          reads=(rawk,), writes=("c_sqt",))
                    ms.op("dve", lambda e, stt_=stt_: e.reduce_sum(
                        stt_[:, 16:32], sqt[:, :].rearrange("p (a b) -> p a b", b=64), axis=AX.X),
                        reads=("c_sqt",), writes=(stk + "q",))
                    ms.op("dve", lambda e, stt_=stt_: e.tensor_scalar(
                        stt_[:, 32:48], stt_[:, 0:16], 1.0 / 64.0, None, ALU.mult),
                        reads=(stk + "s",), writes=(stk + "m",))
                    ms.op("dve", lambda e, stt_=stt_: e.tensor_tensor(
                        tmp16[:, :], stt_[:, 32:48], stt_[:, 32:48], ALU.mult),
                        reads=(stk + "m",), writes=("c_tmp16",))
                    ms.op("dve", lambda e, stt_=stt_: e.tensor_scalar(
                        stt_[:, 48:64], stt_[:, 16:32], 1.0 / 64.0, None, ALU.mult),
                        reads=(stk + "q",), writes=(stk + "v",))
                    ms.op("dve", lambda e, stt_=stt_: e.tensor_tensor(
                        stt_[:, 48:64], stt_[:, 48:64], tmp16[:, :], ALU.subtract),
                        reads=(stk + "v", "c_tmp16"), writes=(stk + "v",))
                    ms.op("dve", lambda e, stt_=stt_: e.tensor_scalar(
                        stt_[:, 48:64], stt_[:, 48:64], 1.0, 1e-5, ALU.mult, ALU.add),
                        reads=(stk + "v",), writes=(stk + "v",))
                    ms.op("act", lambda e, stt_=stt_: e.activation(
                        stt_[:, 48:64], stt_[:, 48:64], AF.Ln),
                        reads=(stk + "v",), writes=(stk + "v",))
                    ms.op("act", lambda e, stt_=stt_: e.activation(
                        stt_[:, 48:64], stt_[:, 48:64], AF.Exp, scale=-0.5),
                        reads=(stk + "v",), writes=(stk + "v",))
                    t3 = t3b[b]
                    m3d = stt_[:, 32:48].rearrange("p (a b) -> p a b", b=1).broadcast_to([128, 16, 64])
                    r3d = stt_[:, 48:64].rearrange("p (a b) -> p a b", b=1).broadcast_to([128, 16, 64])
                    raw3d = raw[:, :].rearrange("p (a b) -> p a b", b=64)
                    t33d = t3[:, :].rearrange("p (a b) -> p a b", b=64)
                    ms.op("dve", lambda e, t33d=t33d, raw3d=raw3d, m3d=m3d: e.tensor_tensor(
                        t33d, raw3d, m3d, ALU.subtract),
                        reads=(rawk, stk + "m"), writes=("c_t3",))
                    ms.op("pool", lambda e, t33d=t33d, r3d=r3d: e.tensor_tensor(
                        t33d, t33d, r3d, ALU.mult),
                        reads=("c_t3", stk + "v"), writes=("c_t3",))
                    ms.op("dve", lambda e, t3=t3, g=reps[gname]: e.tensor_tensor(
                        t3[:, :], t3[:, :], g[:, :], ALU.mult),
                        reads=("c_t3", gname), writes=("c_t3",))
                    ms.op("dve", lambda e, lnout=lnout, t3=t3, bb=reps[bname]: e.tensor_tensor(
                        lnout[:, :], t3[:, :], bb[:, :], ALU.add),
                        reads=("c_t3", bname), writes=(lnk,))
                for h in range(16):
                    bank = h // 8
                    phn = f"pkv{bank}"
                    pt = ps[phn]
                    ms.op("pe", lambda e, pt=pt, h=h, tt=tt, b=b: e.matmul(
                        pt[:, (h % 8) * 64:(h % 8) * 64 + 64],
                        klnb[b][:, h * 64:(h + 1) * 64], vlnb[b][:, h * 64:(h + 1) * 64],
                        start=(tt == 0 and h % 8 == 0), stop=(tt == NT - 1 and h % 8 == 7)),
                        reads=(f"c_kln{b}", f"c_vln{b}"), writes=(phn,), accum_key=("kv", bank))
            for bank in range(2):
                pt = ps[f"pkv{bank}"]
                ms.op("dve", lambda e, pt=pt, bank=bank: e.scalar_tensor_tensor(
                    kvf[:, bank * 512:(bank + 1) * 512], pt[:, :], 1.0 / 4096.0,
                    reps["temp_rep"][:, bank * 512:(bank + 1) * 512], ALU.mult, ALU.mult),
                    reads=(f"pkv{bank}", "temp_rep"), writes=("c_kvf",))
            ms.op("pool", lambda e: e.tensor_tensor(kvf[:, :], kvf[:, :], reps["kvbias"][:, :], ALU.add),
                  reads=("c_kvf", "kvbias"), writes=("c_kvf",))
            ms.op("pool", lambda e: e.memset(kvpair[:, :], 0.0), writes=("kvpair",))
            for h in range(16):
                pr = h // 2
                rofs = (h % 2) * 64
                cofs = pr * 128 + (h % 2) * 64
                ms.op("dve", lambda e, h=h, rofs=rofs, cofs=cofs: e.tensor_copy(
                    kvpair[rofs:rofs + 64, cofs:cofs + 64], kvf[:, h * 64:(h + 1) * 64]),
                    reads=("c_kvf",), writes=("kvpair",))
            ms.run_block(nc)
        with ExitStack() as st:
            ms = MS(sems)
            ps = alloc_psums(nc, st, [("pa0", (128, 512)), ("pa1", (128, 512)),
                                      ("pt0", (128, 512)), ("pt1", (128, 512)),
                                      ("pp0", (128, 512)), ("pp1", (128, 512)),
                                      ("pq0", (128, 512)), ("pq1", (128, 512))], ms)
            p1w = sb(nc, st, "c_p1w", [128, 8 * 1024], BF16)
            p2w = sb(nc, st, "c_p2w", [128, 8 * 1024], BF16)
            p1b = sb(nc, st, "c_p1b", [128, 1024], F32)
            p2b = sb(nc, st, "c_p2b", [128, 1024], F32)
            x2r = [sb(nc, st, f"c_x2r{i}", [128, 1024], F32) for i in range(2)]
            xb2 = [sb(nc, st, f"c_xb2{i}", [128, 1024], F32) for i in range(2)]
            a1 = [sb(nc, st, f"c_a1{i}", [128, 1024], F32) for i in range(2)]
            a1T = [sb(nc, st, f"c_a1T{i}", [128, 1024], BF16) for i in range(2)]
            h1f = [sb(nc, st, f"c_h1f{i}", [128, 1024], F32) for i in range(2)]
            h1g = [sb(nc, st, f"c_h1g{i}", [128, 1024], F32) for i in range(2)]
            h1T = [sb(nc, st, f"c_h1T{i}", [128, 1024], BF16) for i in range(2)]
            qtile = [sb(nc, st, f"c_qtile{i}", [128, 8 * 128], BF16) for i in range(2)]
            lne = LNEmit(nc, st, "cln")
            ms.op("sp", lambda e: e.dma_start(out=p1b[:, :], in_=D["p1b_rep"][:, :]), writes=("p1b",), dma=True)
            ms.op("sp", lambda e: e.dma_start(out=p2b[:, :], in_=D["p2b_rep"][:, :]), writes=("p2b",), dma=True)
            for cc in range(8):
                ms.op("pool", lambda e, cc=cc: e.dma_start(
                    out=p1w[:, cc * 1024:(cc + 1) * 1024],
                    in_=D["p1T"][cc * 128:(cc + 1) * 128, :]),
                    writes=(f"p1w{cc}",), dma=True)
                ms.op("pool", lambda e, cc=cc: e.dma_start(
                    out=p2w[:, cc * 1024:(cc + 1) * 1024],
                    in_=D["p2T"][cc * 128:(cc + 1) * 128, :]),
                    writes=(f"p2w{cc}",), dma=True)
            def transpose_group(tt, srck, srct, dstk, dstt, b):
                for half in range(2):
                    phn = f"pt{half}"
                    pt = ps[phn]
                    for k in range(4):
                        cc = half * 4 + k
                        ms.op("pe", lambda e, pt=pt, k=k, cc=cc, srct=srct: e.matmul(
                            pt[:, k * 128:(k + 1) * 128], srct[:, cc * 128:(cc + 1) * 128],
                            cx.identf[:, :], start=(k == 0), stop=(k == 3), is_transpose=True),
                            reads=(srck, "ident"), writes=(phn,), accum_key=("tr", tt, srck, half))
                    q = "dve" if half == 0 else "act"
                    dst = dstt[:, half * 512:(half + 1) * 512]
                    if q == "dve":
                        ms.op(q, lambda e, pt=pt, dst=dst: e.tensor_copy(dst, pt[:, :]),
                              reads=(phn,), writes=(dstk,))
                    else:
                        ms.op(q, lambda e, pt=pt, dst=dst: e.copy(dst, pt[:, :]),
                              reads=(phn,), writes=(dstk,))
            for tt in range(NT):
                b = tt % 2
                ms.op("sp", lambda e, b=b, tt=tt: e.dma_start(
                    out=x2r[b][:, :], in_=D["resid2"][tt * 128:(tt + 1) * 128, :]),
                    writes=(f"x2r{b}",), dma=True)
                ms.op("sp", lambda e, b=b, tt=tt: e.dma_start(
                    out=qtile[b][:, :].rearrange("p (a t) -> p a t", t=128),
                    in_=D["s_qc"][:, :, tt * 128:(tt + 1) * 128].rearrange("a p t -> p a t")),
                    writes=(f"qtile{b}",), dma=True)
                ms.op("pool", lambda e, b=b: e.tensor_tensor(
                    xb2[b][:, :], x2r[b][:, :], x2r[b][:, :], ALU.add),
                    reads=(f"x2r{b}",), writes=(f"xb2{b}",))
                ms.op("pool", lambda e, b=b: e.tensor_tensor(
                    xb2[b][:, :], xb2[b][:, :], p2b[:, :], ALU.add),
                    reads=(f"xb2{b}", "p2b"), writes=(f"xb2{b}",))
                for jc in range(8):
                    phn = f"pa{jc // 4}"
                    pt = ps[phn]
                    ms.op("pe", lambda e, pt=pt, jc=jc, b=b: e.matmul(
                        pt[:, (jc % 4) * 128:(jc % 4) * 128 + 128],
                        qtile[b][:, jc * 128:(jc + 1) * 128],
                        kvpair[:, jc * 128:(jc + 1) * 128],
                        start=(jc % 4 == 0), stop=(jc % 4 == 3)),
                        reads=(f"qtile{b}", "kvpair"), writes=(phn,), accum_key=("att", tt, jc // 4))
                for half in range(2):
                    pt = ps[f"pa{half}"]
                    ms.op("dve", lambda e, pt=pt, b=b, half=half: e.scalar_tensor_tensor(
                        a1[b][:, half * 512:(half + 1) * 512], pt[:, :], 1.0,
                        x2r[b][:, half * 512:(half + 1) * 512], ALU.bypass, ALU.add),
                        reads=(f"pa{half}", f"x2r{b}"), writes=(f"a1{b}",))
                transpose_group(tt, f"a1{b}", a1[b], f"a1T{b}", a1T[b], b)
                for cc in range(8):
                    for half in range(2):
                        pt = ps[f"pp{half}"]
                        ms.op("pe", lambda e, pt=pt, cc=cc, b=b, half=half: e.matmul(
                            pt[:, :], a1T[b][:, cc * 128:(cc + 1) * 128],
                            p1w[:, cc * 1024 + half * 512: cc * 1024 + half * 512 + 512],
                            start=(cc == 0), stop=(cc == 7)),
                            reads=(f"a1T{b}", f"p1w{cc}"), writes=(f"pp{half}",), accum_key=("p1", tt, half))
                for half in range(2):
                    phn = f"pp{half}"
                    pt = ps[phn]
                    ms.op("dve", lambda e, pt=pt, b=b, half=half: e.scalar_tensor_tensor(
                        h1f[b][:, half * 512:(half + 1) * 512], pt[:, :], 1.0,
                        p1b[:, half * 512:(half + 1) * 512], ALU.bypass, ALU.add),
                        reads=(phn, "p1b"), writes=(f"h1f{b}",))
                ms.op("act", lambda e, b=b: e.activation(h1g[b][:, :], h1f[b][:, :], AF.Gelu),
                      reads=(f"h1f{b}",), writes=(f"h1g{b}",))
                transpose_group(tt, f"h1g{b}", h1g[b], f"h1T{b}", h1T[b], b)
                for cc in range(8):
                    for half in range(2):
                        pt = ps[f"pq{half}"]
                        ms.op("pe", lambda e, pt=pt, cc=cc, b=b, half=half: e.matmul(
                            pt[:, :], h1T[b][:, cc * 128:(cc + 1) * 128],
                            p2w[:, cc * 1024 + half * 512: cc * 1024 + half * 512 + 512],
                            start=(cc == 0), stop=(cc == 7)),
                            reads=(f"h1T{b}", f"p2w{cc}"), writes=(f"pq{half}",), accum_key=("p2", tt, half))
                for half in range(2):
                    phn = f"pq{half}"
                    pt = ps[phn]
                    lne.evac_half(ms, b, half, pt[:, :], phn, xb2[b], f"xb2{b}", 1.0)
                lne.ln(ms, b, D["resid3"], tt)
            ms.run_block(nc)


PHASE_FNS["C"] = phaseC


D_NQ = 4
D_NPAIR = 17


def phaseD(nc, cx, prog_stack):
    D = cx.D
    sems = cx.sems
    with ExitStack() as st:
        cx.xT = sb(nc, st, "d_xT", [128, 8 * 4096], BF16)
        ms = MS(sems)
        ps = alloc_psums(nc, st, [("pg0", (128, 512)), ("pg1", (128, 512)),
                                  ("pu0", (128, 512)), ("pu1", (128, 512)),
                                  ("pd0", (128, 512)), ("pd1", (128, 512)),
                                  ("pcb", (128, 512)), ("pex", (128, 512))], ms)
        ms.psum_keys.update(["plg", "ptc"])
        plg = ps["pex"][:, 0:32]
        ptc = ps["pex"][0:32, 128:256]
        load_ident(nc, cx, ms)
        xt = [sb(nc, st, f"d_xt{i}", [128, 1024], F32) for i in range(2)]
        stg = sb(nc, st, "d_stg", [128, 1024], F32)
        gw = sb(nc, st, "d_gw", [128, 8 * 32], F32)
        gbr = sb(nc, st, "d_gbr", [128, 32], F32)
        scores = sb(nc, st, "d_scores", [128, 32], F32)
        scb = sb(nc, st, "d_scb", [128, 32], F32)
        eqb = sb(nc, st, "d_eqb", [128, 32], F32)
        wrk = sb(nc, st, "d_wrk", [128, 32], F32)
        gm1 = sb(nc, st, "d_gm1", [128, 8], F32)
        gm2 = sb(nc, st, "d_gm2", [128, 8], F32)
        gs = sb(nc, st, "d_gs", [128, 8], F32)
        g8s = sb(nc, st, "d_g8s", [128, 8], F32)
        gsm = sb(nc, st, "d_gsm", [128, 8], F32)
        sm = sb(nc, st, "d_sm", [128, 32], F32)
        mkd = sb(nc, st, "d_mkd", [128, 32], F32)
        mm8 = sb(nc, st, "d_mm8", [128, 8], F32)
        mrk = sb(nc, st, "d_mrk", [128, 32], F32)
        sel = sb(nc, st, "d_sel", [128, 32], F32)
        cw = sb(nc, st, "d_cw", [128, 32], F32)
        den = sb(nc, st, "d_den", [128, 2], F32)
        comb = sb(nc, st, "d_comb", [128, 32], F32)
        combT = sb(nc, st, "d_combT", [32, 1024], BF16)
        racc = sb(nc, st, "d_racc", [128, 8 * 1024], F32)
        wge = [sb(nc, st, f"d_wge{i}", [128, 8 * 512], BF16) for i in range(2)]
        wue = [sb(nc, st, f"d_wue{i}", [128, 8 * 512], BF16) for i in range(2)]
        wde = [sb(nc, st, f"d_wde{i}", [128, 4 * 1024], BF16) for i in range(2)]
        gup = sb(nc, st, "d_gup", [128, 8 * 512], BF16)
        s1b = [sb(nc, st, f"d_s1{i}", [128, 512], F32) for i in range(2)]
        s1c = [sb(nc, st, f"d_s1c{i}", [128, 512], F32) for i in range(2)]
        cbs = [sb(nc, st, f"d_cbs{i}", [128, 512], BF16) for i in range(2)]
        cbones = sb(nc, st, "d_cbones", [128, 512], BF16)
        cbrow = [sb(nc, st, f"d_cbrow{i}", [1, 1024], BF16) for i in range(2)]
        onesb1 = sb(nc, st, "d_ones1", [1, 128], BF16)
        x3r = [sb(nc, st, f"d_x3r{i}", [128, 1024], F32) for i in range(2)]
        lne = LNEmit(nc, st, "dln")
        ms.op("pool", lambda e: e.memset(cbones[:, :], 1.0), writes=("cbones",))
        ms.op("pool", lambda e: e.memset(onesb1[:, :], 1.0), writes=("ones1",))
        ms.op("sp", lambda e: e.dma_start(
            out=gw[:, :].rearrange("p (a b) -> p a b", b=32),
            in_=D["gwT"][:, :].rearrange("(a p) e -> p a e", p=128)),
            writes=("gw",), dma=True)
        ms.op("sp", lambda e: e.dma_start(out=gbr[:, :], in_=D["gb_rep"][:, :]),
              writes=("gbr",), dma=True)

        def load_pair_weights(pi):
            shared = (pi == 0)
            for eq in range(2):
                eb = eq
                if shared:
                    wg_in = D["shgT"][:, eq * 512:(eq + 1) * 512]
                    wu_in = D["shuT"][:, eq * 512:(eq + 1) * 512]
                    wd_in = D["shdT"][eq * 512:(eq + 1) * 512, :]
                else:
                    e_ = (pi - 1) * 2 + eq
                    wg_in = D["wgE"][e_]
                    wu_in = D["wuE"][e_]
                    wd_in = D["wdE"][e_]
                ms.op("pool", lambda e, wg_in=wg_in, eb=eb: e.dma_start(
                    out=wge[eb][:, :].rearrange("p (a b) -> p a b", b=512),
                    in_=wg_in.rearrange("(a p) i -> p a i", p=128)),
                    writes=(f"wge{eb}",), dma=True)
                ms.op("pool", lambda e, wu_in=wu_in, eb=eb: e.dma_start(
                    out=wue[eb][:, :].rearrange("p (a b) -> p a b", b=512),
                    in_=wu_in.rearrange("(a p) i -> p a i", p=128)),
                    writes=(f"wue{eb}",), dma=True)
                ms.op("pool", lambda e, wd_in=wd_in, eb=eb: e.dma_start(
                    out=wde[eb][:, :].rearrange("p (a b) -> p a b", b=1024),
                    in_=wd_in.rearrange("(a p) c -> p a c", p=128)),
                    writes=(f"wde{eb}",), dma=True)

        for Q in range(D_NQ):
            load_pair_weights(0)
            for t8 in range(8):
                tt = Q * 8 + t8
                b = tt % 2
                ms.op("sp", lambda e, b=b, tt=tt: e.dma_start(
                    out=xt[b][:, :], in_=D["resid3"][tt * 128:(tt + 1) * 128, :]),
                    writes=(f"xt{b}",), dma=True)
                for half in range(2):
                    phn = f"pd{half}"
                    pt = ps[phn]
                    for k in range(4):
                        cc = half * 4 + k
                        ms.op("pe", lambda e, pt=pt, k=k, b=b, cc=cc: e.matmul(
                            pt[:, k * 128:(k + 1) * 128], xt[b][:, cc * 128:(cc + 1) * 128],
                            cx.identf[:, :], start=(k == 0), stop=(k == 3), is_transpose=True),
                            reads=(f"xt{b}", "ident"), writes=(phn,), accum_key=("tr", tt, half))
                    ms.op("dve", lambda e, pt=pt, half=half: e.tensor_copy(
                        stg[:, half * 512:(half + 1) * 512], pt[:, :]),
                        reads=(phn,), writes=("stg",))
                    dstv = cx.xT[:, :].rearrange("p (a c) -> p a c", c=4096)[:, half * 4:half * 4 + 4,
                                                                            tt * 128:(tt + 1) * 128]
                    ms.op("act", lambda e, pt=pt, dstv=dstv: e.copy(
                        dstv, pt[:, :].rearrange("p (a c) -> p a c", c=128)),
                        reads=(phn,), writes=(f"xTq{Q}",))
                for cc in range(8):
                    ms.op("pe", lambda e, cc=cc: e.matmul(
                        plg, stg[:, cc * 128:(cc + 1) * 128],
                        gw[:, cc * 32:(cc + 1) * 32], start=(cc == 0), stop=(cc == 7)),
                        reads=("stg", "gw"), writes=("plg",), accum_key=("lg", tt))
                ms.op("act", lambda e: e.activation(scores[:, :], plg, AF.Sigmoid),
                      reads=("plg",), writes=("scores",))
                ms.op("pool", lambda e: e.tensor_tensor(scb[:, :], scores[:, :], gbr[:, :], ALU.add),
                      reads=("scores", "gbr"), writes=("scb",))
                scb3 = scb[:, :].rearrange("p (a b) -> p a b", b=4)
                ms.op("dve", lambda e, scb3=scb3: e.reduce_max(gm1[:, :], scb3, axis=AX.X),
                      reads=("scb",), writes=("gm1",))
                m1b = gm1[:, :].rearrange("p (a b) -> p a b", b=1).broadcast_to([128, 8, 4])
                eqb3 = eqb[:, :].rearrange("p (a b) -> p a b", b=4)
                ms.op("dve", lambda e, scb3=scb3, m1b=m1b, eqb3=eqb3: e.tensor_tensor(
                    eqb3, scb3, m1b, ALU.is_equal),
                    reads=("scb", "gm1"), writes=("eqb",))
                ms.op("dve", lambda e: e.scalar_tensor_tensor(
                    wrk[:, :], eqb[:, :], -1e30, scb[:, :], ALU.mult, ALU.add),
                    reads=("eqb", "scb"), writes=("wrk",))
                wrk3 = wrk[:, :].rearrange("p (a b) -> p a b", b=4)
                ms.op("dve", lambda e, wrk3=wrk3: e.reduce_max(gm2[:, :], wrk3, axis=AX.X),
                      reads=("wrk",), writes=("gm2",))
                ms.op("pool", lambda e: e.tensor_tensor(gs[:, :], gm1[:, :], gm2[:, :], ALU.add),
                      reads=("gm1", "gm2"), writes=("gs",))
                ms.op("dve", lambda e: e.max(g8s[:, :], gs[:, :]),
                      reads=("gs",), writes=("g8s",))
                ms.op("dve", lambda e: e.tensor_scalar(gsm[:, :], gs[:, :], g8s[:, 3:4], None, ALU.is_ge),
                      reads=("g8s", "gs"), writes=("gsm",))
                sm3 = sm[:, :].rearrange("p (a b) -> p a b", b=4)
                gb3 = gsm[:, :].rearrange("p (a b) -> p a b", b=1).broadcast_to([128, 8, 4])
                ms.op("dve", lambda e, sm3=sm3, gb3=gb3: e.tensor_copy(sm3, gb3),
                      reads=("gsm",), writes=("sm",))
                ms.op("pool", lambda e: e.tensor_tensor(mkd[:, :], sm[:, :], scb[:, :], ALU.mult),
                      reads=("sm", "scb"), writes=("mkd",))
                ms.op("dve", lambda e: e.max(mm8[:, :], mkd[:, :]),
                      reads=("mkd",), writes=("mm8",))
                ms.op("dve", lambda e: e.match_replace(mrk[:, :], mm8[:, :], mkd[:, :], -1e30),
                      reads=("mm8", "mkd"), writes=("mrk",))
                ms.op("dve", lambda e: e.tensor_single_scalar(sel[:, :], mrk[:, :], -1e20, ALU.is_lt),
                      reads=("mrk",), writes=("sel",))
                ms.op("pool", lambda e: e.tensor_tensor(cw[:, :], sel[:, :], scores[:, :], ALU.mult),
                      reads=("sel", "scores"), writes=("cw",))
                ms.op("dve", lambda e: e.reduce_sum(den[:, 0:1], cw[:, :], axis=AX.X),
                      reads=("cw",), writes=("den",))
                ms.op("dve", lambda e: e.tensor_scalar(den[:, 0:1], den[:, 0:1], 1.0, 1e-20,
                                                        ALU.mult, ALU.add),
                      reads=("den",), writes=("den",))
                ms.op("dve", lambda e: e.reciprocal(den[:, 1:2], den[:, 0:1]),
                      reads=("den",), writes=("denr",))
                ms.op("dve", lambda e: e.tensor_scalar(comb[:, :], cw[:, :], den[:, 1:2], 2.5,
                                                        ALU.mult, ALU.mult),
                      reads=("cw", "denr"), writes=("comb",))
                ms.op("pe", lambda e: e.transpose(ptc, comb[:, :], cx.identf[:, :]),
                      reads=("comb", "ident"), writes=("ptc",))
                ms.op("dve", lambda e, t8=t8: e.tensor_copy(
                    combT[:, t8 * 128:(t8 + 1) * 128], ptc),
                    reads=("ptc",), writes=("combT",))
            for t8 in range(2):
                tt = Q * 8 + t8
                ms.op("act", lambda e, t8=t8, tt=tt: e.dma_start(
                    out=x3r[t8 % 2][:, :], in_=D["resid3"][tt * 128:(tt + 1) * 128, :]),
                    writes=(f"d_x3r{t8 % 2}",), dma=True)
            for pi in range(D_NPAIR):
                shared = (pi == 0)
                if pi > 0:
                    load_pair_weights(pi)
                    for eq in range(2):
                        e_ = (pi - 1) * 2 + eq
                        ms.op("sp", lambda e, e_=e_, eq=eq: e.dma_start(
                            out=cbrow[eq][:, :], in_=combT[e_:e_ + 1, :]),
                            reads=("combT",), writes=(f"cbrow{eq}",), dma=True)
                for tg2 in range(2):
                    tgo = Q * 1024 + tg2 * 512
                    for eq in range(2):
                        eb = eq
                        if shared:
                            cbk = "cbones"
                            cbt = cbones
                        else:
                            cbt = cbs[eq]
                            cbk = f"cbs{eq}"
                            ms.op("pe", lambda e, eb=eb, tg2=tg2: e.matmul(
                                ps["pcb"][:, :], onesb1[:, :],
                                cbrow[eb][:, tg2 * 512:(tg2 + 1) * 512],
                                start=True, stop=True),
                                reads=(f"cbrow{eb}", "ones1"), writes=("pcb",))
                            ms.op("dve", lambda e, cbt=cbt: e.tensor_copy(cbt[:, :], ps["pcb"][:, :]),
                                  reads=("pcb",), writes=(cbk,))
                        for ic in range(4):
                            pgk = f"pg{(ic) % 2}"; puk = f"pu{(ic) % 2}"
                            pg = ps[pgk]; pu = ps[puk]
                            for cc in range(8):
                                ms.op("pe", lambda e, pg=pg, cc=cc, eb=eb, ic=ic, tgo=tgo: e.matmul(
                                    pg[:, :], wge[eb][:, cc * 512 + ic * 128: cc * 512 + (ic + 1) * 128],
                                    cx.xT[:, cc * 4096 + tgo: cc * 4096 + tgo + 512],
                                    start=(cc == 0), stop=(cc == 7)),
                                    reads=(f"wge{eb}", f"xTq{Q}"), writes=(pgk,), accum_key=("g", Q, pi, eq, ic, tg2))
                                ms.op("pe", lambda e, pu=pu, cc=cc, eb=eb, ic=ic, tgo=tgo: e.matmul(
                                    pu[:, :], wue[eb][:, cc * 512 + ic * 128: cc * 512 + (ic + 1) * 128],
                                    cx.xT[:, cc * 4096 + tgo: cc * 4096 + tgo + 512],
                                    start=(cc == 0), stop=(cc == 7)),
                                    reads=(f"wue{eb}", f"xTq{Q}"), writes=(puk,), accum_key=("u", Q, pi, eq, ic, tg2))
                            sb1 = s1b[ic % 2]
                            ms.op("act", lambda e, sb1=sb1, pg=pg: e.activation(
                                sb1[:, :], pg[:, :], AF.Silu),
                                reads=(pgk,), writes=(f"s1{ic % 2}",))
                            sc1 = s1c[ic % 2]
                            ms.op("pool", lambda e, sc1=sc1, sb1=sb1, cbt=cbt: e.tensor_tensor(
                                sc1[:, :], sb1[:, :], cbt[:, :], ALU.mult),
                                reads=(f"s1{ic % 2}", cbk), writes=(f"s1c{ic % 2}",))
                            ms.op("dve", lambda e, pu=pu, sc1=sc1, eq=eq, ic=ic: e.tensor_tensor(
                                gup[:, (eq * 4 + ic) * 512:(eq * 4 + ic + 1) * 512],
                                pu[:, :], sc1[:, :], ALU.mult),
                                reads=(puk, f"s1c{ic % 2}"), writes=(f"gup{eq}_{ic}",))
                    for t4 in range(4):
                        tt = (tgo // 128) + t4
                        for eq in range(2):
                            for ic in range(4):
                                for half in range(2):
                                    pdk = f"pd{half}"
                                    pd = ps[pdk]
                                    ms.op("pe", lambda e, pd=pd, eq=eq, ic=ic, t4=t4, half=half: e.matmul(
                                        pd[:, :],
                                        gup[:, (eq * 4 + ic) * 512 + t4 * 128:(eq * 4 + ic) * 512 + (t4 + 1) * 128],
                                        wde[eq][:, ic * 1024 + half * 512: ic * 1024 + half * 512 + 512],
                                        start=(eq == 0 and ic == 0), stop=(eq == 1 and ic == 3)),
                                        reads=(f"gup{eq}_{ic}", f"wde{eq}"), writes=(pdk,),
                                        accum_key=("d", Q, pi, tg2, t4, half))
                        for half in range(2):
                            pdk = f"pd{half}"
                            pd = ps[pdk]
                            loc = (tt - Q * 8) * 1024 + half * 512
                            if shared:
                                ms.op("dve", lambda e, pd=pd, loc=loc: e.tensor_copy(
                                    racc[:, loc:loc + 512], pd[:, :]),
                                    reads=(pdk,), writes=(f"racc{loc}",))
                            else:
                                ms.op("dve", lambda e, pd=pd, loc=loc: e.scalar_tensor_tensor(
                                    racc[:, loc:loc + 512], pd[:, :], 1.0,
                                    racc[:, loc:loc + 512], ALU.bypass, ALU.add),
                                    reads=(pdk, f"racc{loc}"), writes=(f"racc{loc}",))
            for t8 in range(8):
                tt = Q * 8 + t8
                b = t8 % 2
                for half in range(2):
                    loc = t8 * 1024 + half * 512
                    lne.evac_half(ms, b, half, racc[:, loc:loc + 512], f"racc{loc}",
                                  x3r[t8 % 2], f"d_x3r{t8 % 2}", 1.0)
                lne.ln(ms, b, D["out"], tt)
                if t8 < 6:
                    tt2 = Q * 8 + t8 + 2
                    ms.op("act", lambda e, t8=t8, tt2=tt2: e.dma_start(
                        out=x3r[t8 % 2][:, :], in_=D["resid3"][tt2 * 128:(tt2 + 1) * 128, :]),
                        writes=(f"d_x3r{t8 % 2}",), dma=True)
        ms.run_block(nc)


PHASE_FNS["D"] = phaseD


# version 33
# speedup vs baseline: 1.1121x; 1.1121x over previous
import numpy as np
import ml_dtypes
import concourse.bass as bass
import concourse.mybir as mybir
import concourse.bass_utils as _bass_utils
from concourse.bass_utils import run_bass_kernel_spmd

if not getattr(_bass_utils, "_ldw_opt_patched", False):
    _orig_run_command = _bass_utils.run_command

    def _run_command_ldw(cmd, *a, **kw):
        cmd = ["--enable-ldw-opt=true" if c == "--enable-ldw-opt=false" else c for c in cmd]
        return _orig_run_command(cmd, *a, **kw)

    _bass_utils.run_command = _run_command_ldw
    _bass_utils._ldw_opt_patched = True

F32 = mybir.dt.float32
BF16 = mybir.dt.bfloat16
I32 = mybir.dt.int32
U8 = mybir.dt.uint8
AF = mybir.ActivationFunctionType
ALU = mybir.AluOpType
AX = mybir.AxisListType

T = 4096
C = 1024
NT = T // 128
NG = T // 512
EPS32 = float(np.finfo(np.float32).eps)

ENGMAP = {"pe": "tensor", "act": "scalar", "dve": "vector", "pool": "gpsimd", "sp": "sync"}
QUEUES = ["pe", "act", "dve", "pool", "sp"]


NDMASEM = 6
INHERIT_CLOCKS = True


class SemState:
    def __init__(self, nc, stack):
        self.nc = nc
        self.csem = {}
        self.dsem = {}
        self.ccount = {q: 0 for q in QUEUES}
        self.dcount = {}
        self.dtotal = {q: 0 for q in QUEUES}
        for q in QUEUES:
            self.csem[q] = stack.enter_context(nc.semaphore(f"c_{q}"))
        for q in ("sp", "pool", "act", "dve"):
            self.dsem[q] = [stack.enter_context(nc.semaphore(f"d_{q}{i}")) for i in range(NDMASEM)]
            for i in range(NDMASEM):
                self.dcount[(q, i)] = 0
        self.all_handles = list(self.csem.values()) + [h for l in self.dsem.values() for h in l]

    def emit_clear(self, nc):
        nums = sorted(h.num for h in self.all_handles)
        assert nums == list(range(nums[0], nums[0] + len(nums)))
        rng = range(nums[0], nums[-1] + 1)
        with nc.Block() as blk:
            @blk.gpsimd
            def _(eng):
                eng.dma_reset(rng)
                eng.sem_clear(rng)
        for q in QUEUES:
            self.ccount[q] = 0
            self.dtotal[q] = 0
        for k in self.dcount:
            self.dcount[k] = 0


class Op:
    __slots__ = ["q", "emit", "reads", "writes", "dma", "idx", "deps", "inc", "val",
                 "clock", "qidx", "accum_key"]
    def __init__(self, q, emit, reads, writes, dma, accum_key=None):
        self.q = q; self.emit = emit
        self.reads = reads; self.writes = writes
        self.dma = dma; self.accum_key = accum_key
        self.deps = []
        self.inc = False
        self.val = None
        self.clock = None


class MS:
    def __init__(self, sems):
        self.sems = sems
        self.ops = []
        self.last_writer = {}
        self.readers = {}
        self.psum_keys = set()

    def op(self, q, emit, reads=(), writes=(), dma=False, accum_key=None):
        o = Op(q, emit, tuple(reads), tuple(writes), dma, accum_key)
        o.idx = len(self.ops)
        dep = {}
        def add(d):
            if d is not None and d is not o:
                dep[d.idx] = d
        for b in o.reads:
            add(self.last_writer.get(b))
            if b in self.psum_keys:
                for r in self.readers.get(b, ()):
                    if r.q != o.q:
                        add(r)
        for b in o.writes:
            add(self.last_writer.get(b))
            for r in self.readers.get(b, ()):
                add(r)
        deps = []
        for d in dep.values():
            if d.q == o.q and d.dma and o.dma:
                continue
            if (accum_key is not None and d.accum_key == accum_key and d.q == o.q == "pe"):
                continue
            deps.append(d)
        o.deps = deps
        if dma:
            o.inc = True
        for d in deps:
            d.inc = True
        for b in o.writes:
            self.last_writer[b] = o
            self.readers[b] = []
        for b in o.reads:
            self.readers.setdefault(b, []).append(o)
        self.ops.append(o)
        return o

    def finalize_counts(self):
        last_compute = {}
        for o in self.ops:
            if not o.dma:
                last_compute[o.q] = o
        for o in last_compute.values():
            o.inc = True
        for o in self.ops:
            if o.q == "pool" and not o.dma:
                o.inc = True
        for o in self.ops:
            if o.inc:
                if o.dma:
                    slot = self.sems.dtotal[o.q] % NDMASEM
                    self.sems.dtotal[o.q] += 1
                    self.sems.dcount[(o.q, slot)] += 16
                    o.val = ("d", (o.q, slot), self.sems.dcount[(o.q, slot)])
                else:
                    self.sems.ccount[o.q] += 1
                    o.val = ("c", o.q, self.sems.ccount[o.q])

    @property
    def ops_last_per_queue(self):
        last = {}
        for o in self.ops:
            last[o.q] = o
        return last

    def emit_queue(self, q, eng, final_drain=False):
        known = {}
        def sem_handle(val):
            kind, qq, v = val
            if kind == "d":
                return self.sems.dsem[qq[0]][qq[1]], v
            return self.sems.csem[qq], v
        for o in self.ops:
            if o.q != q:
                continue
            for d in o.deps:
                assert d.val is not None
                key = (d.val[0], d.val[1])
                if known.get(key, 0) >= d.val[2]:
                    continue
                h, v = sem_handle(d.val)
                eng.wait_ge(h, v)
                known[key] = d.val[2]
                if INHERIT_CLOCKS and d.clock:
                    for k2, v2 in d.clock.items():
                        if known.get(k2, 0) < v2:
                            known[k2] = v2
            inst = o.emit(eng)
            if o.inc:
                h, v = sem_handle(o.val)
                inst.then_inc(h, 16 if o.dma else 1)
                o.clock = dict(known)
                o.clock[(o.val[0], o.val[1])] = o.val[2]
            else:
                o.clock = dict(known)
        if final_drain:
            for (qq, slot), v in self.sems.dcount.items():
                if v > 0:
                    eng.wait_ge(self.sems.dsem[qq][slot], v)
            for qq in QUEUES:
                if self.sems.ccount[qq] > 0 and qq != q:
                    eng.wait_ge(self.sems.csem[qq], self.sems.ccount[qq])

    def run_block(self, nc, drain_queue="sp"):
        self.finalize_counts()
        with nc.Block() as blk:
            @blk.tensor
            def _(eng):
                self.emit_queue("pe", eng, final_drain=(drain_queue == "pe"))
            @blk.scalar
            def _(eng):
                self.emit_queue("act", eng, final_drain=(drain_queue == "act"))
            @blk.vector
            def _(eng):
                self.emit_queue("dve", eng, final_drain=(drain_queue == "dve"))
            @blk.gpsimd
            def _(eng):
                self.emit_queue("pool", eng, final_drain=(drain_queue == "pool"))
            @blk.sync
            def _(eng):
                self.emit_queue("sp", eng, final_drain=(drain_queue == "sp"))



def _bf16(a):
    return np.ascontiguousarray(a).astype(ml_dtypes.bfloat16)


def retnet_consts():
    H = 16
    gamma = np.log(1.0 - 2.0 ** (-5.0 - np.arange(H, dtype=np.float64)))
    c_full = np.zeros((128, H * 32), dtype=np.float32)
    sprime = np.arange(128, dtype=np.float64)
    for h in range(H):
        for dti in range(32):
            dt = dti - 3
            c_full[:, h * 32 + dti] = np.exp(gamma[h] * (128.0 * dt - sprime)).astype(np.float32)
    q = np.arange(T, dtype=np.float64)
    r = np.exp(gamma)
    Ap = np.zeros((H, T), dtype=np.float32)
    for h in range(H):
        partial = np.cumsum(r[h] ** np.arange(T))
        scale = np.sqrt(partial)
        Ap[h] = (np.exp(gamma[h] * (q - 512.0 * (q // 512))) / scale).astype(np.float32)
    A_rep = np.ascontiguousarray(np.broadcast_to(Ap[:, None, :], (H, 128, T))).astype(np.float32)
    SAINV = (np.sqrt(EPS32) / Ap).astype(np.float32)
    SAINV_rep = np.ascontiguousarray(np.broadcast_to(SAINV[:, None, :], (H, 128, T))).astype(np.float32)
    return c_full, A_rep, SAINV_rep, np.ascontiguousarray(Ap), np.ascontiguousarray(SAINV)


def host_prep(inp):
    d = {}
    d["ident"] = np.eye(128, dtype=np.float32)
    d["ones_bf"] = np.ones((128, 128), dtype=ml_dtypes.bfloat16)
    c_full, A_rep, SAINV_rep, A_rows, SAINV_rows = retnet_consts()
    d["c_full"] = c_full
    d["A_rows"] = A_rows
    d["SAINV_rows"] = SAINV_rows
    sp = np.arange(128)[:, None]
    qp = np.arange(512)[None, :]
    d["tri"] = np.concatenate([(128 * r + sp <= qp).astype(np.float32) for r in range(4)], axis=1)
    d["A_rep"] = A_rep
    d["SAINV_rep"] = SAINV_rep
    
    d["Wqk"] = inp["L0_attn_W_qk"]
    d["Wv"] = inp["L0_attn_W_v"]
    d["Wg"] = inp["L0_attn_W_g"]
    d["Wo"] = inp["L0_attn_W_o"]
    d["bqk_fm"] = np.ascontiguousarray(inp["L0_attn_b_qk"].reshape(16, 128).T)
    d["bg_fm"] = np.ascontiguousarray(inp["L0_attn_b_g"].reshape(16, 128).T)
    d["bv_rep"] = np.ascontiguousarray(np.broadcast_to(inp["L0_attn_b_v"], (128, 2048)))
    d["bo_rep"] = np.ascontiguousarray(np.broadcast_to(inp["L0_attn_b_o"], (128, 1024)))
    d["f0_wgT"] = np.ascontiguousarray(inp["L0_ffn_w_gate"].T)
    d["f0_wuT"] = np.ascontiguousarray(inp["L0_ffn_w_up"].T)
    d["f0_wdT"] = np.ascontiguousarray(inp["L0_ffn_w_down"].T)
    d["qkvT"] = np.ascontiguousarray(inp["L1_attn_qkv_w"].T)
    qb = inp["L1_attn_qkv_b"]
    d["qb_fm"] = np.ascontiguousarray(qb[0:1024].reshape(8, 128).T)
    d["kb_rep"] = np.ascontiguousarray(np.broadcast_to(qb[1024:2048], (128, 1024)))
    d["vb_rep"] = np.ascontiguousarray(np.broadcast_to(qb[2048:3072], (128, 1024)))
    d["kng_rep"] = np.ascontiguousarray(np.broadcast_to(np.tile(inp["L1_attn_kn_g"], 16), (128, 1024)))
    d["knb_rep"] = np.ascontiguousarray(np.broadcast_to(np.tile(inp["L1_attn_kn_b"], 16), (128, 1024)))
    d["vng_rep"] = np.ascontiguousarray(np.broadcast_to(np.tile(inp["L1_attn_vn_g"], 16), (128, 1024)))
    d["vnb_rep"] = np.ascontiguousarray(np.broadcast_to(np.tile(inp["L1_attn_vn_b"], 16), (128, 1024)))
    temp = inp["L1_attn_temperature"].reshape(16)
    d["temp_rep"] = np.ascontiguousarray(np.broadcast_to(np.repeat(temp, 64)[None, :], (64, 1024)))
    d["kvbias"] = np.ascontiguousarray(inp["L1_attn_attn_bias"].transpose(1, 0, 2).reshape(64, 1024))
    d["p1T"] = np.ascontiguousarray(inp["L1_attn_p1_w"].T)
    d["p1b_rep"] = np.ascontiguousarray(np.broadcast_to(inp["L1_attn_p1_b"], (128, 1024)))
    d["p2T"] = np.ascontiguousarray(inp["L1_attn_p2_w"].T)
    d["p2b_rep"] = np.ascontiguousarray(np.broadcast_to(inp["L1_attn_p2_b"], (128, 1024)))
    d["gwT"] = np.ascontiguousarray(inp["L1_ffn_gate_w"].T)
    d["gb_rep"] = np.ascontiguousarray(np.broadcast_to(inp["L1_ffn_gate_bias"], (128, 32)))
    d["wgE"] = np.ascontiguousarray(inp["L1_ffn_w_gate"].transpose(0, 2, 1))
    d["wuE"] = np.ascontiguousarray(inp["L1_ffn_w_up"].transpose(0, 2, 1))
    d["wdE"] = np.ascontiguousarray(inp["L1_ffn_w_down"].transpose(0, 2, 1))
    d["shgT"] = np.ascontiguousarray(inp["L1_ffn_sh_gate"].T)
    d["shuT"] = np.ascontiguousarray(inp["L1_ffn_sh_up"].T)
    d["shdT"] = np.ascontiguousarray(inp["L1_ffn_sh_down"].T)
    return d


from contextlib import ExitStack


class Ctx:
    pass


def declare_dram(nc, cx, phases, first, last):
    E = "ExternalInput"
    O = "ExternalOutput"
    I = "Internal"
    D = {}
    def t(name, shape, dtype, kind):
        D[name] = nc.dram_tensor(name, list(shape), dtype, kind=kind).ap()
    t("ident", (128, 128), F32, E)
    if "A" in phases:
        t("ones_bf", (128, 128), BF16, E)
        t("x0", (T, C), F32, E)
        t("c_full", (128, 512), F32, E)
        t("tri", (128, 2048), F32, E)
        t("A_rows", (16, T), F32, E)
        t("SAINV_rows", (16, T), F32, E)
        for n, s in [("Wqk", (C, 2048)), ("Wv", (C, 2048)), ("Wg", (C, 2048)), ("Wo", (2048, C))]:
            t(n, s, F32, E)
        t("bqk_fm", (128, 16), F32, E)
        t("bg_fm", (128, 16), F32, E)
        t("bv_rep", (128, 2048), F32, E)
        t("bo_rep", (128, 1024), F32, E)
        t("s_q", (16, 64, T), BF16, I)
        t("s_k", (16, 64, T), BF16, I)
        t("s_v", (16, T, 128), BF16, I)
        t("s_gated", (16, 128, T), BF16, I)
        t("resid1", (T, C), F32, O if last == "A" else I)
    if "B" in phases:
        if "A" not in phases:
            t("resid1", (T, C), F32, E)
        for n, s in [("f0_wgT", (C, 3072)), ("f0_wuT", (C, 3072)), ("f0_wdT", (3072, C))]:
            t(n, s, I32, E)
        t("resid2", (T, C), F32, O if last == "B" else I)
        if B_DEBUG:
            t("dbg_gu", (128, 24 * 512), BF16, O)
            t("dbg_x", (128, 1024), F32, O)
            t("dbg_a", (128, 1024), F32, O)
            t("dbg_xT", (128, 8 * 4096), BF16, O)
            t("dbg_w", (128, 8 * 128), BF16, O)
    if "C" in phases:
        if "B" not in phases:
            t("resid2", (T, C), F32, E)
        t("qkvT", (C, 3072), F32, E)
        t("qb_fm", (128, 8), F32, E)
        for n in ["kb_rep", "vb_rep", "kng_rep", "knb_rep", "vng_rep", "vnb_rep",
                  "p1b_rep", "p2b_rep"]:
            t(n, (128, 1024), F32, E)
        t("temp_rep", (64, 1024), F32, E)
        t("kvbias", (64, 1024), F32, E)
        t("p1T", (C, C), F32, E)
        t("p2T", (C, C), F32, E)
        t("s_qc", (8, 128, T), BF16, I)
        t("resid3", (T, C), F32, O if last == "C" else I)
    if "D" in phases:
        if "C" not in phases:
            t("resid3", (T, C), F32, E)
        t("gwT", (C, 32), F32, E)
        t("gb_rep", (128, 32), F32, E)
        t("wgE", (32, C, 512), F32, E)
        t("wuE", (32, C, 512), F32, E)
        t("wdE", (32, 512, C), F32, E)
        t("shgT", (C, C), F32, E)
        t("shuT", (C, C), F32, E)
        t("shdT", (C, C), F32, E)
        t("out", (T, C), F32, O)
    cx.D = D


_UID = [0]


def uid():
    _UID[0] += 1
    return _UID[0]


def alloc_psums(nc, st, spec, ms=None):
    ps = {}
    for name, shape in spec:
        ps[name] = st.enter_context(nc.psum_tensor(f"{name}_{uid()}", list(shape), F32))
    if ms is not None:
        ms.psum_keys.update(ps.keys())
    return ps


def sb(nc, st, name, shape, dtype):
    return st.enter_context(nc.sbuf_tensor(f"{name}_{uid()}", list(shape), dtype))


def build_xT(nc, cx, stack, src_dram, ms, ps):
    load_ident(nc, cx, ms)
    xt = [sb(nc, stack, f"a0xt{i}", [128, C], F32) for i in range(2)]
    xTv = cx.xT[:, :].rearrange("p (a b) -> p a b", b=4096)
    for tt in range(NT):
        b = tt % 2
        ms.op("sp", lambda e, b=b, tt=tt: e.dma_start(out=xt[b][:, :], in_=src_dram[tt * 128:(tt + 1) * 128, :]),
              reads=(), writes=(f"xt{b}",), dma=True)
        for half in range(2):
            ph = f"p{half}"
            pt = ps[ph]
            for k in range(4):
                cc = half * 4 + k
                ms.op("pe", lambda e, pt=pt, k=k, b=b, cc=cc: e.matmul(
                    pt[:, k * 128:(k + 1) * 128], xt[b][:, cc * 128:(cc + 1) * 128], cx.identf[:, :],
                    start=(k == 0), stop=(k == 3), is_transpose=True),
                    reads=(f"xt{b}", "ident"), writes=(ph,), accum_key=("tr", tt, half))
            q = "dve" if half == 0 else "act"
            dst = xTv[:, half * 4:half * 4 + 4, tt * 128:(tt + 1) * 128]
            if q == "dve":
                ms.op(q, lambda e, pt=pt, dst=dst: e.tensor_copy(
                    dst, pt[:, :].rearrange("p (a b) -> p a b", b=128)),
                    reads=(ph,), writes=("xT",))
            else:
                ms.op(q, lambda e, pt=pt, dst=dst: e.copy(
                    dst, pt[:, :].rearrange("p (a b) -> p a b", b=128)),
                    reads=(ph,), writes=("xT",))


def phaseA(nc, cx, prog_stack):
    D = cx.D
    sems = cx.sems
    cx.a2_pend = None
    ast_ = prog_stack.enter_context(ExitStack())
    cx.xT = sb(nc, ast_, "a_xT", [128, 8 * 4096], BF16)
    with ExitStack() as st:
        ms = MS(sems)
        ps = alloc_psums(nc, st, [("p0", (128, 512)), ("p1", (128, 512))])
        build_xT(nc, cx, st, D["x0"], ms, ps)
        ms.run_block(nc)
    with ExitStack() as st:
        ms = MS(sems)
        ps = alloc_psums(nc, st, [(f"p{i}", (128, 512)) for i in range(4)], ms)
        wqk = st.enter_context(nc.sbuf_tensor("wqk", [128, 8 * 2048], BF16))
        wv = st.enter_context(nc.sbuf_tensor("wv", [128, 8 * 2048], BF16))
        bqk = st.enter_context(nc.sbuf_tensor("bqk", [128, 16], F32))
        bv = st.enter_context(nc.sbuf_tensor("bv", [128, 2048], F32))
        qkst = [st.enter_context(nc.sbuf_tensor(f"qkst{i}", [128, 512], BF16)) for i in range(4)]
        vst = [st.enter_context(nc.sbuf_tensor(f"vst{i}", [128, 512], BF16)) for i in range(4)]
        ms.op("sp", lambda e: e.dma_start(out=bqk[:, :], in_=D["bqk_fm"][:, :]), writes=("bqk",), dma=True)
        ms.op("sp", lambda e: e.dma_start(out=bv[:, :], in_=D["bv_rep"][:, :]), writes=("bv",), dma=True)
        for cc in range(8):
            ms.op("pool", lambda e, cc=cc: e.dma_start(out=wqk[:, cc * 2048:(cc + 1) * 2048],
                                                       in_=D["Wqk"][cc * 128:(cc + 1) * 128, :]),
                  writes=(f"wqk{cc}",), dma=True)
            ms.op("pool", lambda e, cc=cc: e.dma_start(out=wv[:, cc * 2048:(cc + 1) * 2048],
                                                       in_=D["Wv"][cc * 128:(cc + 1) * 128, :]),
                  writes=(f"wv{cc}",), dma=True)
        for jc in range(16):
            for tgh in range(2):
                for cc in range(8):
                    for tg4 in range(4):
                        tg = tgh * 4 + tg4
                        pt = ps[f"p{tg4}"]
                        ms.op("pe", lambda e, pt=pt, cc=cc, jc=jc, tg=tg: e.matmul(
                            pt[:, :], wqk[:, cc * 2048 + jc * 128: cc * 2048 + (jc + 1) * 128],
                            cx.xT[:, cc * 4096 + tg * 512: cc * 4096 + (tg + 1) * 512],
                            start=(cc == 0), stop=(cc == 7)),
                            reads=(f"wqk{cc}", "xT"), writes=(f"p{tg4}",), accum_key=("qk", jc, tg))
                for tg4 in range(4):
                    tg = tgh * 4 + tg4
                    ph = f"p{tg4}"
                    pt = ps[ph]
                    qsb = qkst[tg4]
                    sbk = f"qkst{tg4}"
                    ms.op("act", lambda e, qsb=qsb, pt=pt, jc=jc: e.activation(
                        qsb[:, :], pt[:, :], AF.Identity, bias=bqk[:, jc:jc + 1]),
                        reads=(ph, "bqk"), writes=(sbk,))
                    tgt = D["s_q"] if jc < 8 else D["s_k"]
                    hh = (jc % 8) * 2
                    ms.op("sp", lambda e, tgt=tgt, hh=hh, tg=tg, qsb=qsb: e.dma_start(
                        out=tgt[hh, :, tg * 512:(tg + 1) * 512], in_=qsb[0:64, :]),
                        reads=(sbk,), writes=(), dma=True)
                    ms.op("sp", lambda e, tgt=tgt, hh=hh, tg=tg, qsb=qsb: e.dma_start(
                        out=tgt[hh + 1, :, tg * 512:(tg + 1) * 512], in_=qsb[64:128, :]),
                        reads=(sbk,), writes=(), dma=True)
        for tt in range(NT):
            for cc in range(8):
                for jg in range(4):
                    pt = ps[f"p{jg}"]
                    ms.op("pe", lambda e, pt=pt, cc=cc, tt=tt, jg=jg: e.matmul(
                        pt[:, :], cx.xT[:, cc * 4096 + tt * 128: cc * 4096 + (tt + 1) * 128],
                        wv[:, cc * 2048 + jg * 512: cc * 2048 + (jg + 1) * 512],
                        start=(cc == 0), stop=(cc == 7)),
                        reads=(f"wv{cc}", "xT"), writes=(f"p{jg}",), accum_key=("v", tt, jg))
            for jg in range(4):
                ph = f"p{jg}"
                pt = ps[ph]
                vsb = vst[jg]
                sbk = f"vst{jg}"
                ms.op("dve", lambda e, vsb=vsb, pt=pt, jg=jg: e.scalar_tensor_tensor(
                    vsb[:, :], pt[:, :], 1.0, bv[:, jg * 512:(jg + 1) * 512], ALU.bypass, ALU.add),
                    reads=(ph, "bv"), writes=(sbk,))
                ms.op("sp", lambda e, vsb=vsb, tt=tt, jg=jg: e.dma_start(
                    out=D["s_v"][jg * 4:(jg + 1) * 4, tt * 128:(tt + 1) * 128, :].rearrange("h p v -> p h v"),
                    in_=vsb[:, :].rearrange("p (h v) -> p h v", v=128)),
                    reads=(sbk,), writes=(), dma=True)
        ms.run_block(nc)
    with ExitStack() as st:
        ms = MS(sems)
        ps = alloc_psums(nc, st, [("p0", (128, 512)), ("p1", (128, 512)), ("p2", (128, 512)),
                                  ("p3", (128, 512)),
                                  ("psret0", (128, 512)), ("psret1", (128, 512)),
                                  ("psden", (1, 512)), ("psrms", (128, 512))], ms)
        wg = st.enter_context(nc.sbuf_tensor("wg", [128, 8 * 2048], BF16))
        bg = st.enter_context(nc.sbuf_tensor("bg", [128, 16], F32))
        cfu = st.enter_context(nc.sbuf_tensor("cfu", [128, 512], F32))
        trisb = st.enter_context(nc.sbuf_tensor("trisb", [128, 2048], F32))
        onesb = st.enter_context(nc.sbuf_tensor("onesb", [128, 1], BF16))
        onesf = st.enter_context(nc.sbuf_tensor("onesf", [1, 128], F32))
        qTh = [st.enter_context(nc.sbuf_tensor(f"qTh{i}", [64, T], BF16)) for i in range(2)]
        kTh = [st.enter_context(nc.sbuf_tensor(f"kTh{i}", [64, T], BF16)) for i in range(2)]
        vh = [st.enter_context(nc.sbuf_tensor(f"vh{i}", [128, T], BF16)) for i in range(2)]
        gTh = st.enter_context(nc.sbuf_tensor("gTh", [128, T], BF16))
        msst = [st.enter_context(nc.sbuf_tensor(f"msst{i}", [128, 512], BF16)) for i in range(4)]
        absst = [st.enter_context(nc.sbuf_tensor(f"absst{i}", [128, 512], BF16)) for i in range(4)]
        sqst = st.enter_context(nc.sbuf_tensor("sqst", [128, 512], BF16))
        arow = st.enter_context(nc.sbuf_tensor("arow", [1, 512], F32))
        sarow = st.enter_context(nc.sbuf_tensor("sarow", [1, 512], F32))
        denrow = st.enter_context(nc.sbuf_tensor("denrow", [1, 512], F32))
        rmsrow = st.enter_context(nc.sbuf_tensor("rmsrow", [1, 512], F32))
        dcrow = st.enter_context(nc.sbuf_tensor("dcrow", [1, 512], F32))
        w2row = st.enter_context(nc.sbuf_tensor("w2row", [1, 512], F32))
        frow = st.enter_context(nc.sbuf_tensor("frow", [1, 512], F32))
        fsb = st.enter_context(nc.sbuf_tensor("fsb", [128, 512], F32))
        y1sb = st.enter_context(nc.sbuf_tensor("y1sb", [128, 512], F32))
        gst = st.enter_context(nc.sbuf_tensor("gst", [128, 512], BF16))
        ms.op("sp", lambda e: e.dma_start(out=bg[:, :], in_=D["bg_fm"][:, :]), writes=("bg",), dma=True)
        ms.op("sp", lambda e: e.dma_start(out=cfu[:, :], in_=D["c_full"][:, :]), writes=("cfu",), dma=True)
        ms.op("sp", lambda e: e.dma_start(out=trisb[:, :], in_=D["tri"][:, :]), writes=("tri",), dma=True)
        ms.op("pool", lambda e: e.memset(onesb[:, :], 1.0), writes=("onesb",))
        ms.op("pool", lambda e: e.memset(onesf[:, :], 1.0), writes=("onesf",))
        for cc in range(8):
            ms.op("pool", lambda e, cc=cc: e.dma_start(out=wg[:, cc * 2048:(cc + 1) * 2048],
                                                       in_=D["Wg"][cc * 128:(cc + 1) * 128, :]),
                  writes=(f"wg{cc}",), dma=True)
        for h in range(16):
            hb = h % 2
            if cx.a2_pend is not None:
                cx.a2_pend()
                cx.a2_pend = None
            ms.op("sp", lambda e, h=h, hb=hb: e.dma_start(out=qTh[hb][:, :], in_=D["s_q"][h, :, :]),
                  reads=(), writes=(f"qTh{hb}",), dma=True)
            ms.op("sp", lambda e, h=h, hb=hb: e.dma_start(out=kTh[hb][:, :], in_=D["s_k"][h, :, :]),
                  reads=(), writes=(f"kTh{hb}",), dma=True)
            for part in range(4):
                ms.op("sp", lambda e, h=h, hb=hb, part=part: e.dma_start(
                    out=vh[hb][:, :].rearrange("p (a v) -> p a v", v=128)[:, part * 8:(part + 1) * 8, :],
                    in_=D["s_v"][h, part * 1024:(part + 1) * 1024, :].rearrange("(a p) v -> p a v", p=128)),
                    reads=(), writes=(f"vh{hb}",), dma=True)
            for tgp in range(4):
                for cc in range(8):
                    for t2 in range(2):
                        tg = tgp * 2 + t2
                        pt = ps[f"p{t2}"]
                        ms.op("pe", lambda e, pt=pt, cc=cc, h=h, tg=tg: e.matmul(
                            pt[:, :], wg[:, cc * 2048 + h * 128: cc * 2048 + (h + 1) * 128],
                            cx.xT[:, cc * 4096 + tg * 512: cc * 4096 + (tg + 1) * 512],
                            start=(cc == 0), stop=(cc == 7)),
                            reads=(f"wg{cc}", "xT"), writes=(f"p{t2}",), accum_key=("g", h, tg))
                for t2 in range(2):
                    tg = tgp * 2 + t2
                    ph = f"p{t2}"
                    pt = ps[ph]
                    ms.op("act", lambda e, pt=pt, h=h, tg=tg: e.activation(
                        gTh[:, tg * 512:(tg + 1) * 512], pt[:, :], AF.Identity, bias=bg[:, h:h + 1]),
                        reads=(ph, "bg"), writes=("gTh",))
            for qg in range(NG):
                rk = f"psret{qg % 2}"
                ms.op("sp", lambda e, h=h, qg=qg: e.dma_start(
                    out=arow[:, :], in_=D["A_rows"][h:h + 1, qg * 512:(qg + 1) * 512]),
                    writes=("arow",), dma=True)
                ms.op("sp", lambda e, h=h, qg=qg: e.dma_start(
                    out=sarow[:, :], in_=D["SAINV_rows"][h:h + 1, qg * 512:(qg + 1) * 512]),
                    writes=("sarow",), dma=True)
                nkt = 4 * qg + 4

                def sc_emit(kt, h=h, qg=qg, hb=hb):
                    pn = f"p{kt % 4}"
                    pt = ps[pn]
                    ms.op("pe", lambda e, pt=pt, kt=kt, qg=qg, hb=hb: e.matmul(
                        pt[:, :], kTh[hb][:, kt * 128:(kt + 1) * 128],
                        qTh[hb][:, qg * 512:(qg + 1) * 512], start=True, stop=True),
                        reads=(f"kTh{hb}", f"qTh{hb}"), writes=(pn,))

                def ret_emit(kt, rk=rk, h=h, qg=qg, hb=hb, nkt=nkt):
                    kb = kt % 4
                    ms.op("pe", lambda e, rk=rk, kb=kb, hb=hb, kt=kt, nkt=nkt: e.matmul(
                        ps[rk][:, :], vh[hb][:, kt * 128:(kt + 1) * 128], msst[kb][:, :],
                        start=(kt == 0), stop=(kt == nkt - 1)),
                        reads=(f"vh{hb}", f"msst{kb}"), writes=(rk,), accum_key=("ret", h, qg))

                def den_emit(kt, h=h, qg=qg, nkt=nkt):
                    kb = kt % 4
                    ms.op("pe", lambda e, kb=kb, kt=kt, nkt=nkt: e.matmul(
                        ps["psden"][:, :], onesb[:, 0:1], absst[kb][:, :],
                        start=(kt == 0), stop=(kt == nkt - 1)),
                        reads=("onesb", f"absst{kb}"), writes=("psden",), accum_key=("den", h, qg))

                for kt in range(min(4, nkt)):
                    sc_emit(kt)
                for kt in range(nkt):
                    kb = kt % 4
                    pn = f"p{kb}"
                    pt = ps[pn]
                    didx = 4 * qg - kt + 3
                    r = kt - 4 * qg
                    if r >= 0:
                        ms.op("dve", lambda e, kb=kb, pt=pt, h=h, didx=didx, r=r: e.scalar_tensor_tensor(
                            msst[kb][:, :], pt[:, :], cfu[:, h * 32 + didx: h * 32 + didx + 1],
                            trisb[:, r * 512:(r + 1) * 512], ALU.mult, ALU.mult),
                            reads=(pn, "cfu", "tri"), writes=(f"msst{kb}",))
                    else:
                        ms.op("dve", lambda e, kb=kb, pt=pt, h=h, didx=didx: e.tensor_scalar(
                            msst[kb][:, :], pt[:, :], cfu[:, h * 32 + didx: h * 32 + didx + 1],
                            None, ALU.mult),
                            reads=(pn, "cfu"), writes=(f"msst{kb}",))
                    ms.op("act", lambda e, kb=kb: e.activation(absst[kb][:, :], msst[kb][:, :], AF.Abs),
                          reads=(f"msst{kb}",), writes=(f"absst{kb}",))
                    if kt + 4 < nkt:
                        sc_emit(kt + 4)
                    if kt >= 1:
                        ret_emit(kt - 1)
                    if kt >= 2:
                        den_emit(kt - 2)
                    if kt == 2 and cx.a2_pend is not None:
                        cx.a2_pend()
                        cx.a2_pend = None
                ret_emit(nkt - 1)
                den_emit(nkt - 2)
                den_emit(nkt - 1)
                ms.op("act", lambda e, rk=rk: e.activation(sqst[:, :], ps[rk][:, :], AF.Square),
                      reads=(rk,), writes=("sqst",))
                ms.op("pe", lambda e: e.matmul(ps["psrms"][0:1, :], onesb[:, 0:1], sqst[:, :],
                                               start=True, stop=True),
                      reads=("onesb", "sqst"), writes=("psrms",))
                ms.op("dve", lambda e: e.tensor_copy(denrow[:, :], ps["psden"][:, :]),
                      reads=("psden",), writes=("denrow",))
                ms.op("dve", lambda e: e.tensor_scalar(rmsrow[:, :], ps["psrms"][0:1, :],
                                                       1.0 / 128.0, None, ALU.mult),
                      reads=("psrms",), writes=("rmsrow",))
                ms.op("dve", lambda e: e.tensor_tensor(dcrow[:, :], denrow[:, :], arow[:, :], ALU.mult),
                      reads=("denrow", "arow"), writes=("dcrow",))
                ms.op("dve", lambda e: e.tensor_scalar(dcrow[:, :], dcrow[:, :], 1.0, 5e4, ALU.max, ALU.min),
                      reads=("dcrow",), writes=("dcrow",))
                ms.op("dve", lambda e: e.tensor_tensor(dcrow[:, :], dcrow[:, :], sarow[:, :], ALU.mult),
                      reads=("dcrow", "sarow"), writes=("dcrow",))
                ms.op("act", lambda e: e.activation(w2row[:, :], dcrow[:, :], AF.Square),
                      reads=("dcrow",), writes=("w2row",))
                ms.op("dve", lambda e: e.tensor_tensor(frow[:, :], w2row[:, :], rmsrow[:, :], ALU.add),
                      reads=("w2row", "rmsrow"), writes=("frow",))
                ms.op("act", lambda e: e.activation(frow[:, :], frow[:, :], AF.Ln),
                      reads=("frow",), writes=("frow",))
                ms.op("act", lambda e: e.activation(frow[:, :], frow[:, :], AF.Exp, scale=-0.5),
                      reads=("frow",), writes=("frow",))
                def a2_tail(rk=rk, qg=qg, h=h):
                    ms.op("pe", lambda e: e.matmul(ps["psrms"][:, :], onesf[0:1, :], frow[:, :],
                                                   start=True, stop=True),
                          reads=("onesf", "frow"), writes=("psrms",))
                    ms.op("dve", lambda e: e.tensor_copy(fsb[:, :], ps["psrms"][:, :]),
                          reads=("psrms",), writes=("fsb",))
                    ms.op("dve", lambda e, rk=rk: e.tensor_tensor(y1sb[:, :], ps[rk][:, :], fsb[:, :], ALU.mult),
                          reads=(rk, "fsb"), writes=("y1sb",))
                    ms.op("dve", lambda e, qg=qg: e.tensor_tensor(gst[:, :], y1sb[:, :],
                                                                  gTh[:, qg * 512:(qg + 1) * 512], ALU.mult),
                          reads=("y1sb", "gTh"), writes=("gst",))
                    ms.op("sp", lambda e, h=h, qg=qg: e.dma_start(
                        out=D["s_gated"][h, :, qg * 512:(qg + 1) * 512], in_=gst[:, :]),
                        reads=("gst",), writes=(), dma=True)
                cx.a2_pend = a2_tail
        if cx.a2_pend is not None:
            cx.a2_pend()
            cx.a2_pend = None
        ms.run_block(nc)
    with ExitStack() as st:
        ms = MS(sems)
        emit_outproj_ln(nc, cx, st, ms,
                        n_jc=16, wo_dram=D["Wo"], wo_cols=1024, gsrc=D["s_gated"],
                        bias_rep=D["bo_rep"], res_dram=D["x0"], out_dram=D["resid1"],
                        scale=1.0)
        ms.run_block(nc)
    ast_.close()


def emit_outproj_ln(nc, cx, st, ms, n_jc, wo_dram, wo_cols, gsrc, bias_rep, res_dram,
                    out_dram, scale):
    D = cx.D
    ps = alloc_psums(nc, st, [("po0", (128, 512)), ("po1", (128, 512))], ms)
    wo = st.enter_context(nc.sbuf_tensor("wo_a3", [128, n_jc * wo_cols], BF16))
    brep = st.enter_context(nc.sbuf_tensor("brep_a3", [128, wo_cols], F32))
    gin = [st.enter_context(nc.sbuf_tensor(f"gin{i}", [128, n_jc * 128], BF16)) for i in range(2)]
    xres = [st.enter_context(nc.sbuf_tensor(f"xres{i}", [128, wo_cols], F32)) for i in range(2)]
    xb = [st.enter_context(nc.sbuf_tensor(f"xb{i}", [128, wo_cols], F32)) for i in range(2)]
    asb = [st.enter_context(nc.sbuf_tensor(f"asb{i}", [128, wo_cols], F32)) for i in range(2)]
    sums = st.enter_context(nc.sbuf_tensor("sums_a3", [128, 8], F32))
    xcsb = [st.enter_context(nc.sbuf_tensor(f"xcsb{i}", [128, wo_cols], F32)) for i in range(2)]
    sqtr = st.enter_context(nc.sbuf_tensor("sqtr_a3", [128, wo_cols], F32))
    x1st = [st.enter_context(nc.sbuf_tensor(f"x1st{i}", [128, wo_cols], F32)) for i in range(2)]
    ms.op("sp", lambda e: e.dma_start(out=brep[:, :], in_=bias_rep[:, :]), writes=("brep",), dma=True)
    for jc in range(n_jc):
        ms.op("pool", lambda e, jc=jc: e.dma_start(out=wo[:, jc * wo_cols:(jc + 1) * wo_cols],
                                                   in_=wo_dram[jc * 128:(jc + 1) * 128, :]),
              writes=(f"wo{jc}",), dma=True)
    nh = wo_cols // 512
    for tt in range(NT):
        b = tt % 2
        for part in range(4):
            npart = n_jc // 4
            ms.op("sp", lambda e, b=b, tt=tt, part=part, npart=npart: e.dma_start(
                out=gin[b][:, :].rearrange("p (a t) -> p a t", t=128)[:, part * npart:(part + 1) * npart, :],
                in_=gsrc[part * npart:(part + 1) * npart, :, tt * 128:(tt + 1) * 128].rearrange("a p t -> p a t")),
                writes=(f"gin{b}",), dma=True)
        ms.op("sp", lambda e, b=b, tt=tt: e.dma_start(out=xres[b][:, :],
                                                      in_=res_dram[tt * 128:(tt + 1) * 128, :]),
              writes=(f"xres{b}",), dma=True)
        ms.op("pool", lambda e, b=b: e.tensor_tensor(xb[b][:, :], xres[b][:, :], brep[:, :], ALU.add),
              reads=(f"xres{b}", "brep"), writes=(f"xb{b}",))
        for jc in range(n_jc):
            for half in range(nh):
                pt = ps[f"po{half}"]
                ms.op("pe", lambda e, pt=pt, jc=jc, b=b, half=half, n_jc=n_jc: e.matmul(
                    pt[:, :], gin[b][:, jc * 128:(jc + 1) * 128],
                    wo[:, jc * wo_cols + half * 512: jc * wo_cols + half * 512 + 512],
                    start=(jc == 0), stop=(jc == n_jc - 1)),
                    reads=(f"gin{b}", f"wo{jc}"), writes=(f"po{half}",), accum_key=("o", tt, half))
        for half in range(nh):
            ph = f"po{half}"
            pt = ps[ph]
            ms.op("dve", lambda e, pt=pt, b=b, half=half: e.scalar_tensor_tensor(
                asb[b][:, half * 512:(half + 1) * 512], pt[:, :], scale,
                xb[b][:, half * 512:(half + 1) * 512], ALU.mult, ALU.add,
                accum_out=sums[:, b * 4 + half: b * 4 + half + 1]),
                reads=(ph, f"xb{b}"), writes=(f"asb{b}", f"sums{b}"))
        ms.op("dve", lambda e, b=b, nh=nh: e.tensor_reduce(
            sums[:, b * 4 + 3: b * 4 + 4], sums[:, b * 4: b * 4 + nh], axis=AX.X, op=ALU.add)
            if nh > 2 else e.tensor_tensor(sums[:, b * 4 + 3: b * 4 + 4], sums[:, b * 4: b * 4 + 1],
                                           sums[:, b * 4 + 1: b * 4 + 2], ALU.add),
            reads=(f"sums{b}",), writes=(f"msum{b}",))
        ms.op("dve", lambda e, b=b: e.tensor_scalar(sums[:, b * 4 + 3: b * 4 + 4],
                                                    sums[:, b * 4 + 3: b * 4 + 4],
                                                    1.0 / wo_cols, None, ALU.mult),
              reads=(f"msum{b}",), writes=(f"msum{b}",))
        ms.op("dve", lambda e, b=b: e.tensor_scalar(xcsb[b][:, :], asb[b][:, :],
                                                    sums[:, b * 4 + 3: b * 4 + 4], None, ALU.subtract),
              reads=(f"asb{b}", f"msum{b}"), writes=(f"xcsb{b}",))
        ms.op("act", lambda e, b=b: e.activation(sqtr[:, :], xcsb[b][:, :], AF.Square,
                                                 accum_out=sums[:, b * 4 + 2: b * 4 + 3]),
              reads=(f"xcsb{b}",), writes=("sqtr", f"ssq{b}"))
        ms.op("dve", lambda e, b=b: e.tensor_scalar(sums[:, b * 4 + 2: b * 4 + 3],
                                                    sums[:, b * 4 + 2: b * 4 + 3],
                                                    1.0 / wo_cols, 1e-5, ALU.mult, ALU.add),
              reads=(f"ssq{b}",), writes=(f"ssq{b}",))
        ms.op("act", lambda e, b=b: e.activation(sums[:, b * 4 + 2: b * 4 + 3],
                                                 sums[:, b * 4 + 2: b * 4 + 3], AF.Ln),
              reads=(f"ssq{b}",), writes=(f"ssq{b}",))
        ms.op("act", lambda e, b=b: e.activation(sums[:, b * 4 + 2: b * 4 + 3],
                                                 sums[:, b * 4 + 2: b * 4 + 3], AF.Exp, scale=-0.5),
              reads=(f"ssq{b}",), writes=(f"ssq{b}",))
        ms.op("dve", lambda e, b=b: e.tensor_scalar(x1st[b][:, :], xcsb[b][:, :],
                                                    sums[:, b * 4 + 2: b * 4 + 3], None, ALU.mult),
              reads=(f"xcsb{b}", f"ssq{b}"), writes=(f"x1st{b}",))
        ms.op("sp", lambda e, b=b, tt=tt: e.dma_start(out=out_dram[tt * 128:(tt + 1) * 128, :],
                                                      in_=x1st[b][:, :]),
              reads=(f"x1st{b}",), writes=(), dma=True)



PHASE_FNS = {}


def build_program(phases):
    nc = bass.Bass("TRN2", target_bir_lowering=False)
    cx = Ctx()
    cx.ident_loaded = False
    st = ExitStack()
    with st:
        cx.sems = SemState(nc, st)
        declare_dram(nc, cx, phases, phases[0], phases[-1])
        cx.sems.emit_clear(nc)
        cx.identf = st.enter_context(nc.sbuf_tensor("identf", [128, 128], F32))
        for i, p in enumerate(phases):
            if i > 0:
                cx.sems.emit_clear(nc)
            PHASE_FNS[p](nc, cx, st)
    nc.finalize()
    return nc, cx


def load_ident(nc, cx, ms):
    if not cx.ident_loaded:
        D = cx.D
        ms.op("sp", lambda e: e.dma_start(out=cx.identf[:, :], in_=D["ident"][:, :]),
              writes=("ident",), dma=True)
        cx.ident_loaded = True


IN_KEYS = {
    "A": ["ident", "ones_bf", "c_full", "tri", "A_rows", "SAINV_rows", "Wqk", "Wv", "Wg", "Wo",
          "bqk_fm", "bg_fm", "bv_rep", "bo_rep"],
    "B": ["ident", "f0_wgT", "f0_wuT", "f0_wdT"],
    "C": ["ident", "qkvT", "qb_fm", "kb_rep", "vb_rep", "kng_rep", "knb_rep",
          "vng_rep", "vnb_rep", "temp_rep", "kvbias", "p1T", "p1b_rep", "p2T", "p2b_rep"],
    "D": ["ident", "gwT", "gb_rep", "wgE", "wuE", "wdE", "shgT", "shuT", "shdT"],
}
RESID_IN = {"A": "x0", "B": "resid1", "C": "resid2", "D": "resid3"}
RESID_OUT = {"A": "resid1", "B": "resid2", "C": "resid3", "D": "out"}


def run_phases(phases, prep, resid_per_core, n_cores=8):
    nc, cx = build_program(phases)
    keys = set()
    for p in phases:
        keys.update(IN_KEYS[p])
    base = {k: prep[k] for k in sorted(keys) if k in prep}
    in_maps = []
    for b in range(n_cores):
        m = dict(base)
        m[RESID_IN[phases[0]]] = resid_per_core[b]
        in_maps.append(m)
    res = run_bass_kernel_spmd(nc, in_maps, core_ids=list(range(n_cores)))
    okey = RESID_OUT[phases[-1]]
    return [res.results[b][okey] for b in range(n_cores)]


def kernel(**inputs):
    prep = host_prep(inputs)
    x = np.ascontiguousarray(inputs["L0_attn_x"].astype(np.float32, copy=False))
    n_cores = 8
    resid = [np.ascontiguousarray(x[b]) for b in range(n_cores)]
    for group in PHASE_GROUPS:
        resid = run_phases(group, prep, resid, n_cores)
    return np.stack([np.asarray(r, dtype=np.float32) for r in resid], axis=0)


PHASE_GROUPS = [["A", "B", "C", "D"]]
PHASE_FNS["A"] = phaseA


class LNEmit:
    def __init__(self, nc, st, tag):
        self.asb = [sb(nc, st, f"{tag}asb{i}", [128, 1024], F32) for i in range(2)]
        self.sums = sb(nc, st, f"{tag}sums", [128, 8], F32)
        self.sqtr = sb(nc, st, f"{tag}sqtr", [128, 1024], F32)
        self.tag = tag

    def evac_half(self, ms, b, half, src, phkey, xb, xbkey, scale):
        t = self.tag
        ms.op("dve", lambda e, src=src, b=b, half=half: e.scalar_tensor_tensor(
            self.asb[b][:, half * 512:(half + 1) * 512], src, scale,
            xb[:, half * 512:(half + 1) * 512], ALU.mult, ALU.add,
            accum_out=self.sums[:, b * 4 + half: b * 4 + half + 1]),
            reads=(phkey, xbkey), writes=(f"{t}asb{b}", f"{t}sums{b}"))

    def ln(self, ms, b, out_dram, tt):
        t = self.tag
        ms.op("dve", lambda e, b=b: e.tensor_tensor(
            self.sums[:, b * 4 + 3: b * 4 + 4], self.sums[:, b * 4: b * 4 + 1],
            self.sums[:, b * 4 + 1: b * 4 + 2], ALU.add),
            reads=(f"{t}sums{b}",), writes=(f"{t}msum{b}",))
        ms.op("dve", lambda e, b=b: e.tensor_scalar(
            self.sums[:, b * 4 + 3: b * 4 + 4], self.sums[:, b * 4 + 3: b * 4 + 4],
            1.0 / 1024.0, None, ALU.mult),
            reads=(f"{t}msum{b}",), writes=(f"{t}msum{b}",))
        ms.op("dve", lambda e, b=b: e.tensor_scalar(
            self.asb[b][:, :], self.asb[b][:, :], self.sums[:, b * 4 + 3: b * 4 + 4],
            None, ALU.subtract),
            reads=(f"{t}asb{b}", f"{t}msum{b}"), writes=(f"{t}asb{b}",))
        ms.op("act", lambda e, b=b: e.activation(self.sqtr[:, :], self.asb[b][:, :], AF.Square,
                                                 accum_out=self.sums[:, b * 4 + 2: b * 4 + 3]),
              reads=(f"{t}asb{b}",), writes=(f"{t}sqtr", f"{t}ssq{b}"))
        ms.op("dve", lambda e, b=b: e.tensor_scalar(
            self.sums[:, b * 4 + 2: b * 4 + 3], self.sums[:, b * 4 + 2: b * 4 + 3],
            1.0 / 1024.0, 1e-5, ALU.mult, ALU.add),
            reads=(f"{t}ssq{b}",), writes=(f"{t}ssq{b}",))
        ms.op("act", lambda e, b=b: e.activation(self.sums[:, b * 4 + 2: b * 4 + 3],
                                                 self.sums[:, b * 4 + 2: b * 4 + 3], AF.Ln),
              reads=(f"{t}ssq{b}",), writes=(f"{t}ssq{b}",))
        ms.op("act", lambda e, b=b: e.activation(self.sums[:, b * 4 + 2: b * 4 + 3],
                                                 self.sums[:, b * 4 + 2: b * 4 + 3], AF.Exp, scale=-0.5),
              reads=(f"{t}ssq{b}",), writes=(f"{t}ssq{b}",))
        ms.op("dve", lambda e, b=b: e.tensor_scalar(
            self.asb[b][:, :], self.asb[b][:, :], self.sums[:, b * 4 + 2: b * 4 + 3],
            None, ALU.mult),
            reads=(f"{t}asb{b}", f"{t}ssq{b}"), writes=(f"{t}asb{b}",))
        ms.op("act", lambda e, b=b, tt=tt: e.dma_start(out=out_dram[tt * 128:(tt + 1) * 128, :],
                                                       in_=self.asb[b][:, :]),
              reads=(f"{t}asb{b}",), writes=(), dma=True)


B_SKIP_LN = False
B_NG = NG
B_DEBUG = False


def phaseB(nc, cx, prog_stack):
    D = cx.D
    sems = cx.sems
    bst_ = prog_stack.enter_context(ExitStack())
    cx.xT = sb(nc, bst_, "b_xT", [128, 8 * 4096], BF16)
    with ExitStack() as st:
        ms = MS(sems)
        ps = alloc_psums(nc, st, [("p0", (128, 512)), ("p1", (128, 512))])
        build_xT(nc, cx, st, D["resid1"], ms, ps)
        ms.run_block(nc)
    with ExitStack() as st:
        ms = MS(sems)
        ps = alloc_psums(nc, st, [(f"pg{i}", (128, 512)) for i in range(4)] +
                                 [(f"pd{i}", (128, 512)) for i in range(4)], ms)
        wd = sb(nc, st, "f0wd", [128, 24 * 1024], BF16)
        wgc = [sb(nc, st, f"f0wg{i}", [128, 8 * 128], BF16) for i in range(2)]
        wuc = [sb(nc, st, f"f0wu{i}", [128, 8 * 128], BF16) for i in range(2)]
        gu = sb(nc, st, "f0gu", [128, 24 * 512], BF16)
        s1 = [sb(nc, st, f"f0s1{i}", [128, 512], F32) for i in range(2)]
        xres = [sb(nc, st, f"f0x{i}", [128, 1024], F32) for i in range(2)]
        lne = LNEmit(nc, st, "bln")
        for hc in range(24):
            ms.op("pool", lambda e, hc=hc: e.dma_start(out=wd[:, hc * 1024:(hc + 1) * 1024],
                                                       in_=D["f0_wdT"][hc * 128:(hc + 1) * 128, :]),
                  writes=(f"wd{hc}",), dma=True)
        for tg in range(B_NG):
            for hc in range(24):
                wb = hc % 2
                ms.op("pool", lambda e, hc=hc, wb=wb: e.dma_start(
                    out=wgc[wb][:, :].rearrange("p (a b) -> p a b", b=128),
                    in_=D["f0_wgT"][:, hc * 128:(hc + 1) * 128].rearrange("(a p) h -> p a h", p=128)),
                    writes=(f"wgc{wb}",), dma=True)
                ms.op("pool", lambda e, hc=hc, wb=wb: e.dma_start(
                    out=wuc[wb][:, :].rearrange("p (a b) -> p a b", b=128),
                    in_=D["f0_wuT"][:, hc * 128:(hc + 1) * 128].rearrange("(a p) h -> p a h", p=128)),
                    writes=(f"wuc{wb}",), dma=True)
                pgk = f"pg{hc % 2}"
                puk = f"pg{2 + hc % 2}"
                pg = ps[pgk]; pu = ps[puk]
                for cc in range(8):
                    ms.op("pe", lambda e, pg=pg, cc=cc, wb=wb, tg=tg: e.matmul(
                        pg[:, :], wgc[wb][:, cc * 128:(cc + 1) * 128],
                        cx.xT[:, cc * 4096 + tg * 512: cc * 4096 + (tg + 1) * 512],
                        start=(cc == 0), stop=(cc == 7)),
                        reads=(f"wgc{wb}", "xT"), writes=(pgk,), accum_key=("g", hc, tg))
                    ms.op("pe", lambda e, pu=pu, cc=cc, wb=wb, tg=tg: e.matmul(
                        pu[:, :], wuc[wb][:, cc * 128:(cc + 1) * 128],
                        cx.xT[:, cc * 4096 + tg * 512: cc * 4096 + (tg + 1) * 512],
                        start=(cc == 0), stop=(cc == 7)),
                        reads=(f"wuc{wb}", "xT"), writes=(puk,), accum_key=("u", hc, tg))
                s1b = s1[hc % 2]
                ms.op("act", lambda e, s1b=s1b, pg=pg: e.activation(
                    s1b[:, :], pg[:, :], AF.Silu, scale=1.0 / 127.0),
                    reads=(pgk,), writes=(f"s1{hc % 2}",))
                ms.op("dve", lambda e, pu=pu, s1b=s1b, hc=hc: e.scalar_tensor_tensor(
                    gu[:, hc * 512:(hc + 1) * 512],
                    pu[:, :], 1.0 / 127.0, s1b[:, :], ALU.mult, ALU.mult),
                    reads=(puk, f"s1{hc % 2}"), writes=(f"gu{hc}",))
            for t4 in range(4):
                tt = tg * 4 + t4
                b = t4 % 2
                ms.op("sp", lambda e, b=b, tt=tt: e.dma_start(
                    out=xres[b][:, :], in_=D["resid1"][tt * 128:(tt + 1) * 128, :]),
                    writes=(f"f0x{b}",), dma=True)
                for hc in range(24):
                    for half in range(2):
                        pdk = f"pd{(t4 % 2) * 2 + half}"
                        pd = ps[pdk]
                        ms.op("pe", lambda e, pd=pd, hc=hc, t4=t4, half=half: e.matmul(
                            pd[:, :], gu[:, hc * 512 + t4 * 128: hc * 512 + (t4 + 1) * 128],
                            wd[:, hc * 1024 + half * 512: hc * 1024 + half * 512 + 512],
                            start=(hc == 0), stop=(hc == 23)),
                            reads=(f"gu{hc}", f"wd{hc}"), writes=(pdk,),
                            accum_key=("d", tt, half))
                for half in range(2):
                    pdk = f"pd{(t4 % 2) * 2 + half}"
                    pd = ps[pdk]
                    lne.evac_half(ms, b, half, pd[:, :], pdk, xres[b], f"f0x{b}", 1.0 / 127.0)
                if B_SKIP_LN:
                    ms.op("sp", lambda e, b=b, tt=tt: e.dma_start(
                        out=D["resid2"][tt * 128:(tt + 1) * 128, :], in_=lne.asb[b][:, :]),
                        reads=(f"blnasb{b}",), writes=(), dma=True)
                else:
                    lne.ln(ms, b, D["resid2"], tt)
                if B_DEBUG and tt == 0:
                    ms.op("sp", lambda e: e.dma_start(out=D["dbg_gu"][:, :], in_=gu[:, :]),
                          reads=tuple(f"gu{h}" for h in range(24)), writes=(), dma=True)
                    ms.op("sp", lambda e: e.dma_start(out=D["dbg_xT"][:, :], in_=cx.xT[:, :]),
                          reads=("xT",), writes=(), dma=True)
                    ms.op("sp", lambda e: e.dma_start(out=D["dbg_w"][:, :], in_=wgc[1][:, :]),
                          reads=("wgc1",), writes=(), dma=True)
                    ms.op("sp", lambda e, b=b: e.dma_start(out=D["dbg_x"][:, :], in_=xres[b][:, :]),
                          reads=(f"f0x{b}",), writes=(), dma=True)
                    ms.op("sp", lambda e, b=b: e.dma_start(out=D["dbg_a"][:, :], in_=lne.asb[b][:, :]),
                          reads=(f"blnasb{b}",), writes=(), dma=True)
        ms.run_block(nc)
    bst_.close()


PHASE_FNS["B"] = phaseB


def phaseC(nc, cx, prog_stack):
    D = cx.D
    sems = cx.sems
    with ExitStack() as cst:
        kvpair = sb(nc, cst, "c_kvpair", [128, 8 * 128], BF16)
        with ExitStack() as st:
            cx.xT = sb(nc, st, "c_xT", [128, 8 * 4096], BF16)
            with ExitStack() as st0:
                ms = MS(sems)
                psx = alloc_psums(nc, st0, [("p0", (128, 512)), ("p1", (128, 512))], ms)
                build_xT(nc, cx, st0, D["resid2"], ms, psx)
                ms.run_block(nc)
            ms = MS(sems)
            ps = alloc_psums(nc, st, [("pk0", (128, 512)), ("pk1", (128, 512)),
                                      ("pv0", (128, 512)), ("pv1", (128, 512)),
                                      ("pkv0", (64, 512)), ("pkv1", (64, 512)),
                                      ("pq0", (128, 512)), ("pq1", (128, 512))], ms)
            wq = sb(nc, st, "c_wq", [128, 8 * 1024], BF16)
            wkv = sb(nc, st, "c_wkv", [128, 8 * 2048], BF16)
            qb = sb(nc, st, "c_qb", [128, 8], F32)
            reps = {}
            for nme, shape in [("kb_rep", (128, 1024)), ("vb_rep", (128, 1024)),
                               ("kng_rep", (128, 1024)), ("knb_rep", (128, 1024)),
                               ("vng_rep", (128, 1024)), ("vnb_rep", (128, 1024)),
                               ("temp_rep", (64, 1024)), ("kvbias", (64, 1024))]:
                reps[nme] = sb(nc, st, "c_" + nme, list(shape), F32)
                ms.op("sp", lambda e, nme=nme: e.dma_start(out=reps[nme][:, :], in_=D[nme][:, :]),
                      writes=(nme,), dma=True)
            ms.op("sp", lambda e: e.dma_start(out=qb[:, :], in_=D["qb_fm"][:, :]),
                  writes=("qb_fm",), dma=True)
            kraw = [sb(nc, st, f"c_kraw{i}", [128, 1024], F32) for i in range(2)]
            vraw = [sb(nc, st, f"c_vraw{i}", [128, 1024], F32) for i in range(2)]
            stats = {("k", 0): sb(nc, st, "c_stk0", [128, 64], F32),
                     ("k", 1): sb(nc, st, "c_stk1", [128, 64], F32),
                     ("v", 0): sb(nc, st, "c_stv0", [128, 64], F32),
                     ("v", 1): sb(nc, st, "c_stv1", [128, 64], F32)}
            tmp16 = sb(nc, st, "c_tmp16", [128, 16], F32)
            sqt = sb(nc, st, "c_sqt", [128, 1024], F32)
            t3b = [sb(nc, st, "c_t3s", [128, 1024], F32)] * 2
            klnb = [sb(nc, st, f"c_kln{i}", [128, 1024], BF16) for i in range(2)]
            vlnb = [sb(nc, st, f"c_vln{i}", [128, 1024], BF16) for i in range(2)]
            kvf = sb(nc, st, "c_kvf", [64, 1024], F32)
            qst = [sb(nc, st, f"c_qst{i}", [128, 512], BF16) for i in range(4)]
            for cc in range(8):
                ms.op("pool", lambda e, cc=cc: e.dma_start(
                    out=wq[:, cc * 1024:(cc + 1) * 1024],
                    in_=D["qkvT"][cc * 128:(cc + 1) * 128, 0:1024]),
                    writes=(f"wq{cc}",), dma=True)
                ms.op("pool", lambda e, cc=cc: e.dma_start(
                    out=wkv[:, cc * 2048:(cc + 1) * 2048],
                    in_=D["qkvT"][cc * 128:(cc + 1) * 128, 1024:3072]),
                    writes=(f"wkv{cc}",), dma=True)
            for jc in range(8):
                for tgp in range(4):
                    for cc in range(8):
                        for t2 in range(2):
                            tg = tgp * 2 + t2
                            pt = ps[f"pq{t2}"]
                            ms.op("pe", lambda e, pt=pt, cc=cc, jc=jc, tg=tg: e.matmul(
                                pt[:, :], wq[:, cc * 1024 + jc * 128: cc * 1024 + (jc + 1) * 128],
                                cx.xT[:, cc * 4096 + tg * 512: cc * 4096 + (tg + 1) * 512],
                                start=(cc == 0), stop=(cc == 7)),
                                reads=(f"wq{cc}", "xT"), writes=(f"pq{t2}",), accum_key=("q", jc, tg))
                    for t2 in range(2):
                        tg = tgp * 2 + t2
                        phn = f"pq{t2}"
                        pt = ps[phn]
                        sbq = qst[tg % 4]
                        sbk = f"qst{tg % 4}"
                        ms.op("act", lambda e, sbq=sbq, pt=pt, jc=jc: e.activation(
                            sbq[:, :], pt[:, :], AF.Identity, bias=qb[:, jc:jc + 1]),
                            reads=(phn, "qb_fm"), writes=(sbk,))
                        ms.op("sp", lambda e, sbq=sbq, jc=jc, tg=tg: e.dma_start(
                            out=D["s_qc"][jc, :, tg * 512:(tg + 1) * 512], in_=sbq[:, :]),
                            reads=(sbk,), writes=(), dma=True)
            for tt in range(NT):
                b = tt % 2
                for cc in range(8):
                    for nm in ("k", "v"):
                        wofs = 0 if nm == "k" else 1024
                        pko = "pk" if nm == "k" else "pv"
                        for half in range(2):
                            phn = f"{pko}{half}"
                            pt = ps[phn]
                            ms.op("pe", lambda e, pt=pt, cc=cc, tt=tt, half=half, wofs=wofs: e.matmul(
                                pt[:, :], cx.xT[:, cc * 4096 + tt * 128: cc * 4096 + (tt + 1) * 128],
                                wkv[:, cc * 2048 + wofs + half * 512: cc * 2048 + wofs + half * 512 + 512],
                                start=(cc == 0), stop=(cc == 7)),
                                reads=(f"wkv{cc}", "xT"), writes=(phn,), accum_key=(nm, tt, half))
                for nmi, nm in enumerate(("k", "v")):
                    raw = (kraw if nm == "k" else vraw)[b]
                    rawk = f"c_{nm}raw{b}"
                    stt_ = stats[(nm, b)]
                    stk = f"c_st{nm}{b}"
                    lnout = (klnb if nm == "k" else vlnb)[b]
                    lnk = f"c_{nm}ln{b}"
                    gname = f"{nm}ng_rep"; bname = f"{nm}nb_rep"
                    biasr = reps["kb_rep" if nm == "k" else "vb_rep"]
                    biask = "kb_rep" if nm == "k" else "vb_rep"
                    pko = "pk" if nm == "k" else "pv"
                    for half in range(2):
                        phn = f"{pko}{half}"
                        pt = ps[phn]
                        ms.op("dve", lambda e, raw=raw, pt=pt, half=half, biasr=biasr: e.scalar_tensor_tensor(
                            raw[:, half * 512:(half + 1) * 512], pt[:, :], 1.0,
                            biasr[:, half * 512:(half + 1) * 512], ALU.bypass, ALU.add),
                            reads=(phn, biask), writes=(rawk,))
                    ms.op("dve", lambda e, raw=raw, stt_=stt_: e.reduce_sum(
                        stt_[:, 0:16], raw[:, :].rearrange("p (a b) -> p a b", b=64), axis=AX.X),
                        reads=(rawk,), writes=(stk + "s",))
                    ms.op("act", lambda e, raw=raw: e.activation(sqt[:, :], raw[:, :], AF.Square),
                          reads=(rawk,), writes=("c_sqt",))
                    ms.op("dve", lambda e, stt_=stt_: e.reduce_sum(
                        stt_[:, 16:32], sqt[:, :].rearrange("p (a b) -> p a b", b=64), axis=AX.X),
                        reads=("c_sqt",), writes=(stk + "q",))
                    ms.op("dve", lambda e, stt_=stt_: e.tensor_scalar(
                        stt_[:, 32:48], stt_[:, 0:16], 1.0 / 64.0, None, ALU.mult),
                        reads=(stk + "s",), writes=(stk + "m",))
                    ms.op("dve", lambda e, stt_=stt_: e.tensor_tensor(
                        tmp16[:, :], stt_[:, 32:48], stt_[:, 32:48], ALU.mult),
                        reads=(stk + "m",), writes=("c_tmp16",))
                    ms.op("dve", lambda e, stt_=stt_: e.tensor_scalar(
                        stt_[:, 48:64], stt_[:, 16:32], 1.0 / 64.0, None, ALU.mult),
                        reads=(stk + "q",), writes=(stk + "v",))
                    ms.op("dve", lambda e, stt_=stt_: e.tensor_tensor(
                        stt_[:, 48:64], stt_[:, 48:64], tmp16[:, :], ALU.subtract),
                        reads=(stk + "v", "c_tmp16"), writes=(stk + "v",))
                    ms.op("dve", lambda e, stt_=stt_: e.tensor_scalar(
                        stt_[:, 48:64], stt_[:, 48:64], 1.0, 1e-5, ALU.mult, ALU.add),
                        reads=(stk + "v",), writes=(stk + "v",))
                    ms.op("act", lambda e, stt_=stt_: e.activation(
                        stt_[:, 48:64], stt_[:, 48:64], AF.Ln),
                        reads=(stk + "v",), writes=(stk + "v",))
                    ms.op("act", lambda e, stt_=stt_: e.activation(
                        stt_[:, 48:64], stt_[:, 48:64], AF.Exp, scale=-0.5),
                        reads=(stk + "v",), writes=(stk + "v",))
                    t3 = t3b[b]
                    m3d = stt_[:, 32:48].rearrange("p (a b) -> p a b", b=1).broadcast_to([128, 16, 64])
                    r3d = stt_[:, 48:64].rearrange("p (a b) -> p a b", b=1).broadcast_to([128, 16, 64])
                    raw3d = raw[:, :].rearrange("p (a b) -> p a b", b=64)
                    t33d = t3[:, :].rearrange("p (a b) -> p a b", b=64)
                    ms.op("dve", lambda e, t33d=t33d, raw3d=raw3d, m3d=m3d: e.tensor_tensor(
                        t33d, raw3d, m3d, ALU.subtract),
                        reads=(rawk, stk + "m"), writes=("c_t3",))
                    ms.op("pool", lambda e, t33d=t33d, r3d=r3d: e.tensor_tensor(
                        t33d, t33d, r3d, ALU.mult),
                        reads=("c_t3", stk + "v"), writes=("c_t3",))
                    ms.op("dve", lambda e, t3=t3, g=reps[gname]: e.tensor_tensor(
                        t3[:, :], t3[:, :], g[:, :], ALU.mult),
                        reads=("c_t3", gname), writes=("c_t3",))
                    ms.op("dve", lambda e, lnout=lnout, t3=t3, bb=reps[bname]: e.tensor_tensor(
                        lnout[:, :], t3[:, :], bb[:, :], ALU.add),
                        reads=("c_t3", bname), writes=(lnk,))
                for h in range(16):
                    bank = h // 8
                    phn = f"pkv{bank}"
                    pt = ps[phn]
                    ms.op("pe", lambda e, pt=pt, h=h, tt=tt, b=b: e.matmul(
                        pt[:, (h % 8) * 64:(h % 8) * 64 + 64],
                        klnb[b][:, h * 64:(h + 1) * 64], vlnb[b][:, h * 64:(h + 1) * 64],
                        start=(tt == 0 and h % 8 == 0), stop=(tt == NT - 1 and h % 8 == 7)),
                        reads=(f"c_kln{b}", f"c_vln{b}"), writes=(phn,), accum_key=("kv", bank))
            for bank in range(2):
                pt = ps[f"pkv{bank}"]
                ms.op("dve", lambda e, pt=pt, bank=bank: e.scalar_tensor_tensor(
                    kvf[:, bank * 512:(bank + 1) * 512], pt[:, :], 1.0 / 4096.0,
                    reps["temp_rep"][:, bank * 512:(bank + 1) * 512], ALU.mult, ALU.mult),
                    reads=(f"pkv{bank}", "temp_rep"), writes=("c_kvf",))
            ms.op("pool", lambda e: e.tensor_tensor(kvf[:, :], kvf[:, :], reps["kvbias"][:, :], ALU.add),
                  reads=("c_kvf", "kvbias"), writes=("c_kvf",))
            ms.op("pool", lambda e: e.memset(kvpair[:, :], 0.0), writes=("kvpair",))
            for h in range(16):
                pr = h // 2
                rofs = (h % 2) * 64
                cofs = pr * 128 + (h % 2) * 64
                ms.op("dve", lambda e, h=h, rofs=rofs, cofs=cofs: e.tensor_copy(
                    kvpair[rofs:rofs + 64, cofs:cofs + 64], kvf[:, h * 64:(h + 1) * 64]),
                    reads=("c_kvf",), writes=("kvpair",))
            ms.run_block(nc)
        with ExitStack() as st:
            ms = MS(sems)
            ps = alloc_psums(nc, st, [("pa0", (128, 512)), ("pa1", (128, 512)),
                                      ("pt0", (128, 512)), ("pt1", (128, 512)),
                                      ("pp0", (128, 512)), ("pp1", (128, 512)),
                                      ("pq0", (128, 512)), ("pq1", (128, 512))], ms)
            p1w = sb(nc, st, "c_p1w", [128, 8 * 1024], BF16)
            p2w = sb(nc, st, "c_p2w", [128, 8 * 1024], BF16)
            p1b = sb(nc, st, "c_p1b", [128, 1024], F32)
            p2b = sb(nc, st, "c_p2b", [128, 1024], F32)
            x2r = [sb(nc, st, f"c_x2r{i}", [128, 1024], F32) for i in range(2)]
            xb2 = [sb(nc, st, f"c_xb2{i}", [128, 1024], F32) for i in range(2)]
            a1 = [sb(nc, st, f"c_a1{i}", [128, 1024], F32) for i in range(2)]
            a1T = [sb(nc, st, f"c_a1T{i}", [128, 1024], BF16) for i in range(2)]
            h1f = [sb(nc, st, f"c_h1f{i}", [128, 1024], F32) for i in range(2)]
            h1g = [sb(nc, st, f"c_h1g{i}", [128, 1024], F32) for i in range(2)]
            h1T = [sb(nc, st, f"c_h1T{i}", [128, 1024], BF16) for i in range(2)]
            qtile = [sb(nc, st, f"c_qtile{i}", [128, 8 * 128], BF16) for i in range(2)]
            lne = LNEmit(nc, st, "cln")
            ms.op("sp", lambda e: e.dma_start(out=p1b[:, :], in_=D["p1b_rep"][:, :]), writes=("p1b",), dma=True)
            ms.op("sp", lambda e: e.dma_start(out=p2b[:, :], in_=D["p2b_rep"][:, :]), writes=("p2b",), dma=True)
            for cc in range(8):
                ms.op("pool", lambda e, cc=cc: e.dma_start(
                    out=p1w[:, cc * 1024:(cc + 1) * 1024],
                    in_=D["p1T"][cc * 128:(cc + 1) * 128, :]),
                    writes=(f"p1w{cc}",), dma=True)
                ms.op("pool", lambda e, cc=cc: e.dma_start(
                    out=p2w[:, cc * 1024:(cc + 1) * 1024],
                    in_=D["p2T"][cc * 128:(cc + 1) * 128, :]),
                    writes=(f"p2w{cc}",), dma=True)
            def transpose_group(tt, srck, srct, dstk, dstt, b):
                for half in range(2):
                    phn = f"pt{half}"
                    pt = ps[phn]
                    for k in range(4):
                        cc = half * 4 + k
                        ms.op("pe", lambda e, pt=pt, k=k, cc=cc, srct=srct: e.matmul(
                            pt[:, k * 128:(k + 1) * 128], srct[:, cc * 128:(cc + 1) * 128],
                            cx.identf[:, :], start=(k == 0), stop=(k == 3), is_transpose=True),
                            reads=(srck, "ident"), writes=(phn,), accum_key=("tr", tt, srck, half))
                    q = "dve" if half == 0 else "act"
                    dst = dstt[:, half * 512:(half + 1) * 512]
                    if q == "dve":
                        ms.op(q, lambda e, pt=pt, dst=dst: e.tensor_copy(dst, pt[:, :]),
                              reads=(phn,), writes=(dstk,))
                    else:
                        ms.op(q, lambda e, pt=pt, dst=dst: e.copy(dst, pt[:, :]),
                              reads=(phn,), writes=(dstk,))
            for tt in range(NT):
                b = tt % 2
                ms.op("sp", lambda e, b=b, tt=tt: e.dma_start(
                    out=x2r[b][:, :], in_=D["resid2"][tt * 128:(tt + 1) * 128, :]),
                    writes=(f"x2r{b}",), dma=True)
                ms.op("sp", lambda e, b=b, tt=tt: e.dma_start(
                    out=qtile[b][:, :].rearrange("p (a t) -> p a t", t=128),
                    in_=D["s_qc"][:, :, tt * 128:(tt + 1) * 128].rearrange("a p t -> p a t")),
                    writes=(f"qtile{b}",), dma=True)
                ms.op("pool", lambda e, b=b: e.tensor_tensor(
                    xb2[b][:, :], x2r[b][:, :], x2r[b][:, :], ALU.add),
                    reads=(f"x2r{b}",), writes=(f"xb2{b}",))
                ms.op("pool", lambda e, b=b: e.tensor_tensor(
                    xb2[b][:, :], xb2[b][:, :], p2b[:, :], ALU.add),
                    reads=(f"xb2{b}", "p2b"), writes=(f"xb2{b}",))
                for jc in range(8):
                    phn = f"pa{jc // 4}"
                    pt = ps[phn]
                    ms.op("pe", lambda e, pt=pt, jc=jc, b=b: e.matmul(
                        pt[:, (jc % 4) * 128:(jc % 4) * 128 + 128],
                        qtile[b][:, jc * 128:(jc + 1) * 128],
                        kvpair[:, jc * 128:(jc + 1) * 128],
                        start=(jc % 4 == 0), stop=(jc % 4 == 3)),
                        reads=(f"qtile{b}", "kvpair"), writes=(phn,), accum_key=("att", tt, jc // 4))
                for half in range(2):
                    pt = ps[f"pa{half}"]
                    ms.op("dve", lambda e, pt=pt, b=b, half=half: e.scalar_tensor_tensor(
                        a1[b][:, half * 512:(half + 1) * 512], pt[:, :], 1.0,
                        x2r[b][:, half * 512:(half + 1) * 512], ALU.bypass, ALU.add),
                        reads=(f"pa{half}", f"x2r{b}"), writes=(f"a1{b}",))
                transpose_group(tt, f"a1{b}", a1[b], f"a1T{b}", a1T[b], b)
                for cc in range(8):
                    for half in range(2):
                        pt = ps[f"pp{half}"]
                        ms.op("pe", lambda e, pt=pt, cc=cc, b=b, half=half: e.matmul(
                            pt[:, :], a1T[b][:, cc * 128:(cc + 1) * 128],
                            p1w[:, cc * 1024 + half * 512: cc * 1024 + half * 512 + 512],
                            start=(cc == 0), stop=(cc == 7)),
                            reads=(f"a1T{b}", f"p1w{cc}"), writes=(f"pp{half}",), accum_key=("p1", tt, half))
                for half in range(2):
                    phn = f"pp{half}"
                    pt = ps[phn]
                    ms.op("dve", lambda e, pt=pt, b=b, half=half: e.scalar_tensor_tensor(
                        h1f[b][:, half * 512:(half + 1) * 512], pt[:, :], 1.0,
                        p1b[:, half * 512:(half + 1) * 512], ALU.bypass, ALU.add),
                        reads=(phn, "p1b"), writes=(f"h1f{b}",))
                ms.op("act", lambda e, b=b: e.activation(h1g[b][:, :], h1f[b][:, :], AF.Gelu),
                      reads=(f"h1f{b}",), writes=(f"h1g{b}",))
                transpose_group(tt, f"h1g{b}", h1g[b], f"h1T{b}", h1T[b], b)
                for cc in range(8):
                    for half in range(2):
                        pt = ps[f"pq{half}"]
                        ms.op("pe", lambda e, pt=pt, cc=cc, b=b, half=half: e.matmul(
                            pt[:, :], h1T[b][:, cc * 128:(cc + 1) * 128],
                            p2w[:, cc * 1024 + half * 512: cc * 1024 + half * 512 + 512],
                            start=(cc == 0), stop=(cc == 7)),
                            reads=(f"h1T{b}", f"p2w{cc}"), writes=(f"pq{half}",), accum_key=("p2", tt, half))
                for half in range(2):
                    phn = f"pq{half}"
                    pt = ps[phn]
                    lne.evac_half(ms, b, half, pt[:, :], phn, xb2[b], f"xb2{b}", 1.0)
                lne.ln(ms, b, D["resid3"], tt)
            ms.run_block(nc)


PHASE_FNS["C"] = phaseC


D_NQ = 4
D_NPAIR = 17


def phaseD(nc, cx, prog_stack):
    D = cx.D
    sems = cx.sems
    with ExitStack() as st:
        cx.xT = sb(nc, st, "d_xT", [128, 8 * 4096], BF16)
        ms = MS(sems)
        ps = alloc_psums(nc, st, [("pg0", (128, 512)), ("pg1", (128, 512)),
                                  ("pu0", (128, 512)), ("pu1", (128, 512)),
                                  ("pd0", (128, 512)), ("pd1", (128, 512)),
                                  ("pcb", (128, 512)), ("pex", (128, 512))], ms)
        ms.psum_keys.update(["plg", "ptc"])
        plg = ps["pex"][:, 0:32]
        ptc = ps["pex"][0:32, 128:256]
        load_ident(nc, cx, ms)
        xt = [sb(nc, st, f"d_xt{i}", [128, 1024], F32) for i in range(2)]
        stg = sb(nc, st, "d_stg", [128, 1024], F32)
        gw = sb(nc, st, "d_gw", [128, 8 * 32], F32)
        gbr = sb(nc, st, "d_gbr", [128, 32], F32)
        scores = sb(nc, st, "d_scores", [128, 32], F32)
        scb = sb(nc, st, "d_scb", [128, 32], F32)
        eqb = sb(nc, st, "d_eqb", [128, 32], F32)
        wrk = sb(nc, st, "d_wrk", [128, 32], F32)
        gm1 = sb(nc, st, "d_gm1", [128, 8], F32)
        gm2 = sb(nc, st, "d_gm2", [128, 8], F32)
        gs = sb(nc, st, "d_gs", [128, 8], F32)
        g8s = sb(nc, st, "d_g8s", [128, 8], F32)
        gsm = sb(nc, st, "d_gsm", [128, 8], F32)
        sm = sb(nc, st, "d_sm", [128, 32], F32)
        mkd = sb(nc, st, "d_mkd", [128, 32], F32)
        mm8 = sb(nc, st, "d_mm8", [128, 8], F32)
        mrk = sb(nc, st, "d_mrk", [128, 32], F32)
        sel = sb(nc, st, "d_sel", [128, 32], F32)
        cw = sb(nc, st, "d_cw", [128, 32], F32)
        den = sb(nc, st, "d_den", [128, 2], F32)
        comb = sb(nc, st, "d_comb", [128, 32], F32)
        combT = sb(nc, st, "d_combT", [32, 1024], BF16)
        racc = sb(nc, st, "d_racc", [128, 8 * 1024], F32)
        wge = [sb(nc, st, f"d_wge{i}", [128, 8 * 512], BF16) for i in range(2)]
        wue = [sb(nc, st, f"d_wue{i}", [128, 8 * 512], BF16) for i in range(2)]
        wde = [sb(nc, st, f"d_wde{i}", [128, 4 * 1024], BF16) for i in range(2)]
        gup = [sb(nc, st, f"d_gup{i}", [128, 8 * 512], BF16) for i in range(2)]
        s1b = [sb(nc, st, f"d_s1{i}", [128, 512], F32) for i in range(2)]
        s1c = [sb(nc, st, f"d_s1c{i}", [128, 512], BF16) for i in range(2)]
        cbs = [sb(nc, st, f"d_cbs{i}", [128, 512], BF16) for i in range(2)]
        cbones = sb(nc, st, "d_cbones", [128, 512], BF16)
        cbrow = [sb(nc, st, f"d_cbrow{i}", [1, 512], BF16) for i in range(2)]
        onesb1 = sb(nc, st, "d_ones1", [1, 128], BF16)
        x3r = [sb(nc, st, "d_x3r0", [128, 1024], F32)]
        lne = LNEmit(nc, st, "dln")
        ms.op("pool", lambda e: e.memset(cbones[:, :], 1.0), writes=("cbones",))
        ms.op("pool", lambda e: e.memset(onesb1[:, :], 1.0), writes=("ones1",))
        ms.op("sp", lambda e: e.dma_start(
            out=gw[:, :].rearrange("p (a b) -> p a b", b=32),
            in_=D["gwT"][:, :].rearrange("(a p) e -> p a e", p=128)),
            writes=("gw",), dma=True)
        ms.op("sp", lambda e: e.dma_start(out=gbr[:, :], in_=D["gb_rep"][:, :]),
              writes=("gbr",), dma=True)

        def load_pair_weights(pi):
            shared = (pi == 0)
            for eq in range(2):
                eb = eq
                if shared:
                    wg_in = D["shgT"][:, eq * 512:(eq + 1) * 512]
                    wu_in = D["shuT"][:, eq * 512:(eq + 1) * 512]
                    wd_in = D["shdT"][eq * 512:(eq + 1) * 512, :]
                else:
                    e_ = (pi - 1) * 2 + eq
                    wg_in = D["wgE"][e_]
                    wu_in = D["wuE"][e_]
                    wd_in = D["wdE"][e_]
                ms.op("pool", lambda e, wg_in=wg_in, eb=eb: e.dma_start(
                    out=wge[eb][:, :].rearrange("p (a b) -> p a b", b=512),
                    in_=wg_in.rearrange("(a p) i -> p a i", p=128)),
                    writes=(f"wge{eb}",), dma=True)
                ms.op("pool", lambda e, wu_in=wu_in, eb=eb: e.dma_start(
                    out=wue[eb][:, :].rearrange("p (a b) -> p a b", b=512),
                    in_=wu_in.rearrange("(a p) i -> p a i", p=128)),
                    writes=(f"wue{eb}",), dma=True)
                ms.op("pool", lambda e, wd_in=wd_in, eb=eb: e.dma_start(
                    out=wde[eb][:, :].rearrange("p (a b) -> p a b", b=1024),
                    in_=wd_in.rearrange("(a p) c -> p a c", p=128)),
                    writes=(f"wde{eb}",), dma=True)

        for Q in range(D_NQ):
            load_pair_weights(0)
            for t8 in range(8):
                tt = Q * 8 + t8
                b = tt % 2
                ms.op("sp", lambda e, b=b, tt=tt: e.dma_start(
                    out=xt[b][:, :], in_=D["resid3"][tt * 128:(tt + 1) * 128, :]),
                    writes=(f"xt{b}",), dma=True)
                for half in range(2):
                    phn = f"pd{half}"
                    pt = ps[phn]
                    for k in range(4):
                        cc = half * 4 + k
                        ms.op("pe", lambda e, pt=pt, k=k, b=b, cc=cc: e.matmul(
                            pt[:, k * 128:(k + 1) * 128], xt[b][:, cc * 128:(cc + 1) * 128],
                            cx.identf[:, :], start=(k == 0), stop=(k == 3), is_transpose=True),
                            reads=(f"xt{b}", "ident"), writes=(phn,), accum_key=("tr", tt, half))
                    ms.op("dve", lambda e, pt=pt, half=half: e.tensor_copy(
                        stg[:, half * 512:(half + 1) * 512], pt[:, :]),
                        reads=(phn,), writes=("stg",))
                    dstv = cx.xT[:, :].rearrange("p (a c) -> p a c", c=4096)[:, half * 4:half * 4 + 4,
                                                                            tt * 128:(tt + 1) * 128]
                    ms.op("act", lambda e, pt=pt, dstv=dstv: e.copy(
                        dstv, pt[:, :].rearrange("p (a c) -> p a c", c=128)),
                        reads=(phn,), writes=(f"xTq{Q}",))
                for cc in range(8):
                    ms.op("pe", lambda e, cc=cc: e.matmul(
                        plg, stg[:, cc * 128:(cc + 1) * 128],
                        gw[:, cc * 32:(cc + 1) * 32], start=(cc == 0), stop=(cc == 7)),
                        reads=("stg", "gw"), writes=("plg",), accum_key=("lg", tt))
                ms.op("act", lambda e: e.activation(scores[:, :], plg, AF.Sigmoid),
                      reads=("plg",), writes=("scores",))
                ms.op("pool", lambda e: e.tensor_tensor(scb[:, :], scores[:, :], gbr[:, :], ALU.add),
                      reads=("scores", "gbr"), writes=("scb",))
                scb3 = scb[:, :].rearrange("p (a b) -> p a b", b=4)
                ms.op("dve", lambda e, scb3=scb3: e.reduce_max(gm1[:, :], scb3, axis=AX.X),
                      reads=("scb",), writes=("gm1",))
                m1b = gm1[:, :].rearrange("p (a b) -> p a b", b=1).broadcast_to([128, 8, 4])
                eqb3 = eqb[:, :].rearrange("p (a b) -> p a b", b=4)
                ms.op("dve", lambda e, scb3=scb3, m1b=m1b, eqb3=eqb3: e.tensor_tensor(
                    eqb3, scb3, m1b, ALU.is_equal),
                    reads=("scb", "gm1"), writes=("eqb",))
                ms.op("dve", lambda e: e.scalar_tensor_tensor(
                    wrk[:, :], eqb[:, :], -1e30, scb[:, :], ALU.mult, ALU.add),
                    reads=("eqb", "scb"), writes=("wrk",))
                wrk3 = wrk[:, :].rearrange("p (a b) -> p a b", b=4)
                ms.op("dve", lambda e, wrk3=wrk3: e.reduce_max(gm2[:, :], wrk3, axis=AX.X),
                      reads=("wrk",), writes=("gm2",))
                ms.op("pool", lambda e: e.tensor_tensor(gs[:, :], gm1[:, :], gm2[:, :], ALU.add),
                      reads=("gm1", "gm2"), writes=("gs",))
                ms.op("dve", lambda e: e.max(g8s[:, :], gs[:, :]),
                      reads=("gs",), writes=("g8s",))
                ms.op("dve", lambda e: e.tensor_scalar(gsm[:, :], gs[:, :], g8s[:, 3:4], None, ALU.is_ge),
                      reads=("g8s", "gs"), writes=("gsm",))
                sm3 = sm[:, :].rearrange("p (a b) -> p a b", b=4)
                gb3 = gsm[:, :].rearrange("p (a b) -> p a b", b=1).broadcast_to([128, 8, 4])
                ms.op("dve", lambda e, sm3=sm3, gb3=gb3: e.tensor_copy(sm3, gb3),
                      reads=("gsm",), writes=("sm",))
                ms.op("pool", lambda e: e.tensor_tensor(mkd[:, :], sm[:, :], scb[:, :], ALU.mult),
                      reads=("sm", "scb"), writes=("mkd",))
                ms.op("dve", lambda e: e.max(mm8[:, :], mkd[:, :]),
                      reads=("mkd",), writes=("mm8",))
                ms.op("dve", lambda e: e.match_replace(mrk[:, :], mm8[:, :], mkd[:, :], -1e30),
                      reads=("mm8", "mkd"), writes=("mrk",))
                ms.op("dve", lambda e: e.tensor_single_scalar(sel[:, :], mrk[:, :], -1e20, ALU.is_lt),
                      reads=("mrk",), writes=("sel",))
                ms.op("pool", lambda e: e.tensor_tensor(cw[:, :], sel[:, :], scores[:, :], ALU.mult),
                      reads=("sel", "scores"), writes=("cw",))
                ms.op("dve", lambda e: e.reduce_sum(den[:, 0:1], cw[:, :], axis=AX.X),
                      reads=("cw",), writes=("den",))
                ms.op("dve", lambda e: e.tensor_scalar(den[:, 0:1], den[:, 0:1], 1.0, 1e-20,
                                                        ALU.mult, ALU.add),
                      reads=("den",), writes=("den",))
                ms.op("dve", lambda e: e.reciprocal(den[:, 1:2], den[:, 0:1]),
                      reads=("den",), writes=("denr",))
                ms.op("dve", lambda e: e.tensor_scalar(comb[:, :], cw[:, :], den[:, 1:2], 2.5,
                                                        ALU.mult, ALU.mult),
                      reads=("cw", "denr"), writes=("comb",))
                ms.op("pe", lambda e: e.transpose(ptc, comb[:, :], cx.identf[:, :]),
                      reads=("comb", "ident"), writes=("ptc",))
                ms.op("dve", lambda e, t8=t8: e.tensor_copy(
                    combT[:, t8 * 128:(t8 + 1) * 128], ptc),
                    reads=("ptc",), writes=("combT",))
            ms.op("act", lambda e, tt=Q * 8: e.dma_start(
                out=x3r[0][:, :], in_=D["resid3"][tt * 128:(tt + 1) * 128, :]),
                writes=("d_x3r0",), dma=True)
            for pi in range(D_NPAIR):
                shared = (pi == 0)
                if pi > 0:
                    load_pair_weights(pi)
                for eq in range(2):
                    eb = eq
                    if not shared:
                        e_ = (pi - 1) * 2 + eq
                        for tg2 in range(2):
                            ms.op("sp", lambda e, e_=e_, eq=eq, Q=Q, tg2=tg2: e.dma_start(
                                out=cbrow[tg2][:, :], in_=combT[e_:e_ + 1, tg2 * 512:(tg2 + 1) * 512]),
                                reads=("combT",), writes=(f"cbrow{tg2}",), dma=True)
                            ms.op("pe", lambda e, tg2=tg2: e.matmul(
                                ps["pcb"][:, :], onesb1[:, :], cbrow[tg2][:, :],
                                start=True, stop=True),
                                reads=(f"cbrow{tg2}", "ones1"), writes=("pcb",))
                            ms.op("dve", lambda e, tg2=tg2: e.tensor_copy(cbs[tg2][:, :], ps["pcb"][:, :]),
                                  reads=("pcb",), writes=(f"cbs{tg2}",))
                    for ic in range(4):
                        for cc in range(8):
                            for tg2 in range(2):
                                tgo = Q * 1024 + tg2 * 512
                                pg = ps[f"pg{tg2}"]
                                ms.op("pe", lambda e, pg=pg, cc=cc, eb=eb, ic=ic, tgo=tgo: e.matmul(
                                    pg[:, :], wge[eb][:, cc * 512 + ic * 128: cc * 512 + (ic + 1) * 128],
                                    cx.xT[:, cc * 4096 + tgo: cc * 4096 + tgo + 512],
                                    start=(cc == 0), stop=(cc == 7)),
                                    reads=(f"wge{eb}", f"xTq{Q}"), writes=(f"pg{tg2}",), accum_key=("g", Q, pi, eq, ic, tg2))
                        for cc in range(8):
                            for tg2 in range(2):
                                tgo = Q * 1024 + tg2 * 512
                                pu = ps[f"pu{tg2}"]
                                ms.op("pe", lambda e, pu=pu, cc=cc, eb=eb, ic=ic, tgo=tgo: e.matmul(
                                    pu[:, :], wue[eb][:, cc * 512 + ic * 128: cc * 512 + (ic + 1) * 128],
                                    cx.xT[:, cc * 4096 + tgo: cc * 4096 + tgo + 512],
                                    start=(cc == 0), stop=(cc == 7)),
                                    reads=(f"wue{eb}", f"xTq{Q}"), writes=(f"pu{tg2}",), accum_key=("u", Q, pi, eq, ic, tg2))
                        for tg2 in range(2):
                            pg = ps[f"pg{tg2}"]; pu = ps[f"pu{tg2}"]
                            cbk = "cbones" if shared else f"cbs{tg2}"
                            cbt = cbones if shared else cbs[tg2]
                            sb1 = s1b[tg2]
                            ms.op("act", lambda e, sb1=sb1, pg=pg: e.activation(
                                sb1[:, :], pg[:, :], AF.Silu),
                                reads=(f"pg{tg2}",), writes=(f"s1{tg2}",))
                            sc1 = s1c[tg2]
                            ms.op("pool", lambda e, sc1=sc1, sb1=sb1, cbt=cbt: e.tensor_tensor(
                                sc1[:, :], sb1[:, :], cbt[:, :], ALU.mult),
                                reads=(f"s1{tg2}", cbk), writes=(f"s1c{tg2}",))
                            ms.op("dve", lambda e, pu=pu, sc1=sc1, eq=eq, ic=ic, tg2=tg2: e.tensor_tensor(
                                gup[tg2][:, (eq * 4 + ic) * 512:(eq * 4 + ic + 1) * 512],
                                pu[:, :], sc1[:, :], ALU.mult),
                                reads=(f"pu{tg2}", f"s1c{tg2}"), writes=(f"gup{tg2}_{eq}_{ic}",))
                for tg2 in range(2):
                    tgo = Q * 1024 + tg2 * 512
                    for t4 in range(4):
                        tt = (tgo // 128) + t4
                        for eq in range(2):
                            for ic in range(4):
                                for half in range(2):
                                    pdk = f"pd{half}"
                                    pd = ps[pdk]
                                    ms.op("pe", lambda e, pd=pd, eq=eq, ic=ic, t4=t4, half=half, tg2=tg2: e.matmul(
                                        pd[:, :],
                                        gup[tg2][:, (eq * 4 + ic) * 512 + t4 * 128:(eq * 4 + ic) * 512 + (t4 + 1) * 128],
                                        wde[eq][:, ic * 1024 + half * 512: ic * 1024 + half * 512 + 512],
                                        start=(eq == 0 and ic == 0), stop=(eq == 1 and ic == 3)),
                                        reads=(f"gup{tg2}_{eq}_{ic}", f"wde{eq}"), writes=(pdk,),
                                        accum_key=("d", Q, pi, tg2, t4, half))
                        for half in range(2):
                            pdk = f"pd{half}"
                            pd = ps[pdk]
                            loc = (tt - Q * 8) * 1024 + half * 512
                            if shared:
                                ms.op("dve", lambda e, pd=pd, loc=loc: e.tensor_copy(
                                    racc[:, loc:loc + 512], pd[:, :]),
                                    reads=(pdk,), writes=(f"racc{loc}",))
                            else:
                                ms.op("dve", lambda e, pd=pd, loc=loc: e.scalar_tensor_tensor(
                                    racc[:, loc:loc + 512], pd[:, :], 1.0,
                                    racc[:, loc:loc + 512], ALU.bypass, ALU.add),
                                    reads=(pdk, f"racc{loc}"), writes=(f"racc{loc}",))
            for t8 in range(8):
                tt = Q * 8 + t8
                b = t8 % 2
                for half in range(2):
                    loc = t8 * 1024 + half * 512
                    lne.evac_half(ms, b, half, racc[:, loc:loc + 512], f"racc{loc}",
                                  x3r[0], "d_x3r0", 1.0)
                lne.ln(ms, b, D["out"], tt)
                if t8 < 7:
                    tt2 = Q * 8 + t8 + 1
                    ms.op("act", lambda e, tt2=tt2: e.dma_start(
                        out=x3r[0][:, :], in_=D["resid3"][tt2 * 128:(tt2 + 1) * 128, :]),
                        writes=("d_x3r0",), dma=True)
        ms.run_block(nc)


PHASE_FNS["D"] = phaseD


# version 34 (final)
# speedup vs baseline: 1.1973x; 1.0766x over previous
import numpy as np
import ml_dtypes
import concourse.bass as bass
import concourse.mybir as mybir
import concourse.bass_utils as _bass_utils
from concourse.bass_utils import run_bass_kernel_spmd

if not getattr(_bass_utils, "_ldw_opt_patched", False):
    _orig_run_command = _bass_utils.run_command

    def _run_command_ldw(cmd, *a, **kw):
        cmd = ["--enable-ldw-opt=true" if c == "--enable-ldw-opt=false" else c for c in cmd]
        return _orig_run_command(cmd, *a, **kw)

    _bass_utils.run_command = _run_command_ldw
    _bass_utils._ldw_opt_patched = True

F32 = mybir.dt.float32
BF16 = mybir.dt.bfloat16
I32 = mybir.dt.int32
U8 = mybir.dt.uint8
AF = mybir.ActivationFunctionType
ALU = mybir.AluOpType
AX = mybir.AxisListType

T = 4096
C = 1024
NT = T // 128
NG = T // 512
EPS32 = float(np.finfo(np.float32).eps)

ENGMAP = {"pe": "tensor", "act": "scalar", "dve": "vector", "pool": "gpsimd", "sp": "sync"}
QUEUES = ["pe", "act", "dve", "pool", "sp"]


NDMASEM = 6
INHERIT_CLOCKS = True


class SemState:
    def __init__(self, nc, stack):
        self.nc = nc
        self.csem = {}
        self.dsem = {}
        self.ccount = {q: 0 for q in QUEUES}
        self.dcount = {}
        self.dtotal = {q: 0 for q in QUEUES}
        for q in QUEUES:
            self.csem[q] = stack.enter_context(nc.semaphore(f"c_{q}"))
        for q in ("sp", "pool", "act", "dve"):
            self.dsem[q] = [stack.enter_context(nc.semaphore(f"d_{q}{i}")) for i in range(NDMASEM)]
            for i in range(NDMASEM):
                self.dcount[(q, i)] = 0
        self.all_handles = list(self.csem.values()) + [h for l in self.dsem.values() for h in l]

    def emit_clear(self, nc):
        nums = sorted(h.num for h in self.all_handles)
        assert nums == list(range(nums[0], nums[0] + len(nums)))
        rng = range(nums[0], nums[-1] + 1)
        with nc.Block() as blk:
            @blk.gpsimd
            def _(eng):
                eng.dma_reset(rng)
                eng.sem_clear(rng)
        for q in QUEUES:
            self.ccount[q] = 0
            self.dtotal[q] = 0
        for k in self.dcount:
            self.dcount[k] = 0


class Op:
    __slots__ = ["q", "emit", "reads", "writes", "dma", "idx", "deps", "inc", "val",
                 "clock", "qidx", "accum_key"]
    def __init__(self, q, emit, reads, writes, dma, accum_key=None):
        self.q = q; self.emit = emit
        self.reads = reads; self.writes = writes
        self.dma = dma; self.accum_key = accum_key
        self.deps = []
        self.inc = False
        self.val = None
        self.clock = None


class MS:
    def __init__(self, sems):
        self.sems = sems
        self.ops = []
        self.last_writer = {}
        self.readers = {}
        self.psum_keys = set()

    def op(self, q, emit, reads=(), writes=(), dma=False, accum_key=None):
        o = Op(q, emit, tuple(reads), tuple(writes), dma, accum_key)
        o.idx = len(self.ops)
        dep = {}
        def add(d):
            if d is not None and d is not o:
                dep[d.idx] = d
        for b in o.reads:
            add(self.last_writer.get(b))
            if b in self.psum_keys:
                for r in self.readers.get(b, ()):
                    if r.q != o.q:
                        add(r)
        for b in o.writes:
            add(self.last_writer.get(b))
            for r in self.readers.get(b, ()):
                add(r)
        deps = []
        for d in dep.values():
            if d.q == o.q and d.dma and o.dma:
                continue
            if (accum_key is not None and d.accum_key == accum_key and d.q == o.q == "pe"):
                continue
            deps.append(d)
        o.deps = deps
        if dma:
            o.inc = True
        for d in deps:
            d.inc = True
        for b in o.writes:
            self.last_writer[b] = o
            self.readers[b] = []
        for b in o.reads:
            self.readers.setdefault(b, []).append(o)
        self.ops.append(o)
        return o

    def finalize_counts(self):
        last_compute = {}
        for o in self.ops:
            if not o.dma:
                last_compute[o.q] = o
        for o in last_compute.values():
            o.inc = True
        for o in self.ops:
            if o.q == "pool" and not o.dma:
                o.inc = True
        for o in self.ops:
            if o.inc:
                if o.dma:
                    slot = self.sems.dtotal[o.q] % NDMASEM
                    self.sems.dtotal[o.q] += 1
                    self.sems.dcount[(o.q, slot)] += 16
                    o.val = ("d", (o.q, slot), self.sems.dcount[(o.q, slot)])
                else:
                    self.sems.ccount[o.q] += 1
                    o.val = ("c", o.q, self.sems.ccount[o.q])

    @property
    def ops_last_per_queue(self):
        last = {}
        for o in self.ops:
            last[o.q] = o
        return last

    def emit_queue(self, q, eng, final_drain=False):
        known = {}
        def sem_handle(val):
            kind, qq, v = val
            if kind == "d":
                return self.sems.dsem[qq[0]][qq[1]], v
            return self.sems.csem[qq], v
        for o in self.ops:
            if o.q != q:
                continue
            for d in o.deps:
                assert d.val is not None
                key = (d.val[0], d.val[1])
                if known.get(key, 0) >= d.val[2]:
                    continue
                h, v = sem_handle(d.val)
                eng.wait_ge(h, v)
                known[key] = d.val[2]
                if INHERIT_CLOCKS and d.clock:
                    for k2, v2 in d.clock.items():
                        if known.get(k2, 0) < v2:
                            known[k2] = v2
            inst = o.emit(eng)
            if o.inc:
                h, v = sem_handle(o.val)
                inst.then_inc(h, 16 if o.dma else 1)
                o.clock = dict(known)
                o.clock[(o.val[0], o.val[1])] = o.val[2]
            else:
                o.clock = dict(known)
        if final_drain:
            for (qq, slot), v in self.sems.dcount.items():
                if v > 0:
                    eng.wait_ge(self.sems.dsem[qq][slot], v)
            for qq in QUEUES:
                if self.sems.ccount[qq] > 0 and qq != q:
                    eng.wait_ge(self.sems.csem[qq], self.sems.ccount[qq])

    def run_block(self, nc, drain_queue="sp"):
        self.finalize_counts()
        with nc.Block() as blk:
            @blk.tensor
            def _(eng):
                self.emit_queue("pe", eng, final_drain=(drain_queue == "pe"))
            @blk.scalar
            def _(eng):
                self.emit_queue("act", eng, final_drain=(drain_queue == "act"))
            @blk.vector
            def _(eng):
                self.emit_queue("dve", eng, final_drain=(drain_queue == "dve"))
            @blk.gpsimd
            def _(eng):
                self.emit_queue("pool", eng, final_drain=(drain_queue == "pool"))
            @blk.sync
            def _(eng):
                self.emit_queue("sp", eng, final_drain=(drain_queue == "sp"))



def _bf16(a):
    return np.ascontiguousarray(a).astype(ml_dtypes.bfloat16)


def retnet_consts():
    H = 16
    gamma = np.log(1.0 - 2.0 ** (-5.0 - np.arange(H, dtype=np.float64)))
    c_full = np.zeros((128, H * 32), dtype=np.float32)
    sprime = np.arange(128, dtype=np.float64)
    for h in range(H):
        for dti in range(32):
            dt = dti - 3
            c_full[:, h * 32 + dti] = np.exp(gamma[h] * (128.0 * dt - sprime)).astype(np.float32)
    q = np.arange(T, dtype=np.float64)
    r = np.exp(gamma)
    Ap = np.zeros((H, T), dtype=np.float32)
    for h in range(H):
        partial = np.cumsum(r[h] ** np.arange(T))
        scale = np.sqrt(partial)
        Ap[h] = (np.exp(gamma[h] * (q - 512.0 * (q // 512))) / scale).astype(np.float32)
    A_rep = np.ascontiguousarray(np.broadcast_to(Ap[:, None, :], (H, 128, T))).astype(np.float32)
    SAINV = (np.sqrt(EPS32) / Ap).astype(np.float32)
    SAINV_rep = np.ascontiguousarray(np.broadcast_to(SAINV[:, None, :], (H, 128, T))).astype(np.float32)
    return c_full, A_rep, SAINV_rep, np.ascontiguousarray(Ap), np.ascontiguousarray(SAINV)


def host_prep(inp):
    d = {}
    d["ident"] = np.eye(128, dtype=np.float32)
    d["ones_bf"] = np.ones((128, 128), dtype=ml_dtypes.bfloat16)
    c_full, A_rep, SAINV_rep, A_rows, SAINV_rows = retnet_consts()
    d["c_full"] = c_full
    d["A_rows"] = A_rows
    d["SAINV_rows"] = SAINV_rows
    sp = np.arange(128)[:, None]
    qp = np.arange(512)[None, :]
    d["tri"] = np.concatenate([(128 * r + sp <= qp).astype(np.float32) for r in range(4)], axis=1)
    d["A_rep"] = A_rep
    d["SAINV_rep"] = SAINV_rep
    
    d["Wqk"] = inp["L0_attn_W_qk"]
    d["Wv"] = inp["L0_attn_W_v"]
    d["Wg"] = inp["L0_attn_W_g"]
    d["Wo"] = inp["L0_attn_W_o"]
    d["bqk_fm"] = np.ascontiguousarray(inp["L0_attn_b_qk"].reshape(16, 128).T)
    d["bg_fm"] = np.ascontiguousarray(inp["L0_attn_b_g"].reshape(16, 128).T)
    d["bv_rep"] = np.ascontiguousarray(np.broadcast_to(inp["L0_attn_b_v"], (128, 2048)))
    d["bo_rep"] = np.ascontiguousarray(np.broadcast_to(inp["L0_attn_b_o"], (128, 1024)))
    d["f0_wgT"] = np.ascontiguousarray(inp["L0_ffn_w_gate"].T)
    d["f0_wuT"] = np.ascontiguousarray(inp["L0_ffn_w_up"].T)
    d["f0_wdT"] = np.ascontiguousarray(inp["L0_ffn_w_down"].T)
    d["qkvT"] = np.ascontiguousarray(inp["L1_attn_qkv_w"].T)
    qb = inp["L1_attn_qkv_b"]
    d["qb_fm"] = np.ascontiguousarray(qb[0:1024].reshape(8, 128).T)
    d["kb_rep"] = np.ascontiguousarray(np.broadcast_to(qb[1024:2048], (128, 1024)))
    d["vb_rep"] = np.ascontiguousarray(np.broadcast_to(qb[2048:3072], (128, 1024)))
    d["kng_rep"] = np.ascontiguousarray(np.broadcast_to(np.tile(inp["L1_attn_kn_g"], 16), (128, 1024)))
    d["knb_rep"] = np.ascontiguousarray(np.broadcast_to(np.tile(inp["L1_attn_kn_b"], 16), (128, 1024)))
    d["vng_rep"] = np.ascontiguousarray(np.broadcast_to(np.tile(inp["L1_attn_vn_g"], 16), (128, 1024)))
    d["vnb_rep"] = np.ascontiguousarray(np.broadcast_to(np.tile(inp["L1_attn_vn_b"], 16), (128, 1024)))
    temp = inp["L1_attn_temperature"].reshape(16)
    d["temp_rep"] = np.ascontiguousarray(np.broadcast_to(np.repeat(temp, 64)[None, :], (64, 1024)))
    d["kvbias"] = np.ascontiguousarray(inp["L1_attn_attn_bias"].transpose(1, 0, 2).reshape(64, 1024))
    d["p1T"] = np.ascontiguousarray(inp["L1_attn_p1_w"].T)
    d["p1b_rep"] = np.ascontiguousarray(np.broadcast_to(inp["L1_attn_p1_b"], (128, 1024)))
    d["p2T"] = np.ascontiguousarray(inp["L1_attn_p2_w"].T)
    d["p2b_rep"] = np.ascontiguousarray(np.broadcast_to(inp["L1_attn_p2_b"], (128, 1024)))
    d["gwT"] = np.ascontiguousarray(inp["L1_ffn_gate_w"].T)
    d["gb_rep"] = np.ascontiguousarray(np.broadcast_to(inp["L1_ffn_gate_bias"], (128, 32)))
    d["wgE"] = np.ascontiguousarray(inp["L1_ffn_w_gate"].transpose(0, 2, 1))
    d["wuE"] = np.ascontiguousarray(inp["L1_ffn_w_up"].transpose(0, 2, 1))
    d["wdE"] = np.ascontiguousarray(inp["L1_ffn_w_down"].transpose(0, 2, 1))
    d["shgT"] = np.ascontiguousarray(inp["L1_ffn_sh_gate"].T)
    d["shuT"] = np.ascontiguousarray(inp["L1_ffn_sh_up"].T)
    d["shdT"] = np.ascontiguousarray(inp["L1_ffn_sh_down"].T)
    return d


from contextlib import ExitStack


class Ctx:
    pass


def declare_dram(nc, cx, phases, first, last):
    E = "ExternalInput"
    O = "ExternalOutput"
    I = "Internal"
    D = {}
    def t(name, shape, dtype, kind):
        D[name] = nc.dram_tensor(name, list(shape), dtype, kind=kind).ap()
    t("ident", (128, 128), F32, E)
    if "A" in phases:
        t("ones_bf", (128, 128), BF16, E)
        t("x0", (T, C), F32, E)
        t("c_full", (128, 512), F32, E)
        t("tri", (128, 2048), F32, E)
        t("A_rows", (16, T), F32, E)
        t("SAINV_rows", (16, T), F32, E)
        for n, s in [("Wqk", (C, 2048)), ("Wv", (C, 2048)), ("Wg", (C, 2048)), ("Wo", (2048, C))]:
            t(n, s, F32, E)
        t("bqk_fm", (128, 16), F32, E)
        t("bg_fm", (128, 16), F32, E)
        t("bv_rep", (128, 2048), F32, E)
        t("bo_rep", (128, 1024), F32, E)
        t("s_q", (16, 64, T), BF16, I)
        t("s_k", (16, 64, T), BF16, I)
        t("s_v", (16, T, 128), BF16, I)
        t("s_gated", (16, 128, T), BF16, I)
        t("resid1", (T, C), F32, O if last == "A" else I)
    if "B" in phases:
        if "A" not in phases:
            t("resid1", (T, C), F32, E)
        for n, s in [("f0_wgT", (C, 3072)), ("f0_wuT", (C, 3072)), ("f0_wdT", (3072, C))]:
            t(n, s, I32, E)
        t("resid2", (T, C), F32, O if last == "B" else I)
        if B_DEBUG:
            t("dbg_gu", (128, 24 * 512), BF16, O)
            t("dbg_x", (128, 1024), F32, O)
            t("dbg_a", (128, 1024), F32, O)
            t("dbg_xT", (128, 8 * 4096), BF16, O)
            t("dbg_w", (128, 8 * 128), BF16, O)
    if "C" in phases:
        if "B" not in phases:
            t("resid2", (T, C), F32, E)
        t("qkvT", (C, 3072), F32, E)
        t("qb_fm", (128, 8), F32, E)
        for n in ["kb_rep", "vb_rep", "kng_rep", "knb_rep", "vng_rep", "vnb_rep",
                  "p1b_rep", "p2b_rep"]:
            t(n, (128, 1024), F32, E)
        t("temp_rep", (64, 1024), F32, E)
        t("kvbias", (64, 1024), F32, E)
        t("p1T", (C, C), F32, E)
        t("p2T", (C, C), F32, E)
        t("s_qc", (8, 128, T), BF16, I)
        t("resid3", (T, C), F32, O if last == "C" else I)
    if "D" in phases:
        if "C" not in phases:
            t("resid3", (T, C), F32, E)
        t("gwT", (C, 32), F32, E)
        t("gb_rep", (128, 32), F32, E)
        t("wgE", (32, C, 512), F32, E)
        t("wuE", (32, C, 512), F32, E)
        t("wdE", (32, 512, C), F32, E)
        t("shgT", (C, C), F32, E)
        t("shuT", (C, C), F32, E)
        t("shdT", (C, C), F32, E)
        t("out", (T, C), F32, O)
    cx.D = D


_UID = [0]


def uid():
    _UID[0] += 1
    return _UID[0]


def alloc_psums(nc, st, spec, ms=None):
    ps = {}
    for name, shape in spec:
        ps[name] = st.enter_context(nc.psum_tensor(f"{name}_{uid()}", list(shape), F32))
    if ms is not None:
        ms.psum_keys.update(ps.keys())
    return ps


def sb(nc, st, name, shape, dtype):
    return st.enter_context(nc.sbuf_tensor(f"{name}_{uid()}", list(shape), dtype))


def build_xT(nc, cx, stack, src_dram, ms, ps):
    load_ident(nc, cx, ms)
    xt = [sb(nc, stack, f"a0xt{i}", [128, C], F32) for i in range(2)]
    xTv = cx.xT[:, :].rearrange("p (a b) -> p a b", b=4096)
    for tt in range(NT):
        b = tt % 2
        ms.op("sp", lambda e, b=b, tt=tt: e.dma_start(out=xt[b][:, :], in_=src_dram[tt * 128:(tt + 1) * 128, :]),
              reads=(), writes=(f"xt{b}",), dma=True)
        for half in range(2):
            ph = f"p{half}"
            pt = ps[ph]
            for k in range(4):
                cc = half * 4 + k
                ms.op("pe", lambda e, pt=pt, k=k, b=b, cc=cc: e.matmul(
                    pt[:, k * 128:(k + 1) * 128], xt[b][:, cc * 128:(cc + 1) * 128], cx.identf[:, :],
                    start=(k == 0), stop=(k == 3), is_transpose=True),
                    reads=(f"xt{b}", "ident"), writes=(ph,), accum_key=("tr", tt, half))
            q = "dve" if half == 0 else "act"
            dst = xTv[:, half * 4:half * 4 + 4, tt * 128:(tt + 1) * 128]
            if q == "dve":
                ms.op(q, lambda e, pt=pt, dst=dst: e.tensor_copy(
                    dst, pt[:, :].rearrange("p (a b) -> p a b", b=128)),
                    reads=(ph,), writes=("xT",))
            else:
                ms.op(q, lambda e, pt=pt, dst=dst: e.copy(
                    dst, pt[:, :].rearrange("p (a b) -> p a b", b=128)),
                    reads=(ph,), writes=("xT",))


def phaseA(nc, cx, prog_stack):
    D = cx.D
    sems = cx.sems
    cx.a2_pend = None
    ast_ = prog_stack.enter_context(ExitStack())
    cx.xT = sb(nc, ast_, "a_xT", [128, 8 * 4096], BF16)
    with ExitStack() as st:
        ms = MS(sems)
        ps = alloc_psums(nc, st, [("p0", (128, 512)), ("p1", (128, 512))])
        build_xT(nc, cx, st, D["x0"], ms, ps)
        ms.run_block(nc)
    with ExitStack() as st:
        ms = MS(sems)
        ps = alloc_psums(nc, st, [(f"p{i}", (128, 512)) for i in range(4)], ms)
        wqk = st.enter_context(nc.sbuf_tensor("wqk", [128, 8 * 2048], BF16))
        wv = st.enter_context(nc.sbuf_tensor("wv", [128, 8 * 2048], BF16))
        bqk = st.enter_context(nc.sbuf_tensor("bqk", [128, 16], F32))
        bv = st.enter_context(nc.sbuf_tensor("bv", [128, 2048], F32))
        qkst = [st.enter_context(nc.sbuf_tensor(f"qkst{i}", [128, 512], BF16)) for i in range(4)]
        vst = [st.enter_context(nc.sbuf_tensor(f"vst{i}", [128, 512], BF16)) for i in range(4)]
        ms.op("sp", lambda e: e.dma_start(out=bqk[:, :], in_=D["bqk_fm"][:, :]), writes=("bqk",), dma=True)
        ms.op("sp", lambda e: e.dma_start(out=bv[:, :], in_=D["bv_rep"][:, :]), writes=("bv",), dma=True)
        for cc in range(8):
            ms.op("pool", lambda e, cc=cc: e.dma_start(out=wqk[:, cc * 2048:(cc + 1) * 2048],
                                                       in_=D["Wqk"][cc * 128:(cc + 1) * 128, :]),
                  writes=(f"wqk{cc}",), dma=True)
            ms.op("pool", lambda e, cc=cc: e.dma_start(out=wv[:, cc * 2048:(cc + 1) * 2048],
                                                       in_=D["Wv"][cc * 128:(cc + 1) * 128, :]),
                  writes=(f"wv{cc}",), dma=True)
        for jc in range(16):
            for tgh in range(2):
                for cc in range(8):
                    for tg4 in range(4):
                        tg = tgh * 4 + tg4
                        pt = ps[f"p{tg4}"]
                        ms.op("pe", lambda e, pt=pt, cc=cc, jc=jc, tg=tg: e.matmul(
                            pt[:, :], wqk[:, cc * 2048 + jc * 128: cc * 2048 + (jc + 1) * 128],
                            cx.xT[:, cc * 4096 + tg * 512: cc * 4096 + (tg + 1) * 512],
                            start=(cc == 0), stop=(cc == 7)),
                            reads=(f"wqk{cc}", "xT"), writes=(f"p{tg4}",), accum_key=("qk", jc, tg))
                for tg4 in range(4):
                    tg = tgh * 4 + tg4
                    ph = f"p{tg4}"
                    pt = ps[ph]
                    qsb = qkst[tg4]
                    sbk = f"qkst{tg4}"
                    ms.op("act", lambda e, qsb=qsb, pt=pt, jc=jc: e.activation(
                        qsb[:, :], pt[:, :], AF.Identity, bias=bqk[:, jc:jc + 1]),
                        reads=(ph, "bqk"), writes=(sbk,))
                    tgt = D["s_q"] if jc < 8 else D["s_k"]
                    hh = (jc % 8) * 2
                    ms.op("sp", lambda e, tgt=tgt, hh=hh, tg=tg, qsb=qsb: e.dma_start(
                        out=tgt[hh, :, tg * 512:(tg + 1) * 512], in_=qsb[0:64, :]),
                        reads=(sbk,), writes=(), dma=True)
                    ms.op("sp", lambda e, tgt=tgt, hh=hh, tg=tg, qsb=qsb: e.dma_start(
                        out=tgt[hh + 1, :, tg * 512:(tg + 1) * 512], in_=qsb[64:128, :]),
                        reads=(sbk,), writes=(), dma=True)
        for tt in range(NT):
            for cc in range(8):
                for jg in range(4):
                    pt = ps[f"p{jg}"]
                    ms.op("pe", lambda e, pt=pt, cc=cc, tt=tt, jg=jg: e.matmul(
                        pt[:, :], cx.xT[:, cc * 4096 + tt * 128: cc * 4096 + (tt + 1) * 128],
                        wv[:, cc * 2048 + jg * 512: cc * 2048 + (jg + 1) * 512],
                        start=(cc == 0), stop=(cc == 7)),
                        reads=(f"wv{cc}", "xT"), writes=(f"p{jg}",), accum_key=("v", tt, jg))
            for jg in range(4):
                ph = f"p{jg}"
                pt = ps[ph]
                vsb = vst[jg]
                sbk = f"vst{jg}"
                ms.op("dve", lambda e, vsb=vsb, pt=pt, jg=jg: e.scalar_tensor_tensor(
                    vsb[:, :], pt[:, :], 1.0, bv[:, jg * 512:(jg + 1) * 512], ALU.bypass, ALU.add),
                    reads=(ph, "bv"), writes=(sbk,))
                ms.op("sp", lambda e, vsb=vsb, tt=tt, jg=jg: e.dma_start(
                    out=D["s_v"][jg * 4:(jg + 1) * 4, tt * 128:(tt + 1) * 128, :].rearrange("h p v -> p h v"),
                    in_=vsb[:, :].rearrange("p (h v) -> p h v", v=128)),
                    reads=(sbk,), writes=(), dma=True)
        ms.run_block(nc)
    with ExitStack() as st:
        ms = MS(sems)
        ps = alloc_psums(nc, st, [("p0", (128, 512)), ("p1", (128, 512)), ("p2", (128, 512)),
                                  ("p3", (128, 512)),
                                  ("psret0", (128, 512)), ("psret1", (128, 512)),
                                  ("psden", (1, 512)), ("psrms", (128, 512))], ms)
        wg = st.enter_context(nc.sbuf_tensor("wg", [128, 8 * 2048], BF16))
        bg = st.enter_context(nc.sbuf_tensor("bg", [128, 16], F32))
        cfu = st.enter_context(nc.sbuf_tensor("cfu", [128, 512], F32))
        trisb = st.enter_context(nc.sbuf_tensor("trisb", [128, 2048], F32))
        onesb = st.enter_context(nc.sbuf_tensor("onesb", [128, 1], BF16))
        onesf = st.enter_context(nc.sbuf_tensor("onesf", [1, 128], F32))
        qTh = [st.enter_context(nc.sbuf_tensor(f"qTh{i}", [64, T], BF16)) for i in range(2)]
        kTh = [st.enter_context(nc.sbuf_tensor(f"kTh{i}", [64, T], BF16)) for i in range(2)]
        vh = [st.enter_context(nc.sbuf_tensor(f"vh{i}", [128, T], BF16)) for i in range(2)]
        gTh = st.enter_context(nc.sbuf_tensor("gTh", [128, T], BF16))
        msst = [st.enter_context(nc.sbuf_tensor(f"msst{i}", [128, 512], BF16)) for i in range(4)]
        absst = [st.enter_context(nc.sbuf_tensor(f"absst{i}", [128, 512], BF16)) for i in range(4)]
        sqst = st.enter_context(nc.sbuf_tensor("sqst", [128, 512], BF16))
        arow = st.enter_context(nc.sbuf_tensor("arow", [1, 512], F32))
        sarow = st.enter_context(nc.sbuf_tensor("sarow", [1, 512], F32))
        denrow = st.enter_context(nc.sbuf_tensor("denrow", [1, 512], F32))
        rmsrow = st.enter_context(nc.sbuf_tensor("rmsrow", [1, 512], F32))
        dcrow = st.enter_context(nc.sbuf_tensor("dcrow", [1, 512], F32))
        w2row = st.enter_context(nc.sbuf_tensor("w2row", [1, 512], F32))
        frow = st.enter_context(nc.sbuf_tensor("frow", [1, 512], F32))
        fsb = st.enter_context(nc.sbuf_tensor("fsb", [128, 512], F32))
        y1sb = st.enter_context(nc.sbuf_tensor("y1sb", [128, 512], F32))
        gst = st.enter_context(nc.sbuf_tensor("gst", [128, 512], BF16))
        ms.op("sp", lambda e: e.dma_start(out=bg[:, :], in_=D["bg_fm"][:, :]), writes=("bg",), dma=True)
        ms.op("sp", lambda e: e.dma_start(out=cfu[:, :], in_=D["c_full"][:, :]), writes=("cfu",), dma=True)
        ms.op("sp", lambda e: e.dma_start(out=trisb[:, :], in_=D["tri"][:, :]), writes=("tri",), dma=True)
        ms.op("pool", lambda e: e.memset(onesb[:, :], 1.0), writes=("onesb",))
        ms.op("pool", lambda e: e.memset(onesf[:, :], 1.0), writes=("onesf",))
        for cc in range(8):
            ms.op("pool", lambda e, cc=cc: e.dma_start(out=wg[:, cc * 2048:(cc + 1) * 2048],
                                                       in_=D["Wg"][cc * 128:(cc + 1) * 128, :]),
                  writes=(f"wg{cc}",), dma=True)
        for h in range(16):
            hb = h % 2
            if cx.a2_pend is not None:
                cx.a2_pend()
                cx.a2_pend = None
            ms.op("sp", lambda e, h=h, hb=hb: e.dma_start(out=qTh[hb][:, :], in_=D["s_q"][h, :, :]),
                  reads=(), writes=(f"qTh{hb}",), dma=True)
            ms.op("sp", lambda e, h=h, hb=hb: e.dma_start(out=kTh[hb][:, :], in_=D["s_k"][h, :, :]),
                  reads=(), writes=(f"kTh{hb}",), dma=True)
            for part in range(4):
                ms.op("sp", lambda e, h=h, hb=hb, part=part: e.dma_start(
                    out=vh[hb][:, :].rearrange("p (a v) -> p a v", v=128)[:, part * 8:(part + 1) * 8, :],
                    in_=D["s_v"][h, part * 1024:(part + 1) * 1024, :].rearrange("(a p) v -> p a v", p=128)),
                    reads=(), writes=(f"vh{hb}",), dma=True)
            for tgp in range(4):
                for cc in range(8):
                    for t2 in range(2):
                        tg = tgp * 2 + t2
                        pt = ps[f"p{t2}"]
                        ms.op("pe", lambda e, pt=pt, cc=cc, h=h, tg=tg: e.matmul(
                            pt[:, :], wg[:, cc * 2048 + h * 128: cc * 2048 + (h + 1) * 128],
                            cx.xT[:, cc * 4096 + tg * 512: cc * 4096 + (tg + 1) * 512],
                            start=(cc == 0), stop=(cc == 7)),
                            reads=(f"wg{cc}", "xT"), writes=(f"p{t2}",), accum_key=("g", h, tg))
                for t2 in range(2):
                    tg = tgp * 2 + t2
                    ph = f"p{t2}"
                    pt = ps[ph]
                    ms.op("act", lambda e, pt=pt, h=h, tg=tg: e.activation(
                        gTh[:, tg * 512:(tg + 1) * 512], pt[:, :], AF.Identity, bias=bg[:, h:h + 1]),
                        reads=(ph, "bg"), writes=("gTh",))
            for qg in range(NG):
                rk = f"psret{qg % 2}"
                ms.op("sp", lambda e, h=h, qg=qg: e.dma_start(
                    out=arow[:, :], in_=D["A_rows"][h:h + 1, qg * 512:(qg + 1) * 512]),
                    writes=("arow",), dma=True)
                ms.op("sp", lambda e, h=h, qg=qg: e.dma_start(
                    out=sarow[:, :], in_=D["SAINV_rows"][h:h + 1, qg * 512:(qg + 1) * 512]),
                    writes=("sarow",), dma=True)
                nkt = 4 * qg + 4

                def sc_emit(kt, h=h, qg=qg, hb=hb):
                    pn = f"p{kt % 4}"
                    pt = ps[pn]
                    ms.op("pe", lambda e, pt=pt, kt=kt, qg=qg, hb=hb: e.matmul(
                        pt[:, :], kTh[hb][:, kt * 128:(kt + 1) * 128],
                        qTh[hb][:, qg * 512:(qg + 1) * 512], start=True, stop=True),
                        reads=(f"kTh{hb}", f"qTh{hb}"), writes=(pn,))

                def ret_emit(kt, rk=rk, h=h, qg=qg, hb=hb, nkt=nkt):
                    kb = kt % 4
                    ms.op("pe", lambda e, rk=rk, kb=kb, hb=hb, kt=kt, nkt=nkt: e.matmul(
                        ps[rk][:, :], vh[hb][:, kt * 128:(kt + 1) * 128], msst[kb][:, :],
                        start=(kt == 0), stop=(kt == nkt - 1)),
                        reads=(f"vh{hb}", f"msst{kb}"), writes=(rk,), accum_key=("ret", h, qg))

                def den_emit(kt, h=h, qg=qg, nkt=nkt):
                    kb = kt % 4
                    ms.op("pe", lambda e, kb=kb, kt=kt, nkt=nkt: e.matmul(
                        ps["psden"][:, :], onesb[:, 0:1], absst[kb][:, :],
                        start=(kt == 0), stop=(kt == nkt - 1)),
                        reads=("onesb", f"absst{kb}"), writes=("psden",), accum_key=("den", h, qg))

                for kt in range(min(4, nkt)):
                    sc_emit(kt)
                for kt in range(nkt):
                    kb = kt % 4
                    pn = f"p{kb}"
                    pt = ps[pn]
                    didx = 4 * qg - kt + 3
                    r = kt - 4 * qg
                    if r >= 0:
                        ms.op("dve", lambda e, kb=kb, pt=pt, h=h, didx=didx, r=r: e.scalar_tensor_tensor(
                            msst[kb][:, :], pt[:, :], cfu[:, h * 32 + didx: h * 32 + didx + 1],
                            trisb[:, r * 512:(r + 1) * 512], ALU.mult, ALU.mult),
                            reads=(pn, "cfu", "tri"), writes=(f"msst{kb}",))
                    else:
                        ms.op("dve", lambda e, kb=kb, pt=pt, h=h, didx=didx: e.tensor_scalar(
                            msst[kb][:, :], pt[:, :], cfu[:, h * 32 + didx: h * 32 + didx + 1],
                            None, ALU.mult),
                            reads=(pn, "cfu"), writes=(f"msst{kb}",))
                    ms.op("act", lambda e, kb=kb: e.activation(absst[kb][:, :], msst[kb][:, :], AF.Abs),
                          reads=(f"msst{kb}",), writes=(f"absst{kb}",))
                    if kt + 4 < nkt:
                        sc_emit(kt + 4)
                    if kt >= 2:
                        ret_emit(kt - 2)
                    if kt >= 3:
                        den_emit(kt - 3)
                    if kt == 2 and cx.a2_pend is not None:
                        cx.a2_pend()
                        cx.a2_pend = None
                ret_emit(nkt - 2)
                ret_emit(nkt - 1)
                den_emit(nkt - 3)
                den_emit(nkt - 2)
                den_emit(nkt - 1)
                ms.op("act", lambda e, rk=rk: e.activation(sqst[:, :], ps[rk][:, :], AF.Square),
                      reads=(rk,), writes=("sqst",))
                ms.op("pe", lambda e: e.matmul(ps["psrms"][0:1, :], onesb[:, 0:1], sqst[:, :],
                                               start=True, stop=True),
                      reads=("onesb", "sqst"), writes=("psrms",))
                ms.op("dve", lambda e: e.tensor_copy(denrow[:, :], ps["psden"][:, :]),
                      reads=("psden",), writes=("denrow",))
                ms.op("dve", lambda e: e.tensor_scalar(rmsrow[:, :], ps["psrms"][0:1, :],
                                                       1.0 / 128.0, None, ALU.mult),
                      reads=("psrms",), writes=("rmsrow",))
                ms.op("dve", lambda e: e.tensor_tensor(dcrow[:, :], denrow[:, :], arow[:, :], ALU.mult),
                      reads=("denrow", "arow"), writes=("dcrow",))
                ms.op("dve", lambda e: e.tensor_scalar(dcrow[:, :], dcrow[:, :], 1.0, 5e4, ALU.max, ALU.min),
                      reads=("dcrow",), writes=("dcrow",))
                ms.op("dve", lambda e: e.tensor_tensor(dcrow[:, :], dcrow[:, :], sarow[:, :], ALU.mult),
                      reads=("dcrow", "sarow"), writes=("dcrow",))
                ms.op("act", lambda e: e.activation(w2row[:, :], dcrow[:, :], AF.Square),
                      reads=("dcrow",), writes=("w2row",))
                ms.op("dve", lambda e: e.tensor_tensor(frow[:, :], w2row[:, :], rmsrow[:, :], ALU.add),
                      reads=("w2row", "rmsrow"), writes=("frow",))
                ms.op("act", lambda e: e.activation(frow[:, :], frow[:, :], AF.Ln),
                      reads=("frow",), writes=("frow",))
                ms.op("act", lambda e: e.activation(frow[:, :], frow[:, :], AF.Exp, scale=-0.5),
                      reads=("frow",), writes=("frow",))
                def a2_tail(rk=rk, qg=qg, h=h):
                    ms.op("pe", lambda e: e.matmul(ps["psrms"][:, :], onesf[0:1, :], frow[:, :],
                                                   start=True, stop=True),
                          reads=("onesf", "frow"), writes=("psrms",))
                    ms.op("dve", lambda e: e.tensor_copy(fsb[:, :], ps["psrms"][:, :]),
                          reads=("psrms",), writes=("fsb",))
                    ms.op("dve", lambda e, rk=rk: e.tensor_tensor(y1sb[:, :], ps[rk][:, :], fsb[:, :], ALU.mult),
                          reads=(rk, "fsb"), writes=("y1sb",))
                    ms.op("dve", lambda e, qg=qg: e.tensor_tensor(gst[:, :], y1sb[:, :],
                                                                  gTh[:, qg * 512:(qg + 1) * 512], ALU.mult),
                          reads=("y1sb", "gTh"), writes=("gst",))
                    ms.op("sp", lambda e, h=h, qg=qg: e.dma_start(
                        out=D["s_gated"][h, :, qg * 512:(qg + 1) * 512], in_=gst[:, :]),
                        reads=("gst",), writes=(), dma=True)
                cx.a2_pend = a2_tail
        if cx.a2_pend is not None:
            cx.a2_pend()
            cx.a2_pend = None
        ms.run_block(nc)
    with ExitStack() as st:
        ms = MS(sems)
        emit_outproj_ln(nc, cx, st, ms,
                        n_jc=16, wo_dram=D["Wo"], wo_cols=1024, gsrc=D["s_gated"],
                        bias_rep=D["bo_rep"], res_dram=D["x0"], out_dram=D["resid1"],
                        scale=1.0)
        ms.run_block(nc)
    ast_.close()


def emit_outproj_ln(nc, cx, st, ms, n_jc, wo_dram, wo_cols, gsrc, bias_rep, res_dram,
                    out_dram, scale):
    D = cx.D
    ps = alloc_psums(nc, st, [("po0", (128, 512)), ("po1", (128, 512))], ms)
    wo = st.enter_context(nc.sbuf_tensor("wo_a3", [128, n_jc * wo_cols], BF16))
    brep = st.enter_context(nc.sbuf_tensor("brep_a3", [128, wo_cols], F32))
    gin = [st.enter_context(nc.sbuf_tensor(f"gin{i}", [128, n_jc * 128], BF16)) for i in range(2)]
    xres = [st.enter_context(nc.sbuf_tensor(f"xres{i}", [128, wo_cols], F32)) for i in range(2)]
    xb = [st.enter_context(nc.sbuf_tensor(f"xb{i}", [128, wo_cols], F32)) for i in range(2)]
    asb = [st.enter_context(nc.sbuf_tensor(f"asb{i}", [128, wo_cols], F32)) for i in range(2)]
    sums = st.enter_context(nc.sbuf_tensor("sums_a3", [128, 8], F32))
    xcsb = [st.enter_context(nc.sbuf_tensor(f"xcsb{i}", [128, wo_cols], F32)) for i in range(2)]
    sqtr = st.enter_context(nc.sbuf_tensor("sqtr_a3", [128, wo_cols], F32))
    x1st = [st.enter_context(nc.sbuf_tensor(f"x1st{i}", [128, wo_cols], F32)) for i in range(2)]
    ms.op("sp", lambda e: e.dma_start(out=brep[:, :], in_=bias_rep[:, :]), writes=("brep",), dma=True)
    for jc in range(n_jc):
        ms.op("pool", lambda e, jc=jc: e.dma_start(out=wo[:, jc * wo_cols:(jc + 1) * wo_cols],
                                                   in_=wo_dram[jc * 128:(jc + 1) * 128, :]),
              writes=(f"wo{jc}",), dma=True)
    nh = wo_cols // 512
    for tt in range(NT):
        b = tt % 2
        for part in range(4):
            npart = n_jc // 4
            ms.op("sp", lambda e, b=b, tt=tt, part=part, npart=npart: e.dma_start(
                out=gin[b][:, :].rearrange("p (a t) -> p a t", t=128)[:, part * npart:(part + 1) * npart, :],
                in_=gsrc[part * npart:(part + 1) * npart, :, tt * 128:(tt + 1) * 128].rearrange("a p t -> p a t")),
                writes=(f"gin{b}",), dma=True)
        ms.op("sp", lambda e, b=b, tt=tt: e.dma_start(out=xres[b][:, :],
                                                      in_=res_dram[tt * 128:(tt + 1) * 128, :]),
              writes=(f"xres{b}",), dma=True)
        ms.op("pool", lambda e, b=b: e.tensor_tensor(xb[b][:, :], xres[b][:, :], brep[:, :], ALU.add),
              reads=(f"xres{b}", "brep"), writes=(f"xb{b}",))
        for jc in range(n_jc):
            for half in range(nh):
                pt = ps[f"po{half}"]
                ms.op("pe", lambda e, pt=pt, jc=jc, b=b, half=half, n_jc=n_jc: e.matmul(
                    pt[:, :], gin[b][:, jc * 128:(jc + 1) * 128],
                    wo[:, jc * wo_cols + half * 512: jc * wo_cols + half * 512 + 512],
                    start=(jc == 0), stop=(jc == n_jc - 1)),
                    reads=(f"gin{b}", f"wo{jc}"), writes=(f"po{half}",), accum_key=("o", tt, half))
        for half in range(nh):
            ph = f"po{half}"
            pt = ps[ph]
            ms.op("dve", lambda e, pt=pt, b=b, half=half: e.scalar_tensor_tensor(
                asb[b][:, half * 512:(half + 1) * 512], pt[:, :], scale,
                xb[b][:, half * 512:(half + 1) * 512], ALU.mult, ALU.add,
                accum_out=sums[:, b * 4 + half: b * 4 + half + 1]),
                reads=(ph, f"xb{b}"), writes=(f"asb{b}", f"sums{b}"))
        ms.op("dve", lambda e, b=b, nh=nh: e.tensor_reduce(
            sums[:, b * 4 + 3: b * 4 + 4], sums[:, b * 4: b * 4 + nh], axis=AX.X, op=ALU.add)
            if nh > 2 else e.tensor_tensor(sums[:, b * 4 + 3: b * 4 + 4], sums[:, b * 4: b * 4 + 1],
                                           sums[:, b * 4 + 1: b * 4 + 2], ALU.add),
            reads=(f"sums{b}",), writes=(f"msum{b}",))
        ms.op("dve", lambda e, b=b: e.tensor_scalar(sums[:, b * 4 + 3: b * 4 + 4],
                                                    sums[:, b * 4 + 3: b * 4 + 4],
                                                    1.0 / wo_cols, None, ALU.mult),
              reads=(f"msum{b}",), writes=(f"msum{b}",))
        ms.op("dve", lambda e, b=b: e.tensor_scalar(xcsb[b][:, :], asb[b][:, :],
                                                    sums[:, b * 4 + 3: b * 4 + 4], None, ALU.subtract),
              reads=(f"asb{b}", f"msum{b}"), writes=(f"xcsb{b}",))
        ms.op("act", lambda e, b=b: e.activation(sqtr[:, :], xcsb[b][:, :], AF.Square,
                                                 accum_out=sums[:, b * 4 + 2: b * 4 + 3]),
              reads=(f"xcsb{b}",), writes=("sqtr", f"ssq{b}"))
        ms.op("dve", lambda e, b=b: e.tensor_scalar(sums[:, b * 4 + 2: b * 4 + 3],
                                                    sums[:, b * 4 + 2: b * 4 + 3],
                                                    1.0 / wo_cols, 1e-5, ALU.mult, ALU.add),
              reads=(f"ssq{b}",), writes=(f"ssq{b}",))
        ms.op("act", lambda e, b=b: e.activation(sums[:, b * 4 + 2: b * 4 + 3],
                                                 sums[:, b * 4 + 2: b * 4 + 3], AF.Ln),
              reads=(f"ssq{b}",), writes=(f"ssq{b}",))
        ms.op("act", lambda e, b=b: e.activation(sums[:, b * 4 + 2: b * 4 + 3],
                                                 sums[:, b * 4 + 2: b * 4 + 3], AF.Exp, scale=-0.5),
              reads=(f"ssq{b}",), writes=(f"ssq{b}",))
        ms.op("dve", lambda e, b=b: e.tensor_scalar(x1st[b][:, :], xcsb[b][:, :],
                                                    sums[:, b * 4 + 2: b * 4 + 3], None, ALU.mult),
              reads=(f"xcsb{b}", f"ssq{b}"), writes=(f"x1st{b}",))
        ms.op("sp", lambda e, b=b, tt=tt: e.dma_start(out=out_dram[tt * 128:(tt + 1) * 128, :],
                                                      in_=x1st[b][:, :]),
              reads=(f"x1st{b}",), writes=(), dma=True)



PHASE_FNS = {}


def build_program(phases):
    nc = bass.Bass("TRN2", target_bir_lowering=False)
    cx = Ctx()
    cx.ident_loaded = False
    st = ExitStack()
    with st:
        cx.sems = SemState(nc, st)
        declare_dram(nc, cx, phases, phases[0], phases[-1])
        cx.sems.emit_clear(nc)
        cx.identf = st.enter_context(nc.sbuf_tensor("identf", [128, 128], F32))
        for i, p in enumerate(phases):
            if i > 0:
                cx.sems.emit_clear(nc)
            PHASE_FNS[p](nc, cx, st)
    nc.finalize()
    return nc, cx


def load_ident(nc, cx, ms):
    if not cx.ident_loaded:
        D = cx.D
        ms.op("sp", lambda e: e.dma_start(out=cx.identf[:, :], in_=D["ident"][:, :]),
              writes=("ident",), dma=True)
        cx.ident_loaded = True


IN_KEYS = {
    "A": ["ident", "ones_bf", "c_full", "tri", "A_rows", "SAINV_rows", "Wqk", "Wv", "Wg", "Wo",
          "bqk_fm", "bg_fm", "bv_rep", "bo_rep"],
    "B": ["ident", "f0_wgT", "f0_wuT", "f0_wdT"],
    "C": ["ident", "qkvT", "qb_fm", "kb_rep", "vb_rep", "kng_rep", "knb_rep",
          "vng_rep", "vnb_rep", "temp_rep", "kvbias", "p1T", "p1b_rep", "p2T", "p2b_rep"],
    "D": ["ident", "gwT", "gb_rep", "wgE", "wuE", "wdE", "shgT", "shuT", "shdT"],
}
RESID_IN = {"A": "x0", "B": "resid1", "C": "resid2", "D": "resid3"}
RESID_OUT = {"A": "resid1", "B": "resid2", "C": "resid3", "D": "out"}


def run_phases(phases, prep, resid_per_core, n_cores=8):
    nc, cx = build_program(phases)
    keys = set()
    for p in phases:
        keys.update(IN_KEYS[p])
    base = {k: prep[k] for k in sorted(keys) if k in prep}
    in_maps = []
    for b in range(n_cores):
        m = dict(base)
        m[RESID_IN[phases[0]]] = resid_per_core[b]
        in_maps.append(m)
    res = run_bass_kernel_spmd(nc, in_maps, core_ids=list(range(n_cores)))
    okey = RESID_OUT[phases[-1]]
    return [res.results[b][okey] for b in range(n_cores)]


def kernel(**inputs):
    prep = host_prep(inputs)
    x = np.ascontiguousarray(inputs["L0_attn_x"].astype(np.float32, copy=False))
    n_cores = 8
    resid = [np.ascontiguousarray(x[b]) for b in range(n_cores)]
    for group in PHASE_GROUPS:
        resid = run_phases(group, prep, resid, n_cores)
    return np.stack([np.asarray(r, dtype=np.float32) for r in resid], axis=0)


PHASE_GROUPS = [["A", "B", "C", "D"]]
PHASE_FNS["A"] = phaseA


class LNEmit:
    def __init__(self, nc, st, tag):
        self.asb = [sb(nc, st, f"{tag}asb{i}", [128, 1024], F32) for i in range(2)]
        self.sums = sb(nc, st, f"{tag}sums", [128, 8], F32)
        self.sqtr = sb(nc, st, f"{tag}sqtr", [128, 1024], F32)
        self.tag = tag

    def evac_half(self, ms, b, half, src, phkey, xb, xbkey, scale):
        t = self.tag
        ms.op("dve", lambda e, src=src, b=b, half=half: e.scalar_tensor_tensor(
            self.asb[b][:, half * 512:(half + 1) * 512], src, scale,
            xb[:, half * 512:(half + 1) * 512], ALU.mult, ALU.add,
            accum_out=self.sums[:, b * 4 + half: b * 4 + half + 1]),
            reads=(phkey, xbkey), writes=(f"{t}asb{b}", f"{t}sums{b}"))

    def ln(self, ms, b, out_dram, tt):
        t = self.tag
        ms.op("dve", lambda e, b=b: e.tensor_tensor(
            self.sums[:, b * 4 + 3: b * 4 + 4], self.sums[:, b * 4: b * 4 + 1],
            self.sums[:, b * 4 + 1: b * 4 + 2], ALU.add),
            reads=(f"{t}sums{b}",), writes=(f"{t}msum{b}",))
        ms.op("dve", lambda e, b=b: e.tensor_scalar(
            self.sums[:, b * 4 + 3: b * 4 + 4], self.sums[:, b * 4 + 3: b * 4 + 4],
            1.0 / 1024.0, None, ALU.mult),
            reads=(f"{t}msum{b}",), writes=(f"{t}msum{b}",))
        ms.op("dve", lambda e, b=b: e.tensor_scalar(
            self.asb[b][:, :], self.asb[b][:, :], self.sums[:, b * 4 + 3: b * 4 + 4],
            None, ALU.subtract),
            reads=(f"{t}asb{b}", f"{t}msum{b}"), writes=(f"{t}asb{b}",))
        ms.op("act", lambda e, b=b: e.activation(self.sqtr[:, :], self.asb[b][:, :], AF.Square,
                                                 accum_out=self.sums[:, b * 4 + 2: b * 4 + 3]),
              reads=(f"{t}asb{b}",), writes=(f"{t}sqtr", f"{t}ssq{b}"))
        ms.op("dve", lambda e, b=b: e.tensor_scalar(
            self.sums[:, b * 4 + 2: b * 4 + 3], self.sums[:, b * 4 + 2: b * 4 + 3],
            1.0 / 1024.0, 1e-5, ALU.mult, ALU.add),
            reads=(f"{t}ssq{b}",), writes=(f"{t}ssq{b}",))
        ms.op("act", lambda e, b=b: e.activation(self.sums[:, b * 4 + 2: b * 4 + 3],
                                                 self.sums[:, b * 4 + 2: b * 4 + 3], AF.Ln),
              reads=(f"{t}ssq{b}",), writes=(f"{t}ssq{b}",))
        ms.op("act", lambda e, b=b: e.activation(self.sums[:, b * 4 + 2: b * 4 + 3],
                                                 self.sums[:, b * 4 + 2: b * 4 + 3], AF.Exp, scale=-0.5),
              reads=(f"{t}ssq{b}",), writes=(f"{t}ssq{b}",))
        ms.op("dve", lambda e, b=b: e.tensor_scalar(
            self.asb[b][:, :], self.asb[b][:, :], self.sums[:, b * 4 + 2: b * 4 + 3],
            None, ALU.mult),
            reads=(f"{t}asb{b}", f"{t}ssq{b}"), writes=(f"{t}asb{b}",))
        ms.op("act", lambda e, b=b, tt=tt: e.dma_start(out=out_dram[tt * 128:(tt + 1) * 128, :],
                                                       in_=self.asb[b][:, :]),
              reads=(f"{t}asb{b}",), writes=(), dma=True)


B_SKIP_LN = False
B_NG = NG
B_DEBUG = False


def phaseB(nc, cx, prog_stack):
    D = cx.D
    sems = cx.sems
    bst_ = prog_stack.enter_context(ExitStack())
    cx.xT = sb(nc, bst_, "b_xT", [128, 8 * 4096], BF16)
    with ExitStack() as st:
        ms = MS(sems)
        ps = alloc_psums(nc, st, [("p0", (128, 512)), ("p1", (128, 512))])
        build_xT(nc, cx, st, D["resid1"], ms, ps)
        ms.run_block(nc)
    with ExitStack() as st:
        ms = MS(sems)
        ps = alloc_psums(nc, st, [(f"pg{i}", (128, 512)) for i in range(4)] +
                                 [(f"pd{i}", (128, 512)) for i in range(4)], ms)
        wd = sb(nc, st, "f0wd", [128, 24 * 1024], BF16)
        wgc = [sb(nc, st, f"f0wg{i}", [128, 8 * 128], BF16) for i in range(4)]
        wuc = [sb(nc, st, f"f0wu{i}", [128, 8 * 128], BF16) for i in range(4)]
        gu = sb(nc, st, "f0gu", [128, 24 * 512], BF16)
        s1 = [sb(nc, st, f"f0s1{i}", [128, 512], F32) for i in range(2)]
        xres = [sb(nc, st, f"f0x{i}", [128, 1024], F32) for i in range(2)]
        lne = LNEmit(nc, st, "bln")
        for hc in range(24):
            ms.op("pool", lambda e, hc=hc: e.dma_start(out=wd[:, hc * 1024:(hc + 1) * 1024],
                                                       in_=D["f0_wdT"][hc * 128:(hc + 1) * 128, :]),
                  writes=(f"wd{hc}",), dma=True)
        for tg in range(B_NG):
            for hc in range(24):
                wb = hc % 4
                ms.op("pool", lambda e, hc=hc, wb=wb: e.dma_start(
                    out=wgc[wb][:, :].rearrange("p (a b) -> p a b", b=128),
                    in_=D["f0_wgT"][:, hc * 128:(hc + 1) * 128].rearrange("(a p) h -> p a h", p=128)),
                    writes=(f"wgc{wb}",), dma=True)
                ms.op("pool", lambda e, hc=hc, wb=wb: e.dma_start(
                    out=wuc[wb][:, :].rearrange("p (a b) -> p a b", b=128),
                    in_=D["f0_wuT"][:, hc * 128:(hc + 1) * 128].rearrange("(a p) h -> p a h", p=128)),
                    writes=(f"wuc{wb}",), dma=True)
                pgk = f"pg{hc % 2}"
                puk = f"pg{2 + hc % 2}"
                pg = ps[pgk]; pu = ps[puk]
                for cc in range(8):
                    ms.op("pe", lambda e, pg=pg, cc=cc, wb=wb, tg=tg: e.matmul(
                        pg[:, :], wgc[wb][:, cc * 128:(cc + 1) * 128],
                        cx.xT[:, cc * 4096 + tg * 512: cc * 4096 + (tg + 1) * 512],
                        start=(cc == 0), stop=(cc == 7)),
                        reads=(f"wgc{wb}", "xT"), writes=(pgk,), accum_key=("g", hc, tg))
                    ms.op("pe", lambda e, pu=pu, cc=cc, wb=wb, tg=tg: e.matmul(
                        pu[:, :], wuc[wb][:, cc * 128:(cc + 1) * 128],
                        cx.xT[:, cc * 4096 + tg * 512: cc * 4096 + (tg + 1) * 512],
                        start=(cc == 0), stop=(cc == 7)),
                        reads=(f"wuc{wb}", "xT"), writes=(puk,), accum_key=("u", hc, tg))
                s1b = s1[hc % 2]
                ms.op("act", lambda e, s1b=s1b, pg=pg: e.activation(
                    s1b[:, :], pg[:, :], AF.Silu, scale=1.0 / 127.0),
                    reads=(pgk,), writes=(f"s1{hc % 2}",))
                ms.op("dve", lambda e, pu=pu, s1b=s1b, hc=hc: e.scalar_tensor_tensor(
                    gu[:, hc * 512:(hc + 1) * 512],
                    pu[:, :], 1.0 / 127.0, s1b[:, :], ALU.mult, ALU.mult),
                    reads=(puk, f"s1{hc % 2}"), writes=(f"gu{hc}",))
            for t4 in range(4):
                tt = tg * 4 + t4
                b = t4 % 2
                ms.op("sp", lambda e, b=b, tt=tt: e.dma_start(
                    out=xres[b][:, :], in_=D["resid1"][tt * 128:(tt + 1) * 128, :]),
                    writes=(f"f0x{b}",), dma=True)
                for hc in range(24):
                    for half in range(2):
                        pdk = f"pd{(t4 % 2) * 2 + half}"
                        pd = ps[pdk]
                        ms.op("pe", lambda e, pd=pd, hc=hc, t4=t4, half=half: e.matmul(
                            pd[:, :], gu[:, hc * 512 + t4 * 128: hc * 512 + (t4 + 1) * 128],
                            wd[:, hc * 1024 + half * 512: hc * 1024 + half * 512 + 512],
                            start=(hc == 0), stop=(hc == 23)),
                            reads=(f"gu{hc}", f"wd{hc}"), writes=(pdk,),
                            accum_key=("d", tt, half))
                for half in range(2):
                    pdk = f"pd{(t4 % 2) * 2 + half}"
                    pd = ps[pdk]
                    lne.evac_half(ms, b, half, pd[:, :], pdk, xres[b], f"f0x{b}", 1.0 / 127.0)
                if B_SKIP_LN:
                    ms.op("sp", lambda e, b=b, tt=tt: e.dma_start(
                        out=D["resid2"][tt * 128:(tt + 1) * 128, :], in_=lne.asb[b][:, :]),
                        reads=(f"blnasb{b}",), writes=(), dma=True)
                else:
                    lne.ln(ms, b, D["resid2"], tt)
                if B_DEBUG and tt == 0:
                    ms.op("sp", lambda e: e.dma_start(out=D["dbg_gu"][:, :], in_=gu[:, :]),
                          reads=tuple(f"gu{h}" for h in range(24)), writes=(), dma=True)
                    ms.op("sp", lambda e: e.dma_start(out=D["dbg_xT"][:, :], in_=cx.xT[:, :]),
                          reads=("xT",), writes=(), dma=True)
                    ms.op("sp", lambda e: e.dma_start(out=D["dbg_w"][:, :], in_=wgc[1][:, :]),
                          reads=("wgc1",), writes=(), dma=True)
                    ms.op("sp", lambda e, b=b: e.dma_start(out=D["dbg_x"][:, :], in_=xres[b][:, :]),
                          reads=(f"f0x{b}",), writes=(), dma=True)
                    ms.op("sp", lambda e, b=b: e.dma_start(out=D["dbg_a"][:, :], in_=lne.asb[b][:, :]),
                          reads=(f"blnasb{b}",), writes=(), dma=True)
        ms.run_block(nc)
    bst_.close()


PHASE_FNS["B"] = phaseB


def phaseC(nc, cx, prog_stack):
    D = cx.D
    sems = cx.sems
    with ExitStack() as cst:
        kvpair = sb(nc, cst, "c_kvpair", [128, 8 * 128], BF16)
        with ExitStack() as st:
            cx.xT = sb(nc, st, "c_xT", [128, 8 * 4096], BF16)
            with ExitStack() as st0:
                ms = MS(sems)
                psx = alloc_psums(nc, st0, [("p0", (128, 512)), ("p1", (128, 512))], ms)
                build_xT(nc, cx, st0, D["resid2"], ms, psx)
                ms.run_block(nc)
            ms = MS(sems)
            ps = alloc_psums(nc, st, [("pk0", (128, 512)), ("pk1", (128, 512)),
                                      ("pv0", (128, 512)), ("pv1", (128, 512)),
                                      ("pkv0", (64, 512)), ("pkv1", (64, 512)),
                                      ("pq0", (128, 512)), ("pq1", (128, 512))], ms)
            wq = sb(nc, st, "c_wq", [128, 8 * 1024], BF16)
            wkv = sb(nc, st, "c_wkv", [128, 8 * 2048], BF16)
            qb = sb(nc, st, "c_qb", [128, 8], F32)
            reps = {}
            for nme, shape in [("kb_rep", (128, 1024)), ("vb_rep", (128, 1024)),
                               ("kng_rep", (128, 1024)), ("knb_rep", (128, 1024)),
                               ("vng_rep", (128, 1024)), ("vnb_rep", (128, 1024)),
                               ("temp_rep", (64, 1024)), ("kvbias", (64, 1024))]:
                reps[nme] = sb(nc, st, "c_" + nme, list(shape), F32)
                ms.op("sp", lambda e, nme=nme: e.dma_start(out=reps[nme][:, :], in_=D[nme][:, :]),
                      writes=(nme,), dma=True)
            ms.op("sp", lambda e: e.dma_start(out=qb[:, :], in_=D["qb_fm"][:, :]),
                  writes=("qb_fm",), dma=True)
            kraw = [sb(nc, st, f"c_kraw{i}", [128, 1024], F32) for i in range(2)]
            vraw = [sb(nc, st, f"c_vraw{i}", [128, 1024], F32) for i in range(2)]
            stats = {("k", 0): sb(nc, st, "c_stk0", [128, 64], F32),
                     ("k", 1): sb(nc, st, "c_stk1", [128, 64], F32),
                     ("v", 0): sb(nc, st, "c_stv0", [128, 64], F32),
                     ("v", 1): sb(nc, st, "c_stv1", [128, 64], F32)}
            tmp16 = sb(nc, st, "c_tmp16", [128, 16], F32)
            sqt = sb(nc, st, "c_sqt", [128, 1024], F32)
            t3b = [sb(nc, st, "c_t3s", [128, 1024], F32)] * 2
            klnb = [sb(nc, st, f"c_kln{i}", [128, 1024], BF16) for i in range(2)]
            vlnb = [sb(nc, st, f"c_vln{i}", [128, 1024], BF16) for i in range(2)]
            kvf = sb(nc, st, "c_kvf", [64, 1024], F32)
            qst = [sb(nc, st, f"c_qst{i}", [128, 512], BF16) for i in range(4)]
            for cc in range(8):
                ms.op("pool", lambda e, cc=cc: e.dma_start(
                    out=wq[:, cc * 1024:(cc + 1) * 1024],
                    in_=D["qkvT"][cc * 128:(cc + 1) * 128, 0:1024]),
                    writes=(f"wq{cc}",), dma=True)
                ms.op("pool", lambda e, cc=cc: e.dma_start(
                    out=wkv[:, cc * 2048:(cc + 1) * 2048],
                    in_=D["qkvT"][cc * 128:(cc + 1) * 128, 1024:3072]),
                    writes=(f"wkv{cc}",), dma=True)
            for jc in range(8):
                for tgp in range(4):
                    for cc in range(8):
                        for t2 in range(2):
                            tg = tgp * 2 + t2
                            pt = ps[f"pq{t2}"]
                            ms.op("pe", lambda e, pt=pt, cc=cc, jc=jc, tg=tg: e.matmul(
                                pt[:, :], wq[:, cc * 1024 + jc * 128: cc * 1024 + (jc + 1) * 128],
                                cx.xT[:, cc * 4096 + tg * 512: cc * 4096 + (tg + 1) * 512],
                                start=(cc == 0), stop=(cc == 7)),
                                reads=(f"wq{cc}", "xT"), writes=(f"pq{t2}",), accum_key=("q", jc, tg))
                    for t2 in range(2):
                        tg = tgp * 2 + t2
                        phn = f"pq{t2}"
                        pt = ps[phn]
                        sbq = qst[tg % 4]
                        sbk = f"qst{tg % 4}"
                        ms.op("act", lambda e, sbq=sbq, pt=pt, jc=jc: e.activation(
                            sbq[:, :], pt[:, :], AF.Identity, bias=qb[:, jc:jc + 1]),
                            reads=(phn, "qb_fm"), writes=(sbk,))
                        ms.op("sp", lambda e, sbq=sbq, jc=jc, tg=tg: e.dma_start(
                            out=D["s_qc"][jc, :, tg * 512:(tg + 1) * 512], in_=sbq[:, :]),
                            reads=(sbk,), writes=(), dma=True)
            for tt in range(NT):
                b = tt % 2
                for cc in range(8):
                    for nm in ("k", "v"):
                        wofs = 0 if nm == "k" else 1024
                        pko = "pk" if nm == "k" else "pv"
                        for half in range(2):
                            phn = f"{pko}{half}"
                            pt = ps[phn]
                            ms.op("pe", lambda e, pt=pt, cc=cc, tt=tt, half=half, wofs=wofs: e.matmul(
                                pt[:, :], cx.xT[:, cc * 4096 + tt * 128: cc * 4096 + (tt + 1) * 128],
                                wkv[:, cc * 2048 + wofs + half * 512: cc * 2048 + wofs + half * 512 + 512],
                                start=(cc == 0), stop=(cc == 7)),
                                reads=(f"wkv{cc}", "xT"), writes=(phn,), accum_key=(nm, tt, half))
                for nmi, nm in enumerate(("k", "v")):
                    raw = (kraw if nm == "k" else vraw)[b]
                    rawk = f"c_{nm}raw{b}"
                    stt_ = stats[(nm, b)]
                    stk = f"c_st{nm}{b}"
                    lnout = (klnb if nm == "k" else vlnb)[b]
                    lnk = f"c_{nm}ln{b}"
                    gname = f"{nm}ng_rep"; bname = f"{nm}nb_rep"
                    biasr = reps["kb_rep" if nm == "k" else "vb_rep"]
                    biask = "kb_rep" if nm == "k" else "vb_rep"
                    pko = "pk" if nm == "k" else "pv"
                    for half in range(2):
                        phn = f"{pko}{half}"
                        pt = ps[phn]
                        ms.op("dve", lambda e, raw=raw, pt=pt, half=half, biasr=biasr: e.scalar_tensor_tensor(
                            raw[:, half * 512:(half + 1) * 512], pt[:, :], 1.0,
                            biasr[:, half * 512:(half + 1) * 512], ALU.bypass, ALU.add),
                            reads=(phn, biask), writes=(rawk,))
                    ms.op("dve", lambda e, raw=raw, stt_=stt_: e.reduce_sum(
                        stt_[:, 0:16], raw[:, :].rearrange("p (a b) -> p a b", b=64), axis=AX.X),
                        reads=(rawk,), writes=(stk + "s",))
                    ms.op("act", lambda e, raw=raw: e.activation(sqt[:, :], raw[:, :], AF.Square),
                          reads=(rawk,), writes=("c_sqt",))
                    ms.op("dve", lambda e, stt_=stt_: e.reduce_sum(
                        stt_[:, 16:32], sqt[:, :].rearrange("p (a b) -> p a b", b=64), axis=AX.X),
                        reads=("c_sqt",), writes=(stk + "q",))
                    ms.op("dve", lambda e, stt_=stt_: e.tensor_scalar(
                        stt_[:, 32:48], stt_[:, 0:16], 1.0 / 64.0, None, ALU.mult),
                        reads=(stk + "s",), writes=(stk + "m",))
                    ms.op("dve", lambda e, stt_=stt_: e.tensor_tensor(
                        tmp16[:, :], stt_[:, 32:48], stt_[:, 32:48], ALU.mult),
                        reads=(stk + "m",), writes=("c_tmp16",))
                    ms.op("dve", lambda e, stt_=stt_: e.tensor_scalar(
                        stt_[:, 48:64], stt_[:, 16:32], 1.0 / 64.0, None, ALU.mult),
                        reads=(stk + "q",), writes=(stk + "v",))
                    ms.op("dve", lambda e, stt_=stt_: e.tensor_tensor(
                        stt_[:, 48:64], stt_[:, 48:64], tmp16[:, :], ALU.subtract),
                        reads=(stk + "v", "c_tmp16"), writes=(stk + "v",))
                    ms.op("dve", lambda e, stt_=stt_: e.tensor_scalar(
                        stt_[:, 48:64], stt_[:, 48:64], 1.0, 1e-5, ALU.mult, ALU.add),
                        reads=(stk + "v",), writes=(stk + "v",))
                    ms.op("act", lambda e, stt_=stt_: e.activation(
                        stt_[:, 48:64], stt_[:, 48:64], AF.Ln),
                        reads=(stk + "v",), writes=(stk + "v",))
                    ms.op("act", lambda e, stt_=stt_: e.activation(
                        stt_[:, 48:64], stt_[:, 48:64], AF.Exp, scale=-0.5),
                        reads=(stk + "v",), writes=(stk + "v",))
                    t3 = t3b[b]
                    m3d = stt_[:, 32:48].rearrange("p (a b) -> p a b", b=1).broadcast_to([128, 16, 64])
                    r3d = stt_[:, 48:64].rearrange("p (a b) -> p a b", b=1).broadcast_to([128, 16, 64])
                    raw3d = raw[:, :].rearrange("p (a b) -> p a b", b=64)
                    t33d = t3[:, :].rearrange("p (a b) -> p a b", b=64)
                    ms.op("dve", lambda e, t33d=t33d, raw3d=raw3d, m3d=m3d: e.tensor_tensor(
                        t33d, raw3d, m3d, ALU.subtract),
                        reads=(rawk, stk + "m"), writes=("c_t3",))
                    ms.op("pool", lambda e, t33d=t33d, r3d=r3d: e.tensor_tensor(
                        t33d, t33d, r3d, ALU.mult),
                        reads=("c_t3", stk + "v"), writes=("c_t3",))
                    ms.op("dve", lambda e, t3=t3, g=reps[gname]: e.tensor_tensor(
                        t3[:, :], t3[:, :], g[:, :], ALU.mult),
                        reads=("c_t3", gname), writes=("c_t3",))
                    ms.op("dve", lambda e, lnout=lnout, t3=t3, bb=reps[bname]: e.tensor_tensor(
                        lnout[:, :], t3[:, :], bb[:, :], ALU.add),
                        reads=("c_t3", bname), writes=(lnk,))
                for h in range(16):
                    bank = h // 8
                    phn = f"pkv{bank}"
                    pt = ps[phn]
                    ms.op("pe", lambda e, pt=pt, h=h, tt=tt, b=b: e.matmul(
                        pt[:, (h % 8) * 64:(h % 8) * 64 + 64],
                        klnb[b][:, h * 64:(h + 1) * 64], vlnb[b][:, h * 64:(h + 1) * 64],
                        start=(tt == 0 and h % 8 == 0), stop=(tt == NT - 1 and h % 8 == 7)),
                        reads=(f"c_kln{b}", f"c_vln{b}"), writes=(phn,), accum_key=("kv", bank))
            for bank in range(2):
                pt = ps[f"pkv{bank}"]
                ms.op("dve", lambda e, pt=pt, bank=bank: e.scalar_tensor_tensor(
                    kvf[:, bank * 512:(bank + 1) * 512], pt[:, :], 1.0 / 4096.0,
                    reps["temp_rep"][:, bank * 512:(bank + 1) * 512], ALU.mult, ALU.mult),
                    reads=(f"pkv{bank}", "temp_rep"), writes=("c_kvf",))
            ms.op("pool", lambda e: e.tensor_tensor(kvf[:, :], kvf[:, :], reps["kvbias"][:, :], ALU.add),
                  reads=("c_kvf", "kvbias"), writes=("c_kvf",))
            ms.op("pool", lambda e: e.memset(kvpair[:, :], 0.0), writes=("kvpair",))
            for h in range(16):
                pr = h // 2
                rofs = (h % 2) * 64
                cofs = pr * 128 + (h % 2) * 64
                ms.op("dve", lambda e, h=h, rofs=rofs, cofs=cofs: e.tensor_copy(
                    kvpair[rofs:rofs + 64, cofs:cofs + 64], kvf[:, h * 64:(h + 1) * 64]),
                    reads=("c_kvf",), writes=("kvpair",))
            ms.run_block(nc)
        with ExitStack() as st:
            ms = MS(sems)
            ps = alloc_psums(nc, st, [("pa0", (128, 512)), ("pa1", (128, 512)),
                                      ("pt0", (128, 512)), ("pt1", (128, 512)),
                                      ("pp0", (128, 512)), ("pp1", (128, 512)),
                                      ("pq0", (128, 512)), ("pq1", (128, 512))], ms)
            p1w = sb(nc, st, "c_p1w", [128, 8 * 1024], BF16)
            p2w = sb(nc, st, "c_p2w", [128, 8 * 1024], BF16)
            p1b = sb(nc, st, "c_p1b", [128, 1024], F32)
            p2b = sb(nc, st, "c_p2b", [128, 1024], F32)
            x2r = [sb(nc, st, f"c_x2r{i}", [128, 1024], F32) for i in range(2)]
            xb2 = [sb(nc, st, f"c_xb2{i}", [128, 1024], F32) for i in range(2)]
            a1 = [sb(nc, st, f"c_a1{i}", [128, 1024], F32) for i in range(2)]
            a1T = [sb(nc, st, f"c_a1T{i}", [128, 1024], BF16) for i in range(2)]
            h1f = [sb(nc, st, f"c_h1f{i}", [128, 1024], F32) for i in range(2)]
            h1g = [sb(nc, st, f"c_h1g{i}", [128, 1024], F32) for i in range(2)]
            h1T = [sb(nc, st, f"c_h1T{i}", [128, 1024], BF16) for i in range(2)]
            qtile = [sb(nc, st, f"c_qtile{i}", [128, 8 * 128], BF16) for i in range(2)]
            lne = LNEmit(nc, st, "cln")
            ms.op("sp", lambda e: e.dma_start(out=p1b[:, :], in_=D["p1b_rep"][:, :]), writes=("p1b",), dma=True)
            ms.op("sp", lambda e: e.dma_start(out=p2b[:, :], in_=D["p2b_rep"][:, :]), writes=("p2b",), dma=True)
            for cc in range(8):
                ms.op("pool", lambda e, cc=cc: e.dma_start(
                    out=p1w[:, cc * 1024:(cc + 1) * 1024],
                    in_=D["p1T"][cc * 128:(cc + 1) * 128, :]),
                    writes=(f"p1w{cc}",), dma=True)
                ms.op("pool", lambda e, cc=cc: e.dma_start(
                    out=p2w[:, cc * 1024:(cc + 1) * 1024],
                    in_=D["p2T"][cc * 128:(cc + 1) * 128, :]),
                    writes=(f"p2w{cc}",), dma=True)
            def transpose_group(tt, srck, srct, dstk, dstt, b):
                for half in range(2):
                    phn = f"pt{half}"
                    pt = ps[phn]
                    for k in range(4):
                        cc = half * 4 + k
                        ms.op("pe", lambda e, pt=pt, k=k, cc=cc, srct=srct: e.matmul(
                            pt[:, k * 128:(k + 1) * 128], srct[:, cc * 128:(cc + 1) * 128],
                            cx.identf[:, :], start=(k == 0), stop=(k == 3), is_transpose=True),
                            reads=(srck, "ident"), writes=(phn,), accum_key=("tr", tt, srck, half))
                    q = "dve" if half == 0 else "act"
                    dst = dstt[:, half * 512:(half + 1) * 512]
                    if q == "dve":
                        ms.op(q, lambda e, pt=pt, dst=dst: e.tensor_copy(dst, pt[:, :]),
                              reads=(phn,), writes=(dstk,))
                    else:
                        ms.op(q, lambda e, pt=pt, dst=dst: e.copy(dst, pt[:, :]),
                              reads=(phn,), writes=(dstk,))
            for tt in range(NT):
                b = tt % 2
                ms.op("sp", lambda e, b=b, tt=tt: e.dma_start(
                    out=x2r[b][:, :], in_=D["resid2"][tt * 128:(tt + 1) * 128, :]),
                    writes=(f"x2r{b}",), dma=True)
                ms.op("sp", lambda e, b=b, tt=tt: e.dma_start(
                    out=qtile[b][:, :].rearrange("p (a t) -> p a t", t=128),
                    in_=D["s_qc"][:, :, tt * 128:(tt + 1) * 128].rearrange("a p t -> p a t")),
                    writes=(f"qtile{b}",), dma=True)
                ms.op("pool", lambda e, b=b: e.tensor_tensor(
                    xb2[b][:, :], x2r[b][:, :], x2r[b][:, :], ALU.add),
                    reads=(f"x2r{b}",), writes=(f"xb2{b}",))
                ms.op("pool", lambda e, b=b: e.tensor_tensor(
                    xb2[b][:, :], xb2[b][:, :], p2b[:, :], ALU.add),
                    reads=(f"xb2{b}", "p2b"), writes=(f"xb2{b}",))
                for jc in range(8):
                    phn = f"pa{jc // 4}"
                    pt = ps[phn]
                    ms.op("pe", lambda e, pt=pt, jc=jc, b=b: e.matmul(
                        pt[:, (jc % 4) * 128:(jc % 4) * 128 + 128],
                        qtile[b][:, jc * 128:(jc + 1) * 128],
                        kvpair[:, jc * 128:(jc + 1) * 128],
                        start=(jc % 4 == 0), stop=(jc % 4 == 3)),
                        reads=(f"qtile{b}", "kvpair"), writes=(phn,), accum_key=("att", tt, jc // 4))
                for half in range(2):
                    pt = ps[f"pa{half}"]
                    ms.op("dve", lambda e, pt=pt, b=b, half=half: e.scalar_tensor_tensor(
                        a1[b][:, half * 512:(half + 1) * 512], pt[:, :], 1.0,
                        x2r[b][:, half * 512:(half + 1) * 512], ALU.bypass, ALU.add),
                        reads=(f"pa{half}", f"x2r{b}"), writes=(f"a1{b}",))
                transpose_group(tt, f"a1{b}", a1[b], f"a1T{b}", a1T[b], b)
                for cc in range(8):
                    for half in range(2):
                        pt = ps[f"pp{half}"]
                        ms.op("pe", lambda e, pt=pt, cc=cc, b=b, half=half: e.matmul(
                            pt[:, :], a1T[b][:, cc * 128:(cc + 1) * 128],
                            p1w[:, cc * 1024 + half * 512: cc * 1024 + half * 512 + 512],
                            start=(cc == 0), stop=(cc == 7)),
                            reads=(f"a1T{b}", f"p1w{cc}"), writes=(f"pp{half}",), accum_key=("p1", tt, half))
                for half in range(2):
                    phn = f"pp{half}"
                    pt = ps[phn]
                    ms.op("dve", lambda e, pt=pt, b=b, half=half: e.scalar_tensor_tensor(
                        h1f[b][:, half * 512:(half + 1) * 512], pt[:, :], 1.0,
                        p1b[:, half * 512:(half + 1) * 512], ALU.bypass, ALU.add),
                        reads=(phn, "p1b"), writes=(f"h1f{b}",))
                ms.op("act", lambda e, b=b: e.activation(h1g[b][:, :], h1f[b][:, :], AF.Gelu),
                      reads=(f"h1f{b}",), writes=(f"h1g{b}",))
                transpose_group(tt, f"h1g{b}", h1g[b], f"h1T{b}", h1T[b], b)
                for cc in range(8):
                    for half in range(2):
                        pt = ps[f"pq{half}"]
                        ms.op("pe", lambda e, pt=pt, cc=cc, b=b, half=half: e.matmul(
                            pt[:, :], h1T[b][:, cc * 128:(cc + 1) * 128],
                            p2w[:, cc * 1024 + half * 512: cc * 1024 + half * 512 + 512],
                            start=(cc == 0), stop=(cc == 7)),
                            reads=(f"h1T{b}", f"p2w{cc}"), writes=(f"pq{half}",), accum_key=("p2", tt, half))
                for half in range(2):
                    phn = f"pq{half}"
                    pt = ps[phn]
                    lne.evac_half(ms, b, half, pt[:, :], phn, xb2[b], f"xb2{b}", 1.0)
                lne.ln(ms, b, D["resid3"], tt)
            ms.run_block(nc)


PHASE_FNS["C"] = phaseC


D_NQ = 4
D_NPAIR = 17


def phaseD(nc, cx, prog_stack):
    D = cx.D
    sems = cx.sems
    with ExitStack() as st:
        cx.xT = sb(nc, st, "d_xT", [128, 8 * 4096], BF16)
        ms = MS(sems)
        ps = alloc_psums(nc, st, [("pg0", (128, 512)), ("pg1", (128, 512)),
                                  ("pu0", (128, 512)), ("pu1", (128, 512)),
                                  ("pd0", (128, 512)), ("pd1", (128, 512)),
                                  ("pcb", (128, 512)), ("pex", (128, 512))], ms)
        ms.psum_keys.update(["plg", "ptc"])
        plg = ps["pex"][:, 0:32]
        ptc = ps["pex"][0:32, 128:256]
        load_ident(nc, cx, ms)
        xt = [sb(nc, st, f"d_xt{i}", [128, 1024], F32) for i in range(2)]
        stg = sb(nc, st, "d_stg", [128, 1024], F32)
        gw = sb(nc, st, "d_gw", [128, 8 * 32], F32)
        gbr = sb(nc, st, "d_gbr", [128, 32], F32)
        scores = sb(nc, st, "d_scores", [128, 32], F32)
        scb = sb(nc, st, "d_scb", [128, 32], F32)
        eqb = sb(nc, st, "d_eqb", [128, 32], F32)
        wrk = sb(nc, st, "d_wrk", [128, 32], F32)
        gm1 = sb(nc, st, "d_gm1", [128, 8], F32)
        gm2 = sb(nc, st, "d_gm2", [128, 8], F32)
        gs = sb(nc, st, "d_gs", [128, 8], F32)
        g8s = sb(nc, st, "d_g8s", [128, 8], F32)
        gsm = sb(nc, st, "d_gsm", [128, 8], F32)
        sm = sb(nc, st, "d_sm", [128, 32], F32)
        mkd = sb(nc, st, "d_mkd", [128, 32], F32)
        mm8 = sb(nc, st, "d_mm8", [128, 8], F32)
        mrk = sb(nc, st, "d_mrk", [128, 32], F32)
        sel = sb(nc, st, "d_sel", [128, 32], F32)
        cw = sb(nc, st, "d_cw", [128, 32], F32)
        den = sb(nc, st, "d_den", [128, 2], F32)
        comb = sb(nc, st, "d_comb", [128, 32], F32)
        combT = sb(nc, st, "d_combT", [32, 1024], BF16)
        racc = sb(nc, st, "d_racc", [128, 8 * 1024], F32)
        wge = [sb(nc, st, f"d_wge{i}", [128, 8 * 512], BF16) for i in range(2)]
        wue = [sb(nc, st, f"d_wue{i}", [128, 8 * 512], BF16) for i in range(2)]
        wde = [sb(nc, st, f"d_wde{i}", [128, 4 * 1024], BF16) for i in range(2)]
        gup = sb(nc, st, "d_gup", [128, 8 * 512], BF16)
        s1b = [sb(nc, st, f"d_s1{i}", [128, 512], F32) for i in range(2)]
        s1c = [sb(nc, st, f"d_s1c{i}", [128, 512], F32) for i in range(2)]
        cbs = [sb(nc, st, f"d_cbs{i}", [128, 512], BF16) for i in range(2)]
        cbones = sb(nc, st, "d_cbones", [128, 512], BF16)
        cbrow = [sb(nc, st, f"d_cbrow{i}", [1, 1024], BF16) for i in range(2)]
        onesb1 = sb(nc, st, "d_ones1", [1, 128], BF16)
        x3r = [sb(nc, st, f"d_x3r{i}", [128, 1024], F32) for i in range(2)]
        lne = LNEmit(nc, st, "dln")
        ms.op("pool", lambda e: e.memset(cbones[:, :], 1.0), writes=("cbones",))
        ms.op("pool", lambda e: e.memset(onesb1[:, :], 1.0), writes=("ones1",))
        ms.op("sp", lambda e: e.dma_start(
            out=gw[:, :].rearrange("p (a b) -> p a b", b=32),
            in_=D["gwT"][:, :].rearrange("(a p) e -> p a e", p=128)),
            writes=("gw",), dma=True)
        ms.op("sp", lambda e: e.dma_start(out=gbr[:, :], in_=D["gb_rep"][:, :]),
              writes=("gbr",), dma=True)

        def load_pair_weights(pi):
            shared = (pi == 0)
            for eq in range(2):
                eb = eq
                if shared:
                    wg_in = D["shgT"][:, eq * 512:(eq + 1) * 512]
                    wu_in = D["shuT"][:, eq * 512:(eq + 1) * 512]
                    wd_in = D["shdT"][eq * 512:(eq + 1) * 512, :]
                else:
                    e_ = (pi - 1) * 2 + eq
                    wg_in = D["wgE"][e_]
                    wu_in = D["wuE"][e_]
                    wd_in = D["wdE"][e_]
                ms.op("pool", lambda e, wg_in=wg_in, eb=eb: e.dma_start(
                    out=wge[eb][:, :].rearrange("p (a b) -> p a b", b=512),
                    in_=wg_in.rearrange("(a p) i -> p a i", p=128)),
                    writes=(f"wge{eb}",), dma=True)
                ms.op("pool", lambda e, wu_in=wu_in, eb=eb: e.dma_start(
                    out=wue[eb][:, :].rearrange("p (a b) -> p a b", b=512),
                    in_=wu_in.rearrange("(a p) i -> p a i", p=128)),
                    writes=(f"wue{eb}",), dma=True)
                ms.op("pool", lambda e, wd_in=wd_in, eb=eb: e.dma_start(
                    out=wde[eb][:, :].rearrange("p (a b) -> p a b", b=1024),
                    in_=wd_in.rearrange("(a p) c -> p a c", p=128)),
                    writes=(f"wde{eb}",), dma=True)

        for Q in range(D_NQ):
            load_pair_weights(0)
            for t8 in range(8):
                tt = Q * 8 + t8
                b = tt % 2
                ms.op("sp", lambda e, b=b, tt=tt: e.dma_start(
                    out=xt[b][:, :], in_=D["resid3"][tt * 128:(tt + 1) * 128, :]),
                    writes=(f"xt{b}",), dma=True)
                for half in range(2):
                    phn = f"pd{half}"
                    pt = ps[phn]
                    for k in range(4):
                        cc = half * 4 + k
                        ms.op("pe", lambda e, pt=pt, k=k, b=b, cc=cc: e.matmul(
                            pt[:, k * 128:(k + 1) * 128], xt[b][:, cc * 128:(cc + 1) * 128],
                            cx.identf[:, :], start=(k == 0), stop=(k == 3), is_transpose=True),
                            reads=(f"xt{b}", "ident"), writes=(phn,), accum_key=("tr", tt, half))
                    ms.op("dve", lambda e, pt=pt, half=half: e.tensor_copy(
                        stg[:, half * 512:(half + 1) * 512], pt[:, :]),
                        reads=(phn,), writes=("stg",))
                    dstv = cx.xT[:, :].rearrange("p (a c) -> p a c", c=4096)[:, half * 4:half * 4 + 4,
                                                                            tt * 128:(tt + 1) * 128]
                    ms.op("act", lambda e, pt=pt, dstv=dstv: e.copy(
                        dstv, pt[:, :].rearrange("p (a c) -> p a c", c=128)),
                        reads=(phn,), writes=(f"xTq{Q}",))
                for cc in range(8):
                    ms.op("pe", lambda e, cc=cc: e.matmul(
                        plg, stg[:, cc * 128:(cc + 1) * 128],
                        gw[:, cc * 32:(cc + 1) * 32], start=(cc == 0), stop=(cc == 7)),
                        reads=("stg", "gw"), writes=("plg",), accum_key=("lg", tt))
                ms.op("act", lambda e: e.activation(scores[:, :], plg, AF.Sigmoid),
                      reads=("plg",), writes=("scores",))
                ms.op("pool", lambda e: e.tensor_tensor(scb[:, :], scores[:, :], gbr[:, :], ALU.add),
                      reads=("scores", "gbr"), writes=("scb",))
                scb3 = scb[:, :].rearrange("p (a b) -> p a b", b=4)
                ms.op("dve", lambda e, scb3=scb3: e.reduce_max(gm1[:, :], scb3, axis=AX.X),
                      reads=("scb",), writes=("gm1",))
                m1b = gm1[:, :].rearrange("p (a b) -> p a b", b=1).broadcast_to([128, 8, 4])
                eqb3 = eqb[:, :].rearrange("p (a b) -> p a b", b=4)
                ms.op("dve", lambda e, scb3=scb3, m1b=m1b, eqb3=eqb3: e.tensor_tensor(
                    eqb3, scb3, m1b, ALU.is_equal),
                    reads=("scb", "gm1"), writes=("eqb",))
                ms.op("dve", lambda e: e.scalar_tensor_tensor(
                    wrk[:, :], eqb[:, :], -1e30, scb[:, :], ALU.mult, ALU.add),
                    reads=("eqb", "scb"), writes=("wrk",))
                wrk3 = wrk[:, :].rearrange("p (a b) -> p a b", b=4)
                ms.op("dve", lambda e, wrk3=wrk3: e.reduce_max(gm2[:, :], wrk3, axis=AX.X),
                      reads=("wrk",), writes=("gm2",))
                ms.op("pool", lambda e: e.tensor_tensor(gs[:, :], gm1[:, :], gm2[:, :], ALU.add),
                      reads=("gm1", "gm2"), writes=("gs",))
                ms.op("dve", lambda e: e.max(g8s[:, :], gs[:, :]),
                      reads=("gs",), writes=("g8s",))
                ms.op("dve", lambda e: e.tensor_scalar(gsm[:, :], gs[:, :], g8s[:, 3:4], None, ALU.is_ge),
                      reads=("g8s", "gs"), writes=("gsm",))
                sm3 = sm[:, :].rearrange("p (a b) -> p a b", b=4)
                gb3 = gsm[:, :].rearrange("p (a b) -> p a b", b=1).broadcast_to([128, 8, 4])
                ms.op("dve", lambda e, sm3=sm3, gb3=gb3: e.tensor_copy(sm3, gb3),
                      reads=("gsm",), writes=("sm",))
                ms.op("pool", lambda e: e.tensor_tensor(mkd[:, :], sm[:, :], scb[:, :], ALU.mult),
                      reads=("sm", "scb"), writes=("mkd",))
                ms.op("dve", lambda e: e.max(mm8[:, :], mkd[:, :]),
                      reads=("mkd",), writes=("mm8",))
                ms.op("dve", lambda e: e.match_replace(mrk[:, :], mm8[:, :], mkd[:, :], -1e30),
                      reads=("mm8", "mkd"), writes=("mrk",))
                ms.op("dve", lambda e: e.tensor_single_scalar(sel[:, :], mrk[:, :], -1e20, ALU.is_lt),
                      reads=("mrk",), writes=("sel",))
                ms.op("pool", lambda e: e.tensor_tensor(cw[:, :], sel[:, :], scores[:, :], ALU.mult),
                      reads=("sel", "scores"), writes=("cw",))
                ms.op("dve", lambda e: e.reduce_sum(den[:, 0:1], cw[:, :], axis=AX.X),
                      reads=("cw",), writes=("den",))
                ms.op("dve", lambda e: e.tensor_scalar(den[:, 0:1], den[:, 0:1], 1.0, 1e-20,
                                                        ALU.mult, ALU.add),
                      reads=("den",), writes=("den",))
                ms.op("dve", lambda e: e.reciprocal(den[:, 1:2], den[:, 0:1]),
                      reads=("den",), writes=("denr",))
                ms.op("dve", lambda e: e.tensor_scalar(comb[:, :], cw[:, :], den[:, 1:2], 2.5,
                                                        ALU.mult, ALU.mult),
                      reads=("cw", "denr"), writes=("comb",))
                ms.op("pe", lambda e: e.transpose(ptc, comb[:, :], cx.identf[:, :]),
                      reads=("comb", "ident"), writes=("ptc",))
                ms.op("dve", lambda e, t8=t8: e.tensor_copy(
                    combT[:, t8 * 128:(t8 + 1) * 128], ptc),
                    reads=("ptc",), writes=("combT",))
            for t8 in range(2):
                tt = Q * 8 + t8
                ms.op("act", lambda e, t8=t8, tt=tt: e.dma_start(
                    out=x3r[t8 % 2][:, :], in_=D["resid3"][tt * 128:(tt + 1) * 128, :]),
                    writes=(f"d_x3r{t8 % 2}",), dma=True)
            for pi in range(D_NPAIR):
                shared = (pi == 0)
                if pi > 0:
                    load_pair_weights(pi)
                    for eq in range(2):
                        e_ = (pi - 1) * 2 + eq
                        ms.op("sp", lambda e, e_=e_, eq=eq: e.dma_start(
                            out=cbrow[eq][:, :], in_=combT[e_:e_ + 1, :]),
                            reads=("combT",), writes=(f"cbrow{eq}",), dma=True)
                for tg2 in range(2):
                    tgo = Q * 1024 + tg2 * 512
                    for eq in range(2):
                        eb = eq
                        if shared:
                            cbk = "cbones"
                            cbt = cbones
                        else:
                            cbt = cbs[eq]
                            cbk = f"cbs{eq}"
                            ms.op("pe", lambda e, eb=eb, tg2=tg2: e.matmul(
                                ps["pcb"][:, :], onesb1[:, :],
                                cbrow[eb][:, tg2 * 512:(tg2 + 1) * 512],
                                start=True, stop=True),
                                reads=(f"cbrow{eb}", "ones1"), writes=("pcb",))
                            ms.op("dve", lambda e, cbt=cbt: e.tensor_copy(cbt[:, :], ps["pcb"][:, :]),
                                  reads=("pcb",), writes=(cbk,))
                        for ic in range(4):
                            pgk = f"pg{(ic) % 2}"; puk = f"pu{(ic) % 2}"
                            pg = ps[pgk]; pu = ps[puk]
                            for cc in range(8):
                                ms.op("pe", lambda e, pg=pg, cc=cc, eb=eb, ic=ic, tgo=tgo: e.matmul(
                                    pg[:, :], wge[eb][:, cc * 512 + ic * 128: cc * 512 + (ic + 1) * 128],
                                    cx.xT[:, cc * 4096 + tgo: cc * 4096 + tgo + 512],
                                    start=(cc == 0), stop=(cc == 7)),
                                    reads=(f"wge{eb}", f"xTq{Q}"), writes=(pgk,), accum_key=("g", Q, pi, eq, ic, tg2))
                                ms.op("pe", lambda e, pu=pu, cc=cc, eb=eb, ic=ic, tgo=tgo: e.matmul(
                                    pu[:, :], wue[eb][:, cc * 512 + ic * 128: cc * 512 + (ic + 1) * 128],
                                    cx.xT[:, cc * 4096 + tgo: cc * 4096 + tgo + 512],
                                    start=(cc == 0), stop=(cc == 7)),
                                    reads=(f"wue{eb}", f"xTq{Q}"), writes=(puk,), accum_key=("u", Q, pi, eq, ic, tg2))
                            sb1 = s1b[ic % 2]
                            ms.op("act", lambda e, sb1=sb1, pg=pg: e.activation(
                                sb1[:, :], pg[:, :], AF.Silu),
                                reads=(pgk,), writes=(f"s1{ic % 2}",))
                            sc1 = s1c[ic % 2]
                            ms.op("pool", lambda e, sc1=sc1, sb1=sb1, cbt=cbt: e.tensor_tensor(
                                sc1[:, :], sb1[:, :], cbt[:, :], ALU.mult),
                                reads=(f"s1{ic % 2}", cbk), writes=(f"s1c{ic % 2}",))
                            ms.op("dve", lambda e, pu=pu, sc1=sc1, eq=eq, ic=ic: e.tensor_tensor(
                                gup[:, (eq * 4 + ic) * 512:(eq * 4 + ic + 1) * 512],
                                pu[:, :], sc1[:, :], ALU.mult),
                                reads=(puk, f"s1c{ic % 2}"), writes=(f"gup{eq}_{ic}",))
                    for t4 in range(4):
                        tt = (tgo // 128) + t4
                        for eq in range(2):
                            for ic in range(4):
                                for half in range(2):
                                    pdk = f"pd{half}"
                                    pd = ps[pdk]
                                    ms.op("pe", lambda e, pd=pd, eq=eq, ic=ic, t4=t4, half=half: e.matmul(
                                        pd[:, :],
                                        gup[:, (eq * 4 + ic) * 512 + t4 * 128:(eq * 4 + ic) * 512 + (t4 + 1) * 128],
                                        wde[eq][:, ic * 1024 + half * 512: ic * 1024 + half * 512 + 512],
                                        start=(eq == 0 and ic == 0), stop=(eq == 1 and ic == 3)),
                                        reads=(f"gup{eq}_{ic}", f"wde{eq}"), writes=(pdk,),
                                        accum_key=("d", Q, pi, tg2, t4, half))
                        for half in range(2):
                            pdk = f"pd{half}"
                            pd = ps[pdk]
                            loc = (tt - Q * 8) * 1024 + half * 512
                            if shared:
                                ms.op("dve", lambda e, pd=pd, loc=loc: e.tensor_copy(
                                    racc[:, loc:loc + 512], pd[:, :]),
                                    reads=(pdk,), writes=(f"racc{loc}",))
                            else:
                                ms.op("dve", lambda e, pd=pd, loc=loc: e.scalar_tensor_tensor(
                                    racc[:, loc:loc + 512], pd[:, :], 1.0,
                                    racc[:, loc:loc + 512], ALU.bypass, ALU.add),
                                    reads=(pdk, f"racc{loc}"), writes=(f"racc{loc}",))
            for t8 in range(8):
                tt = Q * 8 + t8
                b = t8 % 2
                for half in range(2):
                    loc = t8 * 1024 + half * 512
                    lne.evac_half(ms, b, half, racc[:, loc:loc + 512], f"racc{loc}",
                                  x3r[t8 % 2], f"d_x3r{t8 % 2}", 1.0)
                lne.ln(ms, b, D["out"], tt)
                if t8 < 6:
                    tt2 = Q * 8 + t8 + 2
                    ms.op("act", lambda e, t8=t8, tt2=tt2: e.dma_start(
                        out=x3r[t8 % 2][:, :], in_=D["resid3"][tt2 * 128:(tt2 + 1) * 128, :]),
                        writes=(f"d_x3r{t8 % 2}",), dma=True)
        ms.run_block(nc)


PHASE_FNS["D"] = phaseD
